# version 61
# baseline (speedup 1.0000x reference)
import numpy as np
import ml_dtypes
from contextlib import ExitStack
import concourse.bass as bass
import concourse.mybir as mybir
from concourse.bass_utils import run_bass_kernel_spmd
from concourse.alu_op_type import AluOpType as OP

F32 = mybir.dt.float32
BF16 = mybir.dt.bfloat16
AF = mybir.ActivationFunctionType
AX = mybir.AxisListType

S, D = 4096, 1024
NTOK = S // 128
NCH = D // 128
EPS = 1e-5
NEG_SHIFT = -10.0
SC_A = 1.0 / 32.0
SC_B = 0.125
CONST_VALS = [0.0, 1.0, EPS, -0.5, NEG_SHIFT, SC_A, SC_B, -1.0]

ENGINES = ("sync", "tensor", "vector", "scalar", "gpsimd")
_PHASES = ("A", "F1", "B", "F2")


class Prog:
    def __init__(self):
        self.ops = {e: [] for e in ENGINES}
        self.cnt = {}
        self.lastw = {e: {} for e in ENGINES}
        self.sems = set()

    def wait(self, eng, toks):
        best = {}
        for t in toks:
            if t is None:
                continue
            s, v = t
            if v > best.get(s, 0):
                best[s] = v
        for s, v in sorted(best.items()):
            if v <= 0 or self.lastw[eng].get(s, 0) >= v:
                continue
            self.lastw[eng][s] = v
            self.ops[eng].append(("w", s, v))

    def do(self, eng, fn, waits=(), sem=None, amt=1):
        self.wait(eng, waits)
        tok = None
        if sem is not None:
            self.sems.add(sem)
            self.cnt[sem] = self.cnt.get(sem, 0) + amt
            tok = (sem, self.cnt[sem])
        self.ops[eng].append(("e", fn, sem, amt))
        return tok

    def tok(self, sem):
        self.sems.add(sem)
        return (sem, self.cnt.get(sem, 0))

    def barrier(self):
        toks = [self.tok(s) for s in sorted(self.sems)]
        for e in ENGINES:
            self.wait(e, toks)

    def flush(self, block, semh):
        for e in ENGINES:
            ops = self.ops[e]
            if not ops:
                continue

            def body(engobj, ops=ops):
                for op in ops:
                    if op[0] == "w":
                        engobj.wait_ge(semh[op[1]], op[2])
                    else:
                        ins = op[1](engobj)
                        if op[2] is not None:
                            ins.then_inc(semh[op[2]], op[3])
            getattr(block, e)(body)
            self.ops[e] = []


def slotinfo():
    return {"users": []}


def _host_consts():
    cst = np.zeros((128, 16), dtype=np.float32)
    for i, v in enumerate(CONST_VALS):
        cst[:, i] = v
    idb = np.eye(128, dtype=ml_dtypes.bfloat16)
    idf = np.eye(128, dtype=np.float32)
    j = np.arange(32, dtype=np.float32)
    inv = (1.0 / (10000.0 ** (j / 32.0))).astype(np.float32)
    t = np.arange(S, dtype=np.float32)
    ang = t[None, :] * inv[:, None]
    c = np.cos(ang).astype(np.float32)
    s = np.sin(ang).astype(np.float32)
    ropeC = np.tile(c, (4, 1)).astype(ml_dtypes.bfloat16)
    sgn = np.repeat(np.array([-1.0, 1.0, -1.0, 1.0], np.float32), 32)
    ropeS = (np.tile(s, (4, 1)) * sgn[:, None]).astype(ml_dtypes.bfloat16)
    p32 = np.zeros((128, 128), dtype=ml_dtypes.bfloat16)
    p32[np.arange(128) ^ 32, np.arange(128)] = 1.0
    return {"cst": cst, "idb": idb, "idf": idf, "ropeC": ropeC,
            "ropeS": ropeS, "p32": p32}


def build_nc(phases=None, dbg=False):
    if phases is None:
        phases = _PHASES
    nc = bass.Bass("TRN2", target_bir_lowering=False)
    xin = nc.dram_tensor("x", [S, D], F32, kind="ExternalInput").ap()
    yout = nc.dram_tensor("y", [S, D], F32, kind="ExternalOutput").ap()
    W = {}
    for nm, shp in [
        ("aWqkv", [D, 3 * D]), ("aBqkv", [3 * D]), ("aWo", [D, D]), ("aBo", [D]),
        ("bWq", [D, D]), ("bBq", [D]), ("bWk", [D, D]), ("bBk", [D]),
        ("bWv", [D, D]), ("bBv", [D]), ("bWo", [D, D]), ("bBo", [D]),
        ("rw", [D, 8]), ("w1", [8, D, D]), ("w2", [8, D, D]),
        ("sgu", [D, 2 * D]), ("sdn", [D, D]), ("sgw", [D, 1]),
    ]:
        W[nm] = nc.dram_tensor(nm, shp, F32, kind="ExternalInput").ap()
    C = {}
    for nm, shp, dt in [("cst", [128, 16], F32), ("idb", [128, 128], BF16),
                        ("idf", [128, 128], F32), ("ropeC", [128, S], BF16),
                        ("ropeS", [128, S], BF16), ("p32", [128, 128], BF16)]:
        C[nm] = nc.dram_tensor(nm, shp, dt, kind="ExternalInput").ap()
    scrA = nc.dram_tensor("scrA", [NCH, 128, S], BF16, kind="Internal").ap()
    RES = {"ap": xin}
    DBG = {}
    if dbg:
        for nm, shp, dt in [("dka", [128, NCH * S], BF16), ("dva", [128, NTOK * D], BF16),
                            ("dxl", [NCH, 128, S], BF16), ("dqf", [128, 2048], BF16),
                            ("dpr", [128, 32 * 256], BF16), ("dct", [128, D], BF16),
                            ("dkb", [128, NCH * S], BF16), ("dqb", [NCH, 128, S], BF16),
                            ("dvx", [128, NTOK * 16 * 65], BF16)]:
            DBG[nm] = nc.dram_tensor(nm, shp, dt, kind="ExternalOutput").ap()
    scrB = nc.dram_tensor("scrB", [NCH, 128, S], BF16, kind="Internal").ap()
    scrV = nc.dram_tensor("scrV", [128, NTOK * 1040], BF16, kind="Internal").ap()

    p = Prog()

    with ExitStack() as TOP:
        E = TOP.enter_context
        semh = {}
        for s_ in ("pe", "dve", "act", "pool", "dS", "dW", "dY", "dA", "dB",
                   "dX0", "dX1", "dQ0", "dQ1", "dT0", "dT1", "dR0", "dR1",
                   "dE0", "dE1"):
            semh[s_] = E(nc.semaphore(s_))
        block = E(nc.Block())

        cst = E(nc.sbuf_tensor("cst_s", [128, 16], F32))
        for i, v in enumerate(CONST_VALS):
            nc.const_aps.register(F32, float(v), cst[:, i:i + 1])
        idb = E(nc.sbuf_tensor("idb_s", [128, 128], BF16))
        idf = E(nc.sbuf_tensor("idf_s", [128, 128], F32))
        onesr = E(nc.sbuf_tensor("onesr", [128, 128], BF16))
        onesc = E(nc.sbuf_tensor("onesc", [128, 1], BF16))
        bias_pp = E(nc.sbuf_tensor("bias_pp", [128, 32], F32))
        brow = E(nc.sbuf_tensor("brow", [128, 2 * D], BF16))
        p32 = E(nc.sbuf_tensor("p32_s", [128, 128], BF16))
        mvt2 = [E(nc.sbuf_tensor(f"mvt{i}", [128, 4], F32)) for i in range(2)]
        rwb_t = E(nc.sbuf_tensor("rwb_t", [128, NCH * 8], BF16))
        sgwb_t = E(nc.sbuf_tensor("sgwb_t", [128, 8], BF16))
        st122 = [E(nc.sbuf_tensor(f"st12{i}", [128, 12], F32)) for i in range(2)]
        xlnb2 = [E(nc.sbuf_tensor(f"xlnb{i}", [128, 1024], BF16)) for i in range(2)]

        def dma(eng, dsem, out, in_, waits=()):
            return p.do(eng, lambda g, o=out, i=in_: g.dma_start(out=o, in_=i),
                        waits=waits, sem=dsem, amt=16)

        def wload(dst3, src3, waits=(), dsem="dW"):
            return dma("gpsimd", dsem, dst3, src3, waits=waits)

        def wsl(w_ap, cols=None):
            a = w_ap if cols is None else w_ap[:, cols[0]:cols[1]]
            return a.rearrange("(c p) m -> p c m", p=128)

        LN = [{"mv": None, "st": None, "xln": None, "ap": None} for _ in range(2)]
        LNCNT = [0]

        def ln_tile(src_dram, tg, xt, sl_xt, dsem, ptr, sl_ptr):
            i = LNCNT[0] % 2
            LNCNT[0] += 1
            st12 = st122[i]
            mvt = mvt2[i]
            xlnb = xlnb2[i]
            L = LN[i]
            tload = dma("sync", dsem, xt[:], src_dram[tg * 128:(tg + 1) * 128, :],
                        waits=sl_xt["users"])
            sl_xt["users"] = []
            t1 = p.do("vector", lambda g, st12=st12: g.bn_stats(st12[:, 0:6], xt[:, 0:512]),
                      waits=[tload, L["st"]], sem="dve")
            t2 = p.do("vector", lambda g, st12=st12: g.bn_stats(st12[:, 6:12], xt[:, 512:1024]),
                      waits=[t1], sem="dve")
            t3 = p.do("vector", lambda g, st12=st12, mvt=mvt: g.bn_aggr(mvt[:, 0:2], st12[:, 0:12]),
                      waits=[t2, L["mv"]], sem="dve")
            t4 = p.do("scalar", lambda g, mvt=mvt: g.activation(mvt[:, 2:3], mvt[:, 1:2], AF.Ln,
                                                                bias=EPS, scale=1.0),
                      waits=[t3, L["ap"]], sem="act")
            t5 = p.do("scalar", lambda g, mvt=mvt: g.activation(mvt[:, 2:3], mvt[:, 2:3], AF.Exp,
                                                                bias=0.0, scale=-0.5),
                      waits=[t4], sem="act")
            t6 = p.do("vector", lambda g, mvt=mvt: g.tensor_scalar(mvt[:, 3:4], mvt[:, 0:1],
                                                                   mvt[:, 2:3], -1.0, OP.mult, OP.mult),
                      waits=[t5], sem="dve")
            t7 = p.do("scalar", lambda g, mvt=mvt, xlnb=xlnb: g.activation(
                xlnb[:], xt[:], AF.Identity, bias=mvt[:, 3:4], scale=mvt[:, 2:3]),
                waits=[t6, L["xln"]], sem="act")
            sl_xt["users"].append(t7)
            L["mv"], L["st"], L["ap"] = t6, t3, t7
            ttr = None
            for c in range(NCH):
                ttr = p.do("tensor",
                           lambda g, c=c, xlnb=xlnb: g.transpose(ptr[:, c * 128:(c + 1) * 128],
                                                                 xlnb[:, c * 128:(c + 1) * 128], idb[:]),
                           waits=([t7] + sl_ptr["users"]) if c == 0 else [],
                           sem="pe" if c == NCH - 1 else None)
            sl_ptr["users"] = []
            L["xln"] = ttr
            return tload, ttr

        dma("sync", "dS", cst[:], C["cst"])
        dma("sync", "dS", idb[:], C["idb"])
        dma("sync", "dS", idf[:], C["idf"])
        dma("sync", "dS", p32[:], C["p32"])
        tS = p.tok("dS")
        p.do("gpsimd", lambda g: g.memset(onesr[:, :], 1.0), sem="pool")
        p.do("gpsimd", lambda g: g.memset(onesc[:, :], 1.0), sem="pool")
        dma("gpsimd", "dW", brow[0:1, 0:D],
            W["aBqkv"][2 * D:3 * D].rearrange("(o d) -> o d", o=1))
        dma("gpsimd", "dW", brow[32:33, 0:D], W["aBo"].rearrange("(o d) -> o d", o=1))
        dma("gpsimd", "dW", brow[0:1, D:2 * D], W["bBv"].rearrange("(o d) -> o d", o=1))
        dma("gpsimd", "dW", brow[32:33, D:2 * D], W["bBo"].rearrange("(o d) -> o d", o=1))
        with nc.sbuf_tensor("btmp", [8, 128], F32) as btmp, \
             nc.sbuf_tensor("btmpb", [8, 128], BF16) as btmpb, \
             nc.psum_tensor("pbias", [128, 512], F32) as pbias, \
             nc.psum_tensor("pbias2", [128, 1024], BF16) as pbias2:
            dma("gpsimd", "dW", rwb_t[:].rearrange("p (c m) -> p c m", m=8), wsl(W["rw"]))
            tprev = None
            for k, src in enumerate([W["aBqkv"][0:D], W["aBqkv"][D:2 * D],
                                     W["bBq"], W["bBk"]]):
                tl = dma("gpsimd", "dW", btmp[:], src.rearrange("(c p) -> c p", p=128),
                         waits=[tprev] if tprev else [])
                tt = p.do("tensor", lambda g: g.transpose(pbias[:, 0:8], btmp[:], idf[0:8, 0:8]),
                          waits=[tl, tS, tprev, p.tok("dW")], sem="pe")
                tprev = p.do("vector",
                             lambda g, k=k: g.tensor_copy(bias_pp[:, k * 8:(k + 1) * 8],
                                                          pbias[:, 0:8]),
                             waits=[tt], sem="dve")
            tlg = dma("gpsimd", "dW", btmpb[:],
                      W["sgw"].rearrange("(c p) m -> c (p m)", p=128))
            ttg = p.do("tensor", lambda g: g.transpose(pbias2[:, 0:8], btmpb[:], idb[0:8, 0:8]),
                       waits=[tlg, tprev, p.tok("dW")], sem="pe")
            p.do("vector", lambda g: g.tensor_copy(sgwb_t[:, 0:8], pbias2[:, 0:8]),
                 waits=[ttg], sem="dve")
            seam0 = [p.tok(x) for x in ("dS", "dW", "pe", "dve", "pool")]
            p.flush(block, semh)

        if "A" in phases:
            with ExitStack() as PH:
                EE = PH.enter_context
                ka = EE(nc.sbuf_tensor("ka", [128, NCH * S], BF16))
                va = EE(nc.sbuf_tensor("va", [128, NTOK * D], BF16))
                wA = EE(nc.sbuf_tensor("wAa", [128, NCH * D], BF16))
                wB = EE(nc.sbuf_tensor("wBa", [128, NCH * D], BF16))
                PA1 = PH.enter_context(ExitStack())
                E1 = PA1.enter_context
                xts = [E1(nc.sbuf_tensor(f"xa{i}", [128, D], F32)) for i in range(2)]
                xfm = [E1(nc.sbuf_tensor(f"xfa{i}", [128, NCH * 256], BF16))
                       for i in range(2)]
                ptr = [E1(nc.psum_tensor(f"ptA{i}", [128, 1024], BF16)) for i in range(2)]
                pk = [E1(nc.psum_tensor(f"pkA{i}", [128, 512], F32)) for i in range(2)]
                pv = [E1(nc.psum_tensor(f"pvA{i}", [128, 512], F32)) for i in range(4)]
                sl_xt = [slotinfo() for _ in range(2)]
                sl_xt[0]["users"] += seam0
                sl_xt[1]["users"] += seam0
                sl_fm = [slotinfo() for _ in range(2)]
                sl_ptr = [slotinfo() for _ in range(2)]
                sl_pk = [slotinfo() for _ in range(2)]
                sl_pv = [slotinfo() for _ in range(4)]

                wload(wA[:].rearrange("p (c m) -> p c m", m=D), wsl(W["aWqkv"], (D, 2 * D)))
                wload(wB[:].rearrange("p (c m) -> p c m", m=D), wsl(W["aWqkv"], (2 * D, 3 * D)))
                twall = p.tok("dW")

                for t in range(NTOK):
                    ix = t % 2
                    ipr = (t // 2) % 2
                    th = t % 2
                    tload, ttr = ln_tile(xin, t, xts[ix], sl_xt[ix], f"dX{ix}",
                                         ptr[ix], sl_ptr[ix])
                    tev = p.do("vector",
                               lambda g, ix=ix, ipr=ipr, th=th: g.tensor_copy(
                                   xfm[ipr][:].rearrange("p (c t) -> p c t", t=256)[:, :, th * 128:(th + 1) * 128],
                                   ptr[ix][:].rearrange("p (c t) -> p c t", t=128)),
                               waits=[ttr] + (sl_fm[ipr]["users"] if th == 0 else []),
                               sem="dve")
                    if th == 0:
                        sl_fm[ipr]["users"] = []
                        tev_first = tev
                    sl_ptr[ix]["users"].append(tev)
                    if th == 1:
                        tw = dma("gpsimd", "dA",
                                 scrA[:, :, (t - 1) * 128:(t + 1) * 128].rearrange("c p t -> p c t"),
                                 xfm[ipr][:].rearrange("p (c t) -> p c t", t=256),
                                 waits=[tev_first, tev])
                        sl_fm[ipr]["users"].append(tw)
                    for hf in range(2):
                        iv = 2 * (t % 2) + hf
                        for dc in range(NCH):
                            p.do("tensor",
                                 lambda g, dc=dc, hf=hf, iv=iv, ipr=ipr, th=th: g.matmul(
                                     pv[iv][:, :],
                                     xfm[ipr][:, dc * 256 + th * 128: dc * 256 + th * 128 + 128],
                                     wB[:, dc * D + hf * 512: dc * D + hf * 512 + 512],
                                     start=(dc == 0), stop=False),
                                 waits=([twall, tev] + sl_pv[iv]["users"]) if dc == 0 else [],
                                 sem=None)
                        tmm = p.do("tensor",
                                   lambda g, hf=hf, iv=iv: g.matmul(
                                       pv[iv][:, :], onesr[0:1, :],
                                       brow[0:1, hf * 512: hf * 512 + 512],
                                       start=False, stop=True), sem="pe")
                        sl_pv[iv]["users"] = []
                        tevv = p.do("vector",
                                    lambda g, t=t, hf=hf, iv=iv: g.tensor_copy(
                                        va[:, t * D + hf * 512: t * D + hf * 512 + 512],
                                        pv[iv][:, :]),
                                    waits=[tmm], sem="dve")
                        sl_pv[iv]["users"].append(tevv)
                        sl_fm[ipr]["users"].append(tevv)
                    if th == 1:
                        for dk in range(NCH):
                            ip = dk % 2
                            tmm = None
                            for dc in range(NCH):
                                tmm = p.do("tensor",
                                           lambda g, dc=dc, dk=dk, ip=ip, ipr=ipr: g.matmul(
                                               pk[ip][:, 0:256],
                                               wA[:, dc * D + dk * 128: dc * D + dk * 128 + 128],
                                               xfm[ipr][:, dc * 256:(dc + 1) * 256],
                                               start=(dc == 0), stop=(dc == NCH - 1)),
                                           waits=([twall, tev_first, tev] + sl_pk[ip]["users"]) if dc == 0 else [],
                                           sem="pe" if dc == NCH - 1 else None)
                            sl_pk[ip]["users"] = []
                            tevk = p.do("vector",
                                        lambda g, dk=dk, ip=ip, t=t: g.tensor_scalar(
                                            ka[:, dk * S + (t - 1) * 128: dk * S + (t - 1) * 128 + 256],
                                            pk[ip][:, 0:256], bias_pp[:, 8 + dk:9 + dk],
                                            None, OP.add),
                                        waits=[tmm], sem="dve")
                            sl_pk[ip]["users"].append(tevk)
                            sl_fm[ipr]["users"].append(tmm)
                if dbg:
                    p.barrier()
                    dma("sync", "dS", DBG["dka"], ka[:])
                    dma("sync", "dS", DBG["dva"], va[:])
                    dma("sync", "dS", DBG["dxl"].rearrange("c p t -> p c t"),
                        scrA.rearrange("c p t -> p c t"))
                    p.barrier()
                p.flush(block, semh)
                PA1.close()

                with ExitStack() as P2:
                    E2 = P2.enter_context
                    wload(wA[:].rearrange("p (c m) -> p c m", m=D), wsl(W["aWqkv"], (0, D)),
                          waits=[p.tok("pe")])
                    wload(wB[:].rearrange("p (c m) -> p c m", m=D), wsl(W["aWo"]))
                    twall = p.tok("dW")
                    seam_a = [p.tok(x) for x in ("pe", "act", "dve", "pool", "dA")]
                    QM = 256
                    xq2 = [E2(nc.sbuf_tensor(f"xqA{i}", [128, NCH * QM], BF16))
                           for i in range(2)]
                    qfm = E2(nc.sbuf_tensor("qfA", [128, NCH * QM], BF16))
                    probs = E2(nc.sbuf_tensor("prA", [128, NTOK * QM], BF16))
                    ctxt = E2(nc.sbuf_tensor("ctA", [128, D], BF16))
                    rcp = E2(nc.sbuf_tensor("rcA", [128, 1], F32))
                    xrs = [E2(nc.sbuf_tensor(f"xrA{i}", [128, D], F32)) for i in range(2)]
                    pS = [E2(nc.psum_tensor(f"pSA{i}", [128, 512], F32)) for i in range(2)]
                    pc0 = E2(nc.psum_tensor("pc0A", [128, 512], F32))
                    pc1 = E2(nc.psum_tensor("pc1A", [128, 512], F32))
                    pcS = E2(nc.psum_tensor("pcSA", [128, 512], F32))
                    ptrx = E2(nc.psum_tensor("ptxA", [128, 1024], BF16))
                    pO = [E2(nc.psum_tensor(f"pOA{i}", [128, 512], F32)) for i in range(2)]
                    sl_xq2 = [slotinfo() for _ in range(2)]
                    sl_qf = slotinfo()
                    sl_pS = [slotinfo() for _ in range(2)]
                    sl_probs = slotinfo()
                    sl_ctx = slotinfo()
                    sl_ctxt = slotinfo()
                    sl_ctxf = slotinfo()
                    sl_ptrx = slotinfo()
                    sl_pO = slotinfo()
                    sl_rcp = slotinfo()
                    sl_xr = [slotinfo() for _ in range(2)]

                    for m in range(S // QM):
                        iq = m % 2
                        xq = xq2[iq]
                        sl_xq = sl_xq2[iq]
                        tldq = dma("sync", f"dQ{iq}",
                                   xq[:].rearrange("p (c t) -> p c t", t=QM),
                                   scrA[:, :, m * QM:(m + 1) * QM].rearrange("c p t -> p c t"),
                                   waits=sl_xq["users"] + [p.tok("dA")] +
                                         (seam_a if m == 0 else []))
                        sl_xq["users"] = []
                        tevq = None
                        for dq in range(NCH):
                            ip = dq % 2
                            tmm = None
                            for dc in range(NCH):
                                tmm = p.do("tensor",
                                           lambda g, dc=dc, dq=dq, ip=ip, xq=xq: g.matmul(
                                               pS[ip][:, 0:QM],
                                               wA[:, dc * D + dq * 128: dc * D + dq * 128 + 128],
                                               xq[:, dc * QM:(dc + 1) * QM],
                                               start=(dc == 0), stop=(dc == NCH - 1)),
                                           waits=([twall, tldq] + sl_pS[ip]["users"]) if dc == 0 else [],
                                           sem="pe" if dc == NCH - 1 else None)
                            sl_pS[ip]["users"] = []
                            tevq = p.do("vector",
                                        lambda g, dq=dq, ip=ip: g.tensor_scalar(
                                            qfm[:, dq * QM:(dq + 1) * QM],
                                            pS[ip][:, 0:QM], bias_pp[:, dq:dq + 1],
                                            None, OP.add),
                                        waits=[tmm] + (sl_qf["users"] if dq == 0 else []),
                                        sem="dve")
                            sl_pS[ip]["users"].append(tevq)
                            sl_xq["users"].append(tevq)
                            if dq == 0:
                                sl_qf["users"] = []
                        tq = tevq
                        texp_last = None
                        for kt in range(NTOK):
                            ip = kt % 2
                            tmm = None
                            for dc in range(NCH):
                                tmm = p.do("tensor",
                                           lambda g, dc=dc, kt=kt, ip=ip: g.matmul(
                                               pS[ip][:, 0:QM],
                                               ka[:, dc * S + kt * 128: dc * S + kt * 128 + 128],
                                               qfm[:, dc * QM:(dc + 1) * QM],
                                               start=(dc == 0), stop=(dc == NCH - 1)),
                                           waits=([tq] + sl_pS[ip]["users"] +
                                                  (sl_probs["users"] if kt == 0 else [])) if dc == 0 else [],
                                           sem="pe" if dc == NCH - 1 else None)
                            sl_pS[ip]["users"] = []
                            if kt == 0:
                                sl_probs["users"] = []
                            texp = p.do("scalar",
                                        lambda g, kt=kt, ip=ip: g.activation(
                                            probs[:, kt * QM:(kt + 1) * QM], pS[ip][:, 0:QM],
                                            AF.Exp, bias=NEG_SHIFT, scale=SC_A),
                                        waits=[tmm], sem="act")
                            sl_pS[ip]["users"].append(texp)
                            sl_qf["users"].append(tmm)
                            texp_last = texp
                        for qs in range(2):
                            tmmS = None
                            for kt in range(NTOK):
                                w0 = ([texp_last] + sl_ctx["users"]) if kt == 0 else []
                                p.do("tensor",
                                     lambda g, kt=kt, qs=qs: g.matmul(
                                         pc0[:, :],
                                         probs[:, kt * QM + qs * 128: kt * QM + qs * 128 + 128],
                                         va[:, kt * D: kt * D + 512],
                                         start=(kt == 0), stop=(kt == NTOK - 1)),
                                     waits=w0, sem=None)
                                p.do("tensor",
                                     lambda g, kt=kt, qs=qs: g.matmul(
                                         pc1[:, :],
                                         probs[:, kt * QM + qs * 128: kt * QM + qs * 128 + 128],
                                         va[:, kt * D + 512: kt * D + 1024],
                                         start=(kt == 0), stop=(kt == NTOK - 1)), sem=None)
                                tmmS = p.do("tensor",
                                            lambda g, kt=kt, qs=qs: g.matmul(
                                                pcS[:, 0:1],
                                                probs[:, kt * QM + qs * 128: kt * QM + qs * 128 + 128],
                                                onesc[:, :],
                                                start=(kt == 0), stop=(kt == NTOK - 1)),
                                            sem="pe" if kt == NTOK - 1 else None)
                            sl_ctx["users"] = []
                            if qs == 1:
                                sl_probs["users"].append(tmmS)
                            trc = p.do("vector", lambda g: g.reciprocal(rcp[:, :], pcS[:, 0:1]),
                                       waits=[tmmS] + sl_rcp["users"], sem="dve")
                            sl_rcp["users"] = []
                            tc0 = p.do("vector",
                                       lambda g: g.tensor_scalar(
                                           ctxt[:, 0:512], pc0[:, :],
                                           rcp[:, 0:1], None, OP.mult),
                                       waits=[trc] + sl_ctxt["users"], sem="dve")
                            sl_ctxt["users"] = []
                            tc1 = p.do("vector",
                                       lambda g: g.tensor_scalar(
                                           ctxt[:, 512:1024], pc1[:, :],
                                           rcp[:, 0:1], None, OP.mult),
                                       waits=[trc], sem="dve")
                            sl_rcp["users"] += [tc0, tc1]
                            sl_ctx["users"] += [tc0, tc1]
                            if dbg and m == 0 and qs == 0:
                                dma("sync", "dS", DBG["dqf"], qfm[:], waits=[tq])
                                dma("sync", "dS", DBG["dpr"], probs[:], waits=[texp_last])
                                dma("sync", "dS", DBG["dct"], ctxt[:], waits=[tc0, tc1])
                            ttr = None
                            for c in range(NCH):
                                ttr = p.do("tensor",
                                           lambda g, c=c: g.transpose(
                                               ptrx[:, c * 128:(c + 1) * 128],
                                               ctxt[:, c * 128:(c + 1) * 128],
                                               idb[:]),
                                           waits=([tc0, tc1] + sl_ptrx["users"]) if c == 0 else [],
                                           sem="pe" if c == NCH - 1 else None)
                            sl_ptrx["users"] = []
                            tevc = p.do("vector", lambda g: g.tensor_copy(qfm[:, 0:D], ptrx[:]),
                                        waits=[ttr] + sl_ctxf["users"] + sl_qf["users"],
                                        sem="dve")
                            sl_ptrx["users"].append(tevc)
                            sl_ctxt["users"].append(ttr)
                            sl_ctxf["users"] = []
                            tso = m * QM + qs * 128
                            tpo = [None, None]
                            for hf in range(2):
                                for dc in range(NCH):
                                    p.do("tensor",
                                         lambda g, dc=dc, hf=hf: g.matmul(
                                             pO[hf][:, :],
                                             qfm[:, dc * 128:(dc + 1) * 128],
                                             wB[:, dc * D + hf * 512: dc * D + hf * 512 + 512],
                                             start=(dc == 0), stop=False),
                                         waits=([twall, tevc] + sl_pO["users"]) if (dc == 0 and hf == 0) else [],
                                         sem=None)
                                if hf == 0:
                                    sl_pO["users"] = []
                                tpo[hf] = p.do("tensor",
                                               lambda g, hf=hf: g.matmul(
                                                   pO[hf][:, :], onesr[32:33, :],
                                                   brow[32:33, hf * 512: hf * 512 + 512],
                                                   start=False, stop=True), sem="pe")
                            iy = qs
                            tlx = dma("sync", f"dT{iy}", xrs[iy][:], xin[tso:tso + 128, :],
                                      waits=sl_xr[iy]["users"])
                            sl_xr[iy]["users"] = []
                            ta0 = p.do("vector",
                                       lambda g, iy=iy: g.tensor_tensor(
                                           xrs[iy][:, 0:512], pO[0][:, :],
                                           xrs[iy][:, 0:512], OP.add),
                                       waits=[tpo[0], tlx], sem="dve")
                            ta1 = p.do("vector",
                                       lambda g, iy=iy: g.tensor_tensor(
                                           xrs[iy][:, 512:1024], pO[1][:, :],
                                           xrs[iy][:, 512:1024], OP.add),
                                       waits=[tpo[1]], sem="dve")
                            sl_pO["users"] += [ta0, ta1]
                            sl_ctxf["users"].append(tpo[1])
                            sl_qf["users"].append(tpo[1])
                            tyw = dma("gpsimd", "dY", yout[tso:tso + 128, :], xrs[iy][:],
                                      waits=[ta0, ta1])
                            sl_xr[iy]["users"] += [tyw]
                    p.barrier()
                    p.flush(block, semh)
                RES["ap"] = yout

        def ffn_phase(tag):
            with ExitStack() as PH:
                EE = PH.enter_context
                xlnfm = EE(nc.sbuf_tensor(f"xl{tag}", [128, NCH * 2048], BF16))
                acc = EE(nc.sbuf_tensor(f"ac{tag}", [128, 16 * D], F32))
                combine = EE(nc.sbuf_tensor(f"cb{tag}", [128, 16 * 8], F32))
                gb = EE(nc.sbuf_tensor(f"gb{tag}", [128, 2048], BF16))
                growb = EE(nc.sbuf_tensor(f"gw{tag}", [1, 512], BF16))
                ev8 = EE(nc.sbuf_tensor(f"e8{tag}", [128, 8], F32))
                ev2 = EE(nc.sbuf_tensor(f"e2{tag}", [128, 8], F32))
                sc8 = EE(nc.sbuf_tensor(f"s8{tag}", [128, 16], F32))
                h_sb = EE(nc.sbuf_tensor(f"hs{tag}", [128, NCH * 512], BF16))
                midg = EE(nc.sbuf_tensor(f"mg{tag}", [128, NCH * 512], BF16))
                sgt = EE(nc.sbuf_tensor(f"sgt{tag}", [128, 512], BF16))
                mid1 = EE(nc.sbuf_tensor(f"m1{tag}", [128, 512], BF16))
                xts = [EE(nc.sbuf_tensor(f"xt{tag}{i}", [128, D], F32)) for i in range(2)]
                xre = [EE(nc.sbuf_tensor(f"xr{tag}{i}", [128, 512], F32)) for i in range(2)]
                ptr = [EE(nc.psum_tensor(f"pt{tag}{i}", [128, 1024], BF16)) for i in range(2)]
                pA = [EE(nc.psum_tensor(f"pa{tag}{i}", [128, 512], F32)) for i in range(2)]
                pB = [EE(nc.psum_tensor(f"pb{tag}{i}", [128, 512], F32)) for i in range(2)]
                pY = [EE(nc.psum_tensor(f"py{tag}{i}", [128, 512], F32)) for i in range(2)]
                sl_ptr = [slotinfo() for _ in range(2)]
                sl_pA = [slotinfo() for _ in range(2)]
                sl_pB = [slotinfo() for _ in range(2)]
                sl_pY = [slotinfo() for _ in range(2)]
                sl_xt = [slotinfo() for _ in range(2)]
                sl_hsb = slotinfo()
                sl_midg = slotinfo()
                sl_sgt = slotinfo()
                sl_mid1 = slotinfo()
                sl_gb = slotinfo()
                sl_acc = slotinfo()
                sl_ev = slotinfo()
                sl_cb = slotinfo()
                sl_xe = [slotinfo() for _ in range(2)]
                sl_xlq = [slotinfo() for _ in range(4)]
                sl_exw = slotinfo()

                rwb = rwb_t
                trw = p.tok("dW")

                for half in range(2):
                    hb = half * 2048
                    tx_done = []
                    for tl in range(16):
                        ix = tl % 2
                        tg = (hb // 128) + tl
                        tload, ttr = ln_tile(RES["ap"], tg, xts[ix], sl_xt[ix], f"dX{ix}",
                                             ptr[ix], sl_ptr[ix])
                        tev = p.do("scalar",
                                   lambda g, ix=ix, tl=tl: g.copy(
                                       xlnfm[:].rearrange("p (c t) -> p c t", t=2048)[:, :, tl * 128:(tl + 1) * 128],
                                       ptr[ix][:].rearrange("p (c t) -> p c t", t=128)),
                                   waits=[ttr] + sl_xlq[tl // 4]["users"],
                                   sem="act")
                        sl_xlq[tl // 4]["users"] = []
                        sl_ptr[ix]["users"].append(tev)
                        tx_done.append(tev)
                    for tl in range(16):
                        ip = tl % 2
                        tmm = None
                        for dc in range(NCH):
                            tmm = p.do("tensor",
                                       lambda g, dc=dc, tl=tl, ip=ip: g.matmul(
                                           pA[ip][:, 0:8],
                                           xlnfm[:, dc * 2048 + tl * 128: dc * 2048 + tl * 128 + 128],
                                           rwb[:, dc * 8:(dc + 1) * 8],
                                           start=(dc == 0), stop=(dc == NCH - 1)),
                                       waits=([trw, tx_done[tl]] + sl_pA[ip]["users"]) if dc == 0 else [],
                                       sem="pe" if dc == NCH - 1 else None)
                        sl_pA[ip]["users"] = []
                        t1 = p.do("vector", lambda g, ip=ip: g.reduce_max(
                            sc8[:, 0:1], pA[ip][:, 0:8], axis=AX.X, negate=True),
                            waits=[tmm] + sl_ev["users"], sem="dve")
                        sl_ev["users"] = []
                        t2 = p.do("scalar", lambda g, ip=ip: g.activation(
                            ev8[:, :], pA[ip][:, 0:8], AF.Exp, bias=sc8[:, 0:1], scale=1.0),
                            waits=[t1], sem="act")
                        sl_pA[ip]["users"].append(t2)
                        t3 = p.do("vector", lambda g: g.reduce_sum(sc8[:, 1:2], ev8[:, :],
                                                                   axis=AX.X),
                                  waits=[t2], sem="dve")
                        t4 = p.do("vector", lambda g: g.reciprocal(sc8[:, 2:3], sc8[:, 1:2]),
                                  waits=[t3], sem="dve")
                        t5 = p.do("vector", lambda g: g.max(out=sc8[:, 8:16], in_=ev8[:, :]),
                                  waits=[t2], sem="dve")
                        t6 = p.do("vector", lambda g: g.tensor_scalar(
                            ev2[:, :], ev8[:, :], sc8[:, 9:10], None, OP.is_ge),
                            waits=[t5], sem="dve")
                        t7 = p.do("vector", lambda g: g.tensor_mul(ev2[:, :], ev2[:, :], ev8[:, :]),
                                  waits=[t6], sem="dve")
                        t8 = p.do("vector", lambda g, tl=tl: g.tensor_scalar(
                            combine[:, tl * 8:(tl + 1) * 8], ev2[:, :], sc8[:, 2:3],
                            None, OP.mult),
                            waits=[t7, t4] + (sl_cb["users"] if tl == 0 else []), sem="dve")
                        if tl == 0:
                            sl_cb["users"] = []
                        sl_ev["users"] += [t7, t8]
                    tcomb = t8
                    tg4 = None
                    tgb_prev = None
                    for n in range(4):
                        ip = n % 2
                        tmm = None
                        for dc in range(NCH):
                            tmm = p.do("tensor",
                                       lambda g, n=n, dc=dc, ip=ip: g.matmul(
                                           pA[ip][0:1, :],
                                           sgwb_t[:, dc:dc + 1],
                                           xlnfm[:, dc * 2048 + n * 512: dc * 2048 + n * 512 + 512],
                                           start=(dc == 0), stop=(dc == NCH - 1)),
                                       waits=([trw] + tx_done + sl_pA[ip]["users"]) if dc == 0 else [],
                                       sem="pe" if dc == NCH - 1 else None)
                        sl_pA[ip]["users"] = []
                        tg4 = p.do("scalar", lambda g, ip=ip: g.activation(
                            growb[0:1, :], pA[ip][0:1, :], AF.Sigmoid,
                            bias=0.0, scale=1.0),
                            waits=[tmm] + ([tgb_prev] if tgb_prev else []), sem="act")
                        sl_pA[ip]["users"].append(tg4)
                        tb = p.do("tensor", lambda g, ip=ip: g.matmul(
                            pB[ip][:, :], onesr[0:1, :], growb[0:1, :],
                            start=True, stop=True),
                            waits=[tg4] + sl_pB[ip]["users"], sem="pe")
                        sl_pB[ip]["users"] = []
                        tb2 = p.do("vector", lambda g, n=n, ip=ip: g.tensor_copy(
                            gb[:, n * 512:(n + 1) * 512], pB[ip][:, :]),
                            waits=[tb] + (sl_gb["users"] if n == 0 else []), sem="dve")
                        if n == 0:
                            sl_gb["users"] = []
                        sl_pB[ip]["users"].append(tb2)
                        sl_gb["users"].append(tb2)
                        tgb_prev = tb
                    tgb = tb2
                    with ExitStack() as SH:
                        ES = SH.enter_context
                        wsgA = ES(nc.sbuf_tensor(f"wga{tag}{half}", [128, NCH * D], BF16))
                        wsgB = ES(nc.sbuf_tensor(f"wgb{tag}{half}", [128, NCH * D], BF16))
                        wsdn = ES(nc.sbuf_tensor(f"wgd{tag}{half}", [128, NCH * D], BF16))
                        wload(wsgA[:].rearrange("p (c m) -> p c m", m=D), wsl(W["sgu"], (0, D)),
                              waits=sl_exw["users"] if half else [])
                        wload(wsgB[:].rearrange("p (c m) -> p c m", m=D),
                              wsl(W["sgu"], (D, 2 * D)))
                        wload(wsdn[:].rearrange("p (c m) -> p c m", m=D), wsl(W["sdn"]))
                        tsh = p.tok("dW")
                        for mm in range(4):
                            tmid_done = None
                            for fc in range(NCH):
                                ia = fc % 2
                                tga = None
                                for dc in range(NCH):
                                    tga = p.do("tensor",
                                               lambda g, dc=dc, fc=fc, mm=mm, ia=ia: g.matmul(
                                                   pA[ia][:, :],
                                                   wsgA[:, dc * D + fc * 128: dc * D + fc * 128 + 128],
                                                   xlnfm[:, dc * 2048 + mm * 512: dc * 2048 + mm * 512 + 512],
                                                   start=(dc == 0), stop=(dc == NCH - 1)),
                                               waits=([tsh] + tx_done + sl_pA[ia]["users"]) if dc == 0 else [],
                                               sem="pe" if dc == NCH - 1 else None)
                                sl_pA[ia]["users"] = []
                                tgu = None
                                for dc in range(NCH):
                                    tgu = p.do("tensor",
                                               lambda g, dc=dc, fc=fc, mm=mm, ia=ia: g.matmul(
                                                   pB[ia][:, :],
                                                   wsgB[:, dc * D + fc * 128: dc * D + fc * 128 + 128],
                                                   xlnfm[:, dc * 2048 + mm * 512: dc * 2048 + mm * 512 + 512],
                                                   start=(dc == 0), stop=(dc == NCH - 1)),
                                               waits=([tsh] + sl_pB[ia]["users"]) if dc == 0 else [],
                                               sem="pe" if dc == NCH - 1 else None)
                                sl_pB[ia]["users"] = []
                                ts1 = p.do("scalar", lambda g, ia=ia: g.activation(
                                    sgt[:, :], pA[ia][:, :], AF.Silu, bias=0.0, scale=1.0),
                                    waits=[tga] + sl_sgt["users"], sem="act")
                                sl_sgt["users"] = []
                                sl_pA[ia]["users"].append(ts1)
                                ts2 = p.do("vector", lambda g, ia=ia: g.tensor_tensor(
                                    mid1[:, :], sgt[:, :], pB[ia][:, :], OP.mult),
                                    waits=[ts1, tgu] + sl_mid1["users"], sem="dve")
                                sl_mid1["users"] = []
                                sl_sgt["users"].append(ts2)
                                sl_pB[ia]["users"].append(ts2)
                                ts3 = p.do("vector", lambda g, fc=fc, mm=mm: g.tensor_tensor(
                                    midg[:, fc * 512:(fc + 1) * 512], mid1[:, :],
                                    gb[:, mm * 512:(mm + 1) * 512], OP.mult),
                                    waits=[ts2, tgb] + (sl_midg["users"] if fc == 0 else []),
                                    sem="dve")
                                if fc == 0:
                                    sl_midg["users"] = []
                                sl_mid1["users"].append(ts3)
                                sl_gb["users"].append(ts3)
                                tmid_done = ts3
                            for sub in range(4):
                                tlc = mm * 4 + sub
                                for hf in range(2):
                                    iy = hf
                                    tmm = None
                                    for fc in range(NCH):
                                        tmm = p.do("tensor",
                                                   lambda g, fc=fc, sub=sub, hf=hf, iy=iy: g.matmul(
                                                       pY[iy][:, :],
                                                       midg[:, fc * 512 + sub * 128: fc * 512 + sub * 128 + 128],
                                                       wsdn[:, fc * D + hf * 512: fc * D + hf * 512 + 512],
                                                       start=(fc == 0), stop=(fc == NCH - 1)),
                                                   waits=([tsh, tmid_done] + sl_pY[iy]["users"]) if fc == 0 else [],
                                                   sem="pe" if fc == NCH - 1 else None)
                                    sl_pY[iy]["users"] = []
                                    tac = p.do("vector",
                                               lambda g, tlc=tlc, hf=hf, iy=iy: g.tensor_copy(
                                                   acc[:, tlc * D + hf * 512: tlc * D + hf * 512 + 512],
                                                   pY[iy][:, :]),
                                               waits=[tmm] + (sl_acc["users"] if (mm == 0 and sub == 0 and hf == 0) else []),
                                               sem="dve")
                                    if mm == 0 and sub == 0 and hf == 0:
                                        sl_acc["users"] = []
                                    sl_pY[iy]["users"].append(tac)
                                    sl_midg["users"].append(tmm)
                    acc_final = [[] for _ in range(16)]
                    sl_shr = [p.tok("pe")]
                    with ExitStack() as EX:
                        ES = EX.enter_context
                        w1b = [ES(nc.sbuf_tensor(f"w1{tag}{half}{i}", [128, NCH * D], BF16))
                               for i in range(2)]
                        w2b = [ES(nc.sbuf_tensor(f"w2{tag}{half}{i}", [128, NCH * D], BF16))
                               for i in range(2)]
                        sl_we = [slotinfo() for _ in range(2)]
                        for e in range(8):
                            ie = e % 2
                            wl = sl_we[ie]["users"]
                            if e < 2:
                                wl = wl + sl_shr + sl_exw["users"]
                            wload(w1b[ie][:].rearrange("p (c m) -> p c m", m=D),
                                  W["w1"][e].rearrange("(c p) m -> p c m", p=128),
                                  waits=wl, dsem=f"dE{ie}")
                            twe = wload(w2b[ie][:].rearrange("p (c m) -> p c m", m=D),
                                        W["w2"][e].rearrange("(c p) m -> p c m", p=128),
                                        dsem=f"dE{ie}")
                            sl_we[ie]["users"] = []
                            for mm in range(4):
                                th_done = None
                                for fc in range(NCH):
                                    ia = fc % 2
                                    tmm = None
                                    for dc in range(NCH):
                                        tmm = p.do("tensor",
                                                   lambda g, dc=dc, fc=fc, mm=mm, ia=ia, ie=ie: g.matmul(
                                                       pA[ia][:, :],
                                                       w1b[ie][:, dc * D + fc * 128: dc * D + fc * 128 + 128],
                                                       xlnfm[:, dc * 2048 + mm * 512: dc * 2048 + mm * 512 + 512],
                                                       start=(dc == 0), stop=(dc == NCH - 1)),
                                                   waits=([twe] + sl_pA[ia]["users"] +
                                                          (tx_done if (e == 0 and mm == 0 and fc == 0) else [])) if dc == 0 else [],
                                                   sem="pe" if dc == NCH - 1 else None)
                                    sl_pA[ia]["users"] = []
                                    if fc == NCH - 1:
                                        sl_xlq[mm]["users"] = [tmm]
                                    th = p.do("scalar", lambda g, fc=fc, ia=ia: g.activation(
                                        h_sb[:, fc * 512:(fc + 1) * 512], pA[ia][:, :],
                                        AF.Silu, bias=0.0, scale=1.0),
                                        waits=[tmm] + (sl_hsb["users"] if fc == 0 else []),
                                        sem="act")
                                    if fc == 0:
                                        sl_hsb["users"] = []
                                    sl_pA[ia]["users"].append(th)
                                    th_done = th
                                sl_we[ie]["users"].append(th_done)
                                for sub in range(4):
                                    tlc = mm * 4 + sub
                                    for hf in range(2):
                                        iy = hf
                                        tmm = None
                                        for fc in range(NCH):
                                            tmm = p.do("tensor",
                                                       lambda g, fc=fc, sub=sub, hf=hf, iy=iy, ie=ie: g.matmul(
                                                           pY[iy][:, :],
                                                           h_sb[:, fc * 512 + sub * 128: fc * 512 + sub * 128 + 128],
                                                           w2b[ie][:, fc * D + hf * 512: fc * D + hf * 512 + 512],
                                                           start=(fc == 0), stop=(fc == NCH - 1)),
                                                       waits=([twe, th_done] + sl_pY[iy]["users"]) if fc == 0 else [],
                                                       sem="pe" if fc == NCH - 1 else None)
                                        sl_pY[iy]["users"] = []
                                        tac = p.do("vector",
                                                   lambda g, tlc=tlc, hf=hf, iy=iy, e=e: g.scalar_tensor_tensor(
                                                       acc[:, tlc * D + hf * 512: tlc * D + hf * 512 + 512],
                                                       pY[iy][:, :],
                                                       combine[:, tlc * 8 + e: tlc * 8 + e + 1],
                                                       acc[:, tlc * D + hf * 512: tlc * D + hf * 512 + 512],
                                                       OP.mult, OP.add),
                                                   waits=[tmm, tcomb], sem="dve")
                                        sl_pY[iy]["users"].append(tac)
                                        sl_we[ie]["users"].append(tac)
                                        sl_hsb["users"].append(tmm)
                                        sl_acc["users"].append(tac)
                                        sl_cb["users"].append(tac)
                                        if e == 7:
                                            acc_final[tlc].append(tac)
                        sl_exw["users"] = [p.tok("pe")]
                        for tl in range(16):
                            tg = (hb // 128) + tl
                            for hf in range(2):
                                ix = (2 * tl + hf) % 2
                                tlx = dma("gpsimd", f"dT{ix}", xre[ix][:],
                                          RES["ap"][tg * 128:(tg + 1) * 128,
                                                    hf * 512:(hf + 1) * 512],
                                          waits=sl_xe[ix]["users"])
                                sl_xe[ix]["users"] = []
                                tadd = p.do("gpsimd",
                                            lambda g, tl=tl, ix=ix, hf=hf: g.tensor_tensor(
                                                xre[ix][:],
                                                acc[:, tl * D + hf * 512: tl * D + hf * 512 + 512],
                                                xre[ix][:], OP.add),
                                            waits=[tlx, acc_final[tl][hf]], sem="pool")
                                sl_acc["users"].append(tadd)
                                tyw = dma("gpsimd", "dY",
                                          yout[tg * 128:(tg + 1) * 128,
                                               hf * 512:(hf + 1) * 512],
                                          xre[ix][:], waits=[tadd])
                                sl_xe[ix]["users"] += [tadd, tyw]
                    if half == 1:
                        p.barrier()
                p.flush(block, semh)

        if "F1" in phases:
            ffn_phase("u")
            RES["ap"] = yout

        if "B" in phases:
            with ExitStack() as PH:
                EE = PH.enter_context
                kb = EE(nc.sbuf_tensor("kb", [128, NCH * S], BF16))

                with ExitStack() as P1:
                    E1 = P1.enter_context
                    wA = E1(nc.sbuf_tensor("wqB", [128, NCH * D], BF16))
                    wB = E1(nc.sbuf_tensor("wkB", [128, NCH * D], BF16))
                    wV = E1(nc.sbuf_tensor("wvB", [128, NCH * D], BF16))
                    wload(wA[:].rearrange("p (c m) -> p c m", m=D), wsl(W["bWq"]))
                    wload(wB[:].rearrange("p (c m) -> p c m", m=D), wsl(W["bWk"]))
                    wload(wV[:].rearrange("p (c m) -> p c m", m=D), wsl(W["bWv"]))
                    twall = p.tok("dW")
                    xts = [E1(nc.sbuf_tensor(f"xB{i}", [128, D], F32)) for i in range(2)]
                    xfm = [E1(nc.sbuf_tensor(f"xfB{i}", [128, NCH * 512], BF16))
                           for i in range(2)]
                    qr = E1(nc.sbuf_tensor("qrB", [128, NCH * 512], BF16))
                    kr = E1(nc.sbuf_tensor("krB", [128, NCH * 512], BF16))
                    rCf = E1(nc.sbuf_tensor("rCf", [128, S], BF16))
                    rSf = E1(nc.sbuf_tensor("rSf", [128, S], BF16))
                    tmp2b = [E1(nc.sbuf_tensor(f"tp2B{i}", [128, 512], BF16))
                             for i in range(2)]
                    vstage = [E1(nc.sbuf_tensor(f"vsB{i}", [128, 2080], BF16))
                              for i in range(2)]
                    ptmb = [E1(nc.sbuf_tensor(f"ptmB{i}", [128, 512], F32))
                            for i in range(2)]
                    ptr = [E1(nc.psum_tensor(f"ptB{i}", [128, 1024], BF16)) for i in range(2)]
                    pq = [E1(nc.psum_tensor(f"pqB{i}", [128, 512], F32)) for i in range(2)]
                    psw = [E1(nc.psum_tensor(f"pwB{i}", [128, 512], F32)) for i in range(2)]
                    pv = [E1(nc.psum_tensor(f"pvB{i}", [128, 512], F32)) for i in range(2)]
                    sl_xt = [slotinfo() for _ in range(2)]
                    sl_fm = [slotinfo() for _ in range(2)]
                    sl_ptr = [slotinfo() for _ in range(2)]
                    sl_pq = [slotinfo() for _ in range(2)]
                    sl_psw = [slotinfo() for _ in range(2)]
                    sl_qr = slotinfo()
                    sl_kr = slotinfo()
                    sl_rope = [slotinfo()]
                    sl_tmp1 = [slotinfo() for _ in range(2)]
                    sl_tmp2 = [slotinfo() for _ in range(2)]
                    sl_pv = [slotinfo() for _ in range(2)]
                    sl_vs = [slotinfo() for _ in range(2)]
                    tms0 = p.do("vector", lambda g: g.memset(
                        vstage[0][:].rearrange("p (h j) -> p h j", j=65)[:, :, 64:65], 1.0),
                        sem="dve")
                    tms1 = p.do("vector", lambda g: g.memset(
                        vstage[1][:].rearrange("p (h j) -> p h j", j=65)[:, :, 64:65], 1.0),
                        sem="dve")
                    sl_vs[0]["users"].append(tms0)
                    sl_vs[1]["users"].append(tms1)

                    dma("sync", "dR0", rCf[:], C["ropeC"][:, :])
                    dma("sync", "dR0", rSf[:], C["ropeS"][:, :])
                    trcs_all = p.tok("dR0")
                    for m in range(8):
                        im = m % 2
                        trcs = trcs_all
                        rC = [rCf[:, m * 512:(m + 1) * 512]]
                        rS = [rSf[:, m * 512:(m + 1) * 512]]
                        tfm_done = []
                        for sub in range(4):
                            tg = m * 4 + sub
                            ix = tg % 2
                            tload, ttr = ln_tile(RES["ap"], tg, xts[ix], sl_xt[ix], f"dX{ix}",
                                                 ptr[ix], sl_ptr[ix])
                            tev = p.do("vector",
                                       lambda g, ix=ix, im=im, sub=sub: g.tensor_copy(
                                           xfm[im][:].rearrange("p (c t) -> p c t", t=512)[:, :, sub * 128:(sub + 1) * 128],
                                           ptr[ix][:].rearrange("p (c t) -> p c t", t=128)),
                                       waits=[ttr] + (sl_fm[im]["users"] if sub == 0 else []),
                                       sem="dve")
                            if sub == 0:
                                sl_fm[im]["users"] = []
                            sl_ptr[ix]["users"].append(tev)
                            tfm_done.append(tev)
                            t_glob = m * 4 + sub
                            ivs = (t_glob // 2) % 2
                            hs = t_glob % 2
                            if hs == 0:
                                pair_tevvs = []
                            tevv = None
                            for hf in range(2):
                                tmmv = None
                                for dc in range(NCH):
                                    tmmv = p.do("tensor",
                                                lambda g, dc=dc, hf=hf, im=im, sub=sub: g.matmul(
                                                    pv[hf][:, :],
                                                    xfm[im][:, dc * 512 + sub * 128: dc * 512 + sub * 128 + 128],
                                                    wV[:, dc * D + hf * 512: dc * D + hf * 512 + 512],
                                                    start=(dc == 0), stop=False),
                                                waits=([twall, tev] + sl_pv[hf]["users"]) if dc == 0 else [],
                                                sem=None)
                                tbv = p.do("tensor",
                                           lambda g, hf=hf: g.matmul(
                                               pv[hf][:, :], onesr[0:1, :],
                                               brow[0:1, D + hf * 512: D + hf * 512 + 512],
                                               start=False, stop=True), sem="pe")
                                sl_pv[hf]["users"] = []
                                sl_fm[im]["users"].append(tbv)
                                tevv = p.do("vector",
                                            lambda g, hf=hf, ivs=ivs, hs=hs: g.tensor_copy(
                                                vstage[ivs][:, hs * 1040:(hs + 1) * 1040].rearrange("p (h j) -> p h j", j=65)[:, hf * 8:(hf + 1) * 8, 0:64],
                                                pv[hf][:, :].rearrange("p (o h j) -> p o h j",
                                                                       o=1, j=64)[:, 0, :, :]),
                                            waits=[tbv] + (sl_vs[ivs]["users"] if (hf == 0 and hs == 0) else []),
                                            sem="dve")
                                if hf == 0 and hs == 0:
                                    sl_vs[ivs]["users"] = []
                                sl_pv[hf]["users"].append(tevv)
                                sl_vs[ivs]["users"].append(tevv)
                                pair_tevvs.append(tevv)
                            if hs == 1:
                                tvw = dma("gpsimd", "dA",
                                          scrV[:, (t_glob - 1) * 1040:(t_glob + 1) * 1040],
                                          vstage[ivs][:], waits=pair_tevvs)
                                sl_vs[ivs]["users"].append(tvw)
                        pend_ropes = []
                        for which in range(2):
                            wbuf = wA if which == 0 else wB
                            raw = qr if which == 0 else kr
                            sl_raw = sl_qr if which == 0 else sl_kr
                            for ch in range(NCH):
                                ip = ch % 2
                                tmm = None
                                for dc in range(NCH):
                                    tmm = p.do("tensor",
                                               lambda g, dc=dc, ch=ch, ip=ip, im=im, wbuf=wbuf: g.matmul(
                                                   pq[ip][:, :],
                                                   wbuf[:, dc * D + ch * 128: dc * D + ch * 128 + 128],
                                                   xfm[im][:, dc * 512:(dc + 1) * 512],
                                                   start=(dc == 0), stop=(dc == NCH - 1)),
                                               waits=([twall] + tfm_done + sl_pq[ip]["users"]) if dc == 0 else [],
                                               sem="pe" if dc == NCH - 1 else None)
                                sl_pq[ip]["users"] = []
                                tev = p.do("vector",
                                           lambda g, ch=ch, ip=ip, raw=raw, which=which: g.tensor_scalar(
                                               raw[:, ch * 512:(ch + 1) * 512], pq[ip][:, :],
                                               bias_pp[:, 16 + which * 8 + ch: 17 + which * 8 + ch],
                                               None, OP.add),
                                           waits=[tmm] + (sl_raw["users"] if ch == 0 else []),
                                           sem="dve")
                                if ch == 0:
                                    sl_raw["users"] = []
                                sl_pq[ip]["users"].append(tev)
                                sl_fm[im]["users"].append(tmm)

                                def mk_rope(ch=ch, tev=tev, raw=raw, which=which,
                                            m=m, trcs=trcs, sl_raw=sl_raw,
                                            rC=rC, rS=rS):
                                    isw = ch % 2
                                    tmp2 = tmp2b[isw]
                                    ptm = ptmb[isw]
                                    tsw = p.do("tensor",
                                               lambda g, ch=ch, raw=raw, isw=isw: g.matmul(
                                                   psw[isw][:, :], p32[:, :],
                                                   raw[:, ch * 512:(ch + 1) * 512],
                                                   start=True, stop=True),
                                               waits=[tev] + sl_psw[isw]["users"], sem="pe")
                                    sl_psw[isw]["users"] = []
                                    e1, s1 = ("gpsimd", "pool") if isw == 0 else ("vector", "dve")
                                    tt1 = p.do(e1,
                                               lambda g, ch=ch, raw=raw, tmp2=tmp2: g.tensor_tensor(
                                                   tmp2[:, :], raw[:, ch * 512:(ch + 1) * 512],
                                                   rC[0][:, :], OP.mult),
                                               waits=[tev, trcs, tsw] + sl_tmp2[isw]["users"], sem=s1)
                                    sl_tmp2[isw]["users"] = []
                                    tt2 = p.do("vector",
                                               lambda g, isw=isw, ptm=ptm: g.tensor_tensor(
                                                   ptm[:, :], psw[isw][:, :], rS[0][:, :],
                                                   OP.mult),
                                               waits=[tsw, trcs] + sl_tmp1[isw]["users"], sem="dve")
                                    sl_tmp1[isw]["users"] = []
                                    sl_psw[isw]["users"].append(tt2)
                                    if which == 0:
                                        dst_ap = raw[:, ch * 512:(ch + 1) * 512]
                                    else:
                                        dst_ap = kb[:, ch * S + m * 512: ch * S + m * 512 + 512]
                                    e2, s2 = ("vector", "dve") if isw == 0 else ("gpsimd", "pool")
                                    tdst = p.do(e2,
                                                lambda g, dst_ap=dst_ap, tmp2=tmp2, ptm=ptm: g.tensor_tensor(
                                                    dst_ap, tmp2[:, :], ptm[:, :], OP.add),
                                                waits=[tt1, tt2], sem=s2)
                                    sl_tmp1[isw]["users"].append(tdst)
                                    sl_tmp2[isw]["users"].append(tdst)
                                    sl_rope[0]["users"] += [tt1, tt2]
                                    sl_raw["users"].append(tdst)
                                pend_ropes.append(mk_rope)
                                if ch % 2 == 1:
                                    for fn in pend_ropes:
                                        fn()
                                    pend_ropes = []
                            if which == 0:
                                tqw = dma("gpsimd", "dB",
                                          scrB[:, :, m * 512:(m + 1) * 512].rearrange("c p t -> p c t"),
                                          qr[:].rearrange("p (c t) -> p c t", t=512),
                                          waits=list(sl_qr["users"]))
                                sl_qr["users"].append(tqw)
                    if dbg:
                        p.barrier()
                        dma("sync", "dS", DBG["dkb"], kb[:])
                        dma("sync", "dS", DBG["dqb"].rearrange("c p t -> p c t"),
                            scrB.rearrange("c p t -> p c t"))
                        p.barrier()
                    p.flush(block, semh)

                with ExitStack() as P2:
                    E2 = P2.enter_context
                    wO = E2(nc.sbuf_tensor("woB", [128, NCH * D], BF16))
                    vxb = E2(nc.sbuf_tensor("vxb", [128, NTOK * 16 * 65], BF16))
                    wload(wO[:].rearrange("p (c m) -> p c m", m=D), wsl(W["bWo"]),
                          waits=[p.tok("pe")])
                    twall = p.tok("dW")
                    seam_b = [p.tok(x) for x in ("pe", "act", "dve", "pool", "dA", "dB")]
                    tvx = dma("scalar", "dA", vxb[:], scrV[:], waits=seam_b)
                    qm2 = [E2(nc.sbuf_tensor(f"qmB{i}", [128, NCH * 512], BF16))
                           for i in range(2)]
                    probs = E2(nc.sbuf_tensor("prB", [128, 4 * 1024], BF16))
                    ctxf = E2(nc.sbuf_tensor("cfB", [128, NCH * 512], BF16))
                    rrow = E2(nc.sbuf_tensor("rrB", [1, 1024], F32))
                    rrowb = E2(nc.sbuf_tensor("rbB", [1, 1024], BF16))
                    rbt = E2(nc.sbuf_tensor("rtB", [128, 1024], BF16))
                    xrs = [E2(nc.sbuf_tensor(f"xrB{i}", [128, D], F32)) for i in range(2)]
                    pSh = E2(nc.psum_tensor("pSB", [128, 2048], F32))
                    pOE = E2(nc.psum_tensor("pOEB", [128, 1024], F32))
                    pCb = E2(nc.psum_tensor("pCB", [128, 1024], F32))
                    pC = [pCb[:, 0:512], pCb[:, 512:1024]]
                    pO = [pOE[:, 0:512], pOE[:, 512:1024]]
                    sl_qm = [slotinfo() for _ in range(2)]
                    sl_pS0 = [slotinfo() for _ in range(2)]
                    sl_pS1 = slotinfo()
                    sl_pC = [slotinfo() for _ in range(2)]
                    sl_pC[0]["users"].append(tvx)
                    sl_pC[1]["users"].append(tvx)
                    sl_pr = [slotinfo() for _ in range(2)]
                    sl_ctxf = slotinfo()
                    sl_rr = slotinfo()
                    sl_rrb = slotinfo()
                    sl_rbt = slotinfo()
                    sl_xr = [slotinfo() for _ in range(2)]

                    for m in range(8):
                        iq = m % 2
                        qm = qm2[iq]
                        tldq = dma("sync", f"dQ{iq}",
                                   qm[:].rearrange("p (c t) -> p c t", t=512),
                                   scrB[:, :, m * 512:(m + 1) * 512].rearrange("c p t -> p c t"),
                                   waits=sl_qm[iq]["users"] + [p.tok("dB")] +
                                         (seam_b if m == 0 else []))
                        sl_qm[iq]["users"] = []
                        ctx_done = []
                        pend = {"fn": None}

                        def flush_norm():
                            if pend["fn"] is not None:
                                pend["fn"]()
                                pend["fn"] = None
                        for chh in range(NCH):
                            texp = [None] * 16

                            def emit_scores(kp, chh=chh, tldq=tldq, texp=texp, qm=qm):
                                mml = [None, None]
                                tee = [None, None]
                                srcE = pSh[:, 0:1024] if kp % 2 == 0 else pOE[:, :]
                                for sel in range(2):
                                    pb = sel * 64
                                    slot = sl_pS0[kp % 2] if sel == 0 else sl_pS1
                                    for j in range(2):
                                        c = 2 * kp + j
                                        if sel == 0:
                                            dst = srcE[:, j * 512:(j + 1) * 512]
                                        else:
                                            dst = pSh[:, 1024 + j * 512: 1024 + j * 512 + 512]
                                        w0 = []
                                        if j == 0:
                                            w0 = slot["users"] + ([tldq] if sel == 0 else [])
                                        mml[sel] = p.do(
                                            "tensor",
                                            lambda g, c=c, pb=pb, chh=chh, dst=dst, qm=qm: g.matmul(
                                                dst,
                                                kb[pb:pb + 64, chh * S + c * 128: chh * S + c * 128 + 128],
                                                qm[pb:pb + 64, chh * 512:(chh + 1) * 512],
                                                start=True, stop=True,
                                                tile_position=(pb, 0)),
                                            waits=w0,
                                            sem="pe" if j == 1 else None)
                                        if j == 0:
                                            slot["users"] = []
                                        else:
                                            sl_qm[m % 2]["users"].append(mml[sel])
                                    tee[sel] = p.do(
                                        "scalar",
                                        lambda g, kp=kp, sel=sel, srcE=srcE: g.activation(
                                            probs[:, (kp % 2) * 2048 + sel * 1024:
                                                  (kp % 2) * 2048 + sel * 1024 + 1024],
                                            srcE if sel == 0 else pSh[:, 1024:2048],
                                            AF.Exp,
                                            bias=NEG_SHIFT, scale=SC_B),
                                        waits=[mml[sel]] +
                                              (sl_pr[kp % 2]["users"] if sel == 0 else []),
                                        sem="act")
                                    if sel == 0:
                                        sl_pr[kp % 2]["users"] = []
                                    slot["users"].append(tee[sel])
                                texp[kp] = tee

                            def emit_ctx(kpc, chh=chh, texp=texp):
                                outt = [None, None]
                                for sel in range(2):
                                    h = 2 * chh + sel
                                    tc = None
                                    for j in range(2):
                                        c = 2 * kpc + j
                                        w0 = []
                                        if j == 0:
                                            w0 = [texp[kpc][sel]]
                                            if c == 0:
                                                w0 = w0 + sl_pC[sel]["users"]
                                        tc = p.do(
                                            "tensor",
                                            lambda g, c=c, sel=sel, h=h, kpc=kpc, j=j: g.matmul(
                                                pC[sel][0:65, :],
                                                vxb[:, c * 1040 + h * 65: c * 1040 + h * 65 + 65],
                                                probs[:, (kpc % 2) * 2048 + sel * 1024 + j * 512:
                                                      (kpc % 2) * 2048 + sel * 1024 + j * 512 + 512],
                                                start=(c == 0), stop=(c == 31)),
                                            waits=w0,
                                            sem="pe" if j == 1 else None)
                                        if j == 0 and c == 0:
                                            sl_pC[sel]["users"] = []
                                    sl_pr[kpc % 2]["users"].append(tc)
                                    outt[sel] = tc
                                return outt

                            for kp in range(16):
                                emit_scores(kp)
                                if kp == 1:
                                    flush_norm()
                                if kp >= 1:
                                    emit_ctx(kp - 1)
                            tcs = emit_ctx(15)
                            tve = [None, None]
                            for sel in range(2):
                                tve[sel] = p.do("vector",
                                                lambda g, sel=sel, chh=chh: g.tensor_copy(
                                                    ctxf[sel * 64:sel * 64 + 64,
                                                         chh * 512:(chh + 1) * 512],
                                                    pCb[0:64, sel * 512:sel * 512 + 512]),
                                                waits=[tcs[sel]] +
                                                      (sl_ctxf["users"] if chh == 0 and sel == 0 else []),
                                                sem="dve")
                                if chh == 0 and sel == 0:
                                    sl_ctxf["users"] = []
                                sl_pC[sel]["users"].append(tve[sel])
                            tl0 = p.do("scalar", lambda g: g.activation(
                                rrow[0:1, :], pCb[64:65, :], AF.Ln,
                                bias=0.0, scale=1.0),
                                waits=[tcs[0], tcs[1]] + sl_rr["users"], sem="act")
                            sl_rr["users"] = []
                            te0 = p.do("scalar", lambda g: g.activation(
                                rrowb[0:1, :], rrow[0:1, :], AF.Exp,
                                bias=0.0, scale=-1.0),
                                waits=[tl0] + sl_rrb["users"], sem="act")
                            sl_rrb["users"] = []
                            te1 = te0
                            sl_rr["users"] += [te0]

                            def norm(chh=chh, tcs=tcs, te0=te0, te1=te1, tve=tve):
                                tb0 = p.do("tensor", lambda g: g.matmul(
                                    pO[0][:, :], onesr[0:1, :], rrowb[0:1, 0:512],
                                    start=True, stop=True),
                                    waits=[te0] + sl_pS0[1]["users"], sem="pe")
                                sl_pS0[1]["users"] = []
                                tb1 = p.do("tensor", lambda g: g.matmul(
                                    pO[1][:, :], onesr[0:1, :], rrowb[0:1, 512:1024],
                                    start=True, stop=True),
                                    waits=[te1], sem="pe")
                                sl_rrb["users"] += [tb0, tb1]
                                tr4a = p.do("vector", lambda g: g.tensor_copy(
                                    rbt[:, 0:512], pO[0][:, :]),
                                    waits=[tb0] + sl_rbt["users"], sem="dve")
                                sl_rbt["users"] = []
                                tr4 = p.do("vector", lambda g: g.tensor_copy(
                                    rbt[:, 512:1024], pO[1][:, :]),
                                    waits=[tb1], sem="dve")
                                sl_pS0[1]["users"] += [tr4a, tr4]
                                for sel in range(2):
                                    tev = p.do("vector",
                                               lambda g, sel=sel, chh=chh: g.tensor_tensor(
                                                   ctxf[sel * 64:sel * 64 + 64,
                                                        chh * 512:(chh + 1) * 512],
                                                   ctxf[sel * 64:sel * 64 + 64,
                                                        chh * 512:(chh + 1) * 512],
                                                   rbt[sel * 64:sel * 64 + 64,
                                                       sel * 512:sel * 512 + 512],
                                                   OP.mult),
                                               waits=[tr4a if sel == 0 else tr4, tve[sel]],
                                               sem="dve")
                                    sl_rbt["users"].append(tev)
                                    ctx_done.append(tev)
                            pend["fn"] = norm
                        flush_norm()
                        for sub in range(4):
                            tso = m * 512 + sub * 128
                            iy = sub % 2
                            tpo = [None, None]
                            for hf in range(2):
                                for dc in range(NCH):
                                    p.do("tensor",
                                         lambda g, dc=dc, sub=sub, hf=hf: g.matmul(
                                             pO[hf][:, :],
                                             ctxf[:, dc * 512 + sub * 128: dc * 512 + sub * 128 + 128],
                                             wO[:, dc * D + hf * 512: dc * D + hf * 512 + 512],
                                             start=(dc == 0), stop=False),
                                         waits=([twall] + ctx_done + sl_pS0[1]["users"]) if (dc == 0 and hf == 0) else [],
                                         sem=None)
                                if hf == 0:
                                    sl_pS0[1]["users"] = []
                                tpo[hf] = p.do("tensor",
                                               lambda g, hf=hf: g.matmul(
                                                   pO[hf][:, :], onesr[32:33, :],
                                                   brow[32:33, D + hf * 512: D + hf * 512 + 512],
                                                   start=False, stop=True), sem="pe")
                            tlx = dma("sync", f"dT{iy}", xrs[iy][:], RES["ap"][tso:tso + 128, :],
                                      waits=sl_xr[iy]["users"])
                            sl_xr[iy]["users"] = []
                            ta0 = p.do("vector",
                                       lambda g, iy=iy: g.tensor_tensor(
                                           xrs[iy][:, 0:512], pO[0][:, :],
                                           xrs[iy][:, 0:512], OP.add),
                                       waits=[tpo[0], tlx], sem="dve")
                            ta1 = p.do("vector",
                                       lambda g, iy=iy: g.tensor_tensor(
                                           xrs[iy][:, 512:1024], pO[1][:, :],
                                           xrs[iy][:, 512:1024], OP.add),
                                       waits=[tpo[1]], sem="dve")
                            sl_pS0[1]["users"] += [ta0, ta1]
                            sl_ctxf["users"].append(tpo[1])
                            tyw = dma("gpsimd", "dY", yout[tso:tso + 128, :], xrs[iy][:],
                                      waits=[ta0, ta1])
                            sl_xr[iy]["users"] += [tyw]
                    p.barrier()
                    p.flush(block, semh)
                RES["ap"] = yout

        if "F2" in phases:
            ffn_phase("v")

        p.wait("sync", [p.tok("dY")])
        p.flush(block, semh)

    return nc


def kernel(**inputs):
    x_full = np.ascontiguousarray(inputs["L0_attn_a_x"])
    consts = _host_consts()
    wmap = {
        "aWqkv": inputs["L0_attn_a_W_qkv"], "aBqkv": inputs["L0_attn_a_b_qkv"],
        "aWo": inputs["L0_attn_a_W_out"], "aBo": inputs["L0_attn_a_b_out"],
        "bWq": inputs["L0_attn_b_Wq"], "bBq": inputs["L0_attn_b_bq"],
        "bWk": inputs["L0_attn_b_Wk"], "bBk": inputs["L0_attn_b_bk"],
        "bWv": inputs["L0_attn_b_Wv"], "bBv": inputs["L0_attn_b_bv"],
        "bWo": inputs["L0_attn_b_Wo"], "bBo": inputs["L0_attn_b_bo"],
        "rw": inputs["L0_ffn_router_w"], "w1": inputs["L0_ffn_w1"],
        "w2": inputs["L0_ffn_w2"], "sgu": inputs["L0_ffn_shared_gate_up"],
        "sdn": inputs["L0_ffn_shared_down"], "sgw": inputs["L0_ffn_shared_gate_w"],
    }
    wmap = {k: np.ascontiguousarray(np.asarray(v, dtype=np.float32))
            for k, v in wmap.items()}
    nc = build_nc(_PHASES)
    n = 8
    in_maps = []
    for i in range(n):
        mp = {"x": np.ascontiguousarray(x_full[i])}
        mp.update(wmap)
        mp.update(consts)
        in_maps.append(mp)
    res = run_bass_kernel_spmd(nc, in_maps, core_ids=list(range(n)))
    return np.stack([r["y"] for r in res.results], axis=0)


# version 62
# speedup vs baseline: 1.1464x; 1.1464x over previous
import numpy as np
import ml_dtypes
from contextlib import ExitStack
import concourse.bass as bass
import concourse.mybir as mybir
from concourse.bass_utils import run_bass_kernel_spmd
from concourse.alu_op_type import AluOpType as OP

F32 = mybir.dt.float32
BF16 = mybir.dt.bfloat16
AF = mybir.ActivationFunctionType
AX = mybir.AxisListType

S, D = 4096, 1024
NTOK = S // 128
NCH = D // 128
EPS = 1e-5
NEG_SHIFT = -10.0
SC_A = 1.0 / 32.0
SC_B = 0.125
CONST_VALS = [0.0, 1.0, EPS, -0.5, NEG_SHIFT, SC_A, SC_B, -1.0]

ENGINES = ("sync", "tensor", "vector", "scalar", "gpsimd")
_PHASES = ("A", "F1", "B", "F2")


class Prog:
    def __init__(self):
        self.ops = {e: [] for e in ENGINES}
        self.cnt = {}
        self.lastw = {e: {} for e in ENGINES}
        self.sems = set()

    def wait(self, eng, toks):
        best = {}
        for t in toks:
            if t is None:
                continue
            s, v = t
            if v > best.get(s, 0):
                best[s] = v
        for s, v in sorted(best.items()):
            if v <= 0 or self.lastw[eng].get(s, 0) >= v:
                continue
            self.lastw[eng][s] = v
            self.ops[eng].append(("w", s, v))

    def do(self, eng, fn, waits=(), sem=None, amt=1):
        self.wait(eng, waits)
        tok = None
        if sem is not None:
            self.sems.add(sem)
            self.cnt[sem] = self.cnt.get(sem, 0) + amt
            tok = (sem, self.cnt[sem])
        self.ops[eng].append(("e", fn, sem, amt))
        return tok

    def tok(self, sem):
        self.sems.add(sem)
        return (sem, self.cnt.get(sem, 0))

    def barrier(self):
        toks = [self.tok(s) for s in sorted(self.sems)]
        for e in ENGINES:
            self.wait(e, toks)

    def flush(self, block, semh):
        for e in ENGINES:
            ops = self.ops[e]
            if not ops:
                continue

            def body(engobj, ops=ops):
                for op in ops:
                    if op[0] == "w":
                        engobj.wait_ge(semh[op[1]], op[2])
                    else:
                        ins = op[1](engobj)
                        if op[2] is not None:
                            ins.then_inc(semh[op[2]], op[3])
            getattr(block, e)(body)
            self.ops[e] = []


def slotinfo():
    return {"users": []}


def _host_consts():
    cst = np.zeros((128, 16), dtype=np.float32)
    for i, v in enumerate(CONST_VALS):
        cst[:, i] = v
    idb = np.eye(128, dtype=ml_dtypes.bfloat16)
    idf = np.eye(128, dtype=np.float32)
    j = np.arange(32, dtype=np.float32)
    inv = (1.0 / (10000.0 ** (j / 32.0))).astype(np.float32)
    t = np.arange(S, dtype=np.float32)
    ang = t[None, :] * inv[:, None]
    c = np.cos(ang).astype(np.float32)
    s = np.sin(ang).astype(np.float32)
    ropeC = np.tile(c, (4, 1)).astype(ml_dtypes.bfloat16)
    sgn = np.repeat(np.array([-1.0, 1.0, -1.0, 1.0], np.float32), 32)
    ropeS = (np.tile(s, (4, 1)) * sgn[:, None]).astype(ml_dtypes.bfloat16)
    p32 = np.zeros((128, 128), dtype=ml_dtypes.bfloat16)
    p32[np.arange(128) ^ 32, np.arange(128)] = 1.0
    return {"cst": cst, "idb": idb, "idf": idf, "ropeC": ropeC,
            "ropeS": ropeS, "p32": p32}


def build_nc(phases=None, dbg=False):
    if phases is None:
        phases = _PHASES
    nc = bass.Bass("TRN2", target_bir_lowering=False)
    xin = nc.dram_tensor("x", [S, D], F32, kind="ExternalInput").ap()
    yout = nc.dram_tensor("y", [S, D], F32, kind="ExternalOutput").ap()
    W = {}
    for nm, shp in [
        ("aWqkv", [D, 3 * D]), ("aBqkv", [3 * D]), ("aWo", [D, D]), ("aBo", [D]),
        ("bWq", [D, D]), ("bBq", [D]), ("bWk", [D, D]), ("bBk", [D]),
        ("bWv", [D, D]), ("bBv", [D]), ("bWo", [D, D]), ("bBo", [D]),
        ("rw", [D, 8]), ("w1", [8, D, D]), ("w2", [8, D, D]),
        ("sgu", [D, 2 * D]), ("sdn", [D, D]), ("sgw", [D, 1]),
    ]:
        W[nm] = nc.dram_tensor(nm, shp, F32, kind="ExternalInput").ap()
    C = {}
    for nm, shp, dt in [("cst", [128, 16], F32), ("idb", [128, 128], BF16),
                        ("idf", [128, 128], F32), ("ropeC", [128, S], BF16),
                        ("ropeS", [128, S], BF16), ("p32", [128, 128], BF16)]:
        C[nm] = nc.dram_tensor(nm, shp, dt, kind="ExternalInput").ap()
    scrA = nc.dram_tensor("scrA", [NCH, 128, S], BF16, kind="Internal").ap()
    RES = {"ap": xin}
    DBG = {}
    if dbg:
        for nm, shp, dt in [("dka", [128, NCH * S], BF16), ("dva", [128, NTOK * D], BF16),
                            ("dxl", [NCH, 128, S], BF16), ("dqf", [128, 2048], BF16),
                            ("dpr", [128, 32 * 256], BF16), ("dct", [128, D], BF16),
                            ("dkb", [128, NCH * S], BF16), ("dqb", [NCH, 128, S], BF16),
                            ("dvx", [128, NTOK * 16 * 65], BF16)]:
            DBG[nm] = nc.dram_tensor(nm, shp, dt, kind="ExternalOutput").ap()
    scrB = nc.dram_tensor("scrB", [NCH, 128, S], BF16, kind="Internal").ap()
    scrV = nc.dram_tensor("scrV", [128, NTOK * 1040], BF16, kind="Internal").ap()

    p = Prog()

    with ExitStack() as TOP:
        E = TOP.enter_context
        semh = {}
        for s_ in ("pe", "dve", "act", "pool", "dS", "dW", "dY", "dA", "dB",
                   "dX0", "dX1", "dQ0", "dQ1", "dT0", "dT1", "dR0", "dR1",
                   "dE0", "dE1"):
            semh[s_] = E(nc.semaphore(s_))
        block = E(nc.Block())

        cst = E(nc.sbuf_tensor("cst_s", [128, 16], F32))
        for i, v in enumerate(CONST_VALS):
            nc.const_aps.register(F32, float(v), cst[:, i:i + 1])
        idb = E(nc.sbuf_tensor("idb_s", [128, 128], BF16))
        idf = E(nc.sbuf_tensor("idf_s", [128, 128], F32))
        onesr = E(nc.sbuf_tensor("onesr", [128, 128], BF16))
        onesc = E(nc.sbuf_tensor("onesc", [128, 1], BF16))
        bias_pp = E(nc.sbuf_tensor("bias_pp", [128, 32], F32))
        brow = E(nc.sbuf_tensor("brow", [128, 2 * D], BF16))
        p32 = E(nc.sbuf_tensor("p32_s", [128, 128], BF16))
        mvt2 = [E(nc.sbuf_tensor(f"mvt{i}", [128, 4], F32)) for i in range(2)]
        rwb_t = E(nc.sbuf_tensor("rwb_t", [128, NCH * 8], BF16))
        sgwb_t = E(nc.sbuf_tensor("sgwb_t", [128, 8], BF16))
        st122 = [E(nc.sbuf_tensor(f"st12{i}", [128, 12], F32)) for i in range(2)]
        xlnb2 = [E(nc.sbuf_tensor(f"xlnb{i}", [128, 1024], BF16)) for i in range(2)]

        def dma(eng, dsem, out, in_, waits=()):
            return p.do(eng, lambda g, o=out, i=in_: g.dma_start(out=o, in_=i),
                        waits=waits, sem=dsem, amt=16)

        def wload(dst3, src3, waits=(), dsem="dW"):
            return dma("gpsimd", dsem, dst3, src3, waits=waits)

        def wsl(w_ap, cols=None):
            a = w_ap if cols is None else w_ap[:, cols[0]:cols[1]]
            return a.rearrange("(c p) m -> p c m", p=128)

        LN = [{"mv": None, "st": None, "xln": None, "ap": None} for _ in range(2)]
        LNCNT = [0]

        def ln_tile(src_dram, tg, xt, sl_xt, dsem, ptr, sl_ptr):
            i = LNCNT[0] % 2
            LNCNT[0] += 1
            st12 = st122[i]
            mvt = mvt2[i]
            xlnb = xlnb2[i]
            L = LN[i]
            tload = dma("sync", dsem, xt[:], src_dram[tg * 128:(tg + 1) * 128, :],
                        waits=sl_xt["users"])
            sl_xt["users"] = []
            t1 = p.do("vector", lambda g, st12=st12: g.bn_stats(st12[:, 0:6], xt[:, 0:512]),
                      waits=[tload, L["st"]], sem="dve")
            t2 = p.do("vector", lambda g, st12=st12: g.bn_stats(st12[:, 6:12], xt[:, 512:1024]),
                      waits=[t1], sem="dve")
            t3 = p.do("vector", lambda g, st12=st12, mvt=mvt: g.bn_aggr(mvt[:, 0:2], st12[:, 0:12]),
                      waits=[t2, L["mv"]], sem="dve")
            t4 = p.do("scalar", lambda g, mvt=mvt: g.activation(mvt[:, 2:3], mvt[:, 1:2], AF.Ln,
                                                                bias=EPS, scale=1.0),
                      waits=[t3, L["ap"]], sem="act")
            t5 = p.do("scalar", lambda g, mvt=mvt: g.activation(mvt[:, 2:3], mvt[:, 2:3], AF.Exp,
                                                                bias=0.0, scale=-0.5),
                      waits=[t4], sem="act")
            t6 = p.do("vector", lambda g, mvt=mvt: g.tensor_scalar(mvt[:, 3:4], mvt[:, 0:1],
                                                                   mvt[:, 2:3], -1.0, OP.mult, OP.mult),
                      waits=[t5], sem="dve")
            t7 = p.do("scalar", lambda g, mvt=mvt, xlnb=xlnb: g.activation(
                xlnb[:], xt[:], AF.Identity, bias=mvt[:, 3:4], scale=mvt[:, 2:3]),
                waits=[t6, L["xln"]], sem="act")
            sl_xt["users"].append(t7)
            L["mv"], L["st"], L["ap"] = t6, t3, t7
            ttr = None
            for c in range(NCH):
                ttr = p.do("tensor",
                           lambda g, c=c, xlnb=xlnb: g.transpose(ptr[:, c * 128:(c + 1) * 128],
                                                                 xlnb[:, c * 128:(c + 1) * 128], idb[:]),
                           waits=([t7] + sl_ptr["users"]) if c == 0 else [],
                           sem="pe" if c == NCH - 1 else None)
            sl_ptr["users"] = []
            L["xln"] = ttr
            return tload, ttr

        dma("sync", "dS", cst[:], C["cst"])
        dma("sync", "dS", idb[:], C["idb"])
        dma("sync", "dS", idf[:], C["idf"])
        dma("sync", "dS", p32[:], C["p32"])
        tS = p.tok("dS")
        p.do("gpsimd", lambda g: g.memset(onesr[:, :], 1.0), sem="pool")
        p.do("gpsimd", lambda g: g.memset(onesc[:, :], 1.0), sem="pool")
        dma("gpsimd", "dW", brow[0:1, 0:D],
            W["aBqkv"][2 * D:3 * D].rearrange("(o d) -> o d", o=1))
        dma("gpsimd", "dW", brow[32:33, 0:D], W["aBo"].rearrange("(o d) -> o d", o=1))
        dma("gpsimd", "dW", brow[0:1, D:2 * D], W["bBv"].rearrange("(o d) -> o d", o=1))
        dma("gpsimd", "dW", brow[32:33, D:2 * D], W["bBo"].rearrange("(o d) -> o d", o=1))
        with nc.sbuf_tensor("btmp", [8, 128], F32) as btmp, \
             nc.sbuf_tensor("btmpb", [8, 128], BF16) as btmpb, \
             nc.psum_tensor("pbias", [128, 512], F32) as pbias, \
             nc.psum_tensor("pbias2", [128, 1024], BF16) as pbias2:
            dma("gpsimd", "dW", rwb_t[:].rearrange("p (c m) -> p c m", m=8), wsl(W["rw"]))
            tprev = None
            for k, src in enumerate([W["aBqkv"][0:D], W["aBqkv"][D:2 * D],
                                     W["bBq"], W["bBk"]]):
                tl = dma("gpsimd", "dW", btmp[:], src.rearrange("(c p) -> c p", p=128),
                         waits=[tprev] if tprev else [])
                tt = p.do("tensor", lambda g: g.transpose(pbias[:, 0:8], btmp[:], idf[0:8, 0:8]),
                          waits=[tl, tS, tprev, p.tok("dW")], sem="pe")
                tprev = p.do("vector",
                             lambda g, k=k: g.tensor_copy(bias_pp[:, k * 8:(k + 1) * 8],
                                                          pbias[:, 0:8]),
                             waits=[tt], sem="dve")
            tlg = dma("gpsimd", "dW", btmpb[:],
                      W["sgw"].rearrange("(c p) m -> c (p m)", p=128))
            ttg = p.do("tensor", lambda g: g.transpose(pbias2[:, 0:8], btmpb[:], idb[0:8, 0:8]),
                       waits=[tlg, tprev, p.tok("dW")], sem="pe")
            p.do("vector", lambda g: g.tensor_copy(sgwb_t[:, 0:8], pbias2[:, 0:8]),
                 waits=[ttg], sem="dve")
            seam0 = [p.tok(x) for x in ("dS", "dW", "pe", "dve", "pool")]
            p.flush(block, semh)

        if "A" in phases:
            with ExitStack() as PH:
                EE = PH.enter_context
                ka = EE(nc.sbuf_tensor("ka", [128, NCH * S], BF16))
                va = EE(nc.sbuf_tensor("va", [128, NTOK * D], BF16))
                wA = EE(nc.sbuf_tensor("wAa", [128, NCH * D], BF16))
                wB = EE(nc.sbuf_tensor("wBa", [128, NCH * D], BF16))
                PA1 = PH.enter_context(ExitStack())
                E1 = PA1.enter_context
                xts = [E1(nc.sbuf_tensor(f"xa{i}", [128, D], F32)) for i in range(2)]
                xfm = [E1(nc.sbuf_tensor(f"xfa{i}", [128, NCH * 256], BF16))
                       for i in range(2)]
                ptr = [E1(nc.psum_tensor(f"ptA{i}", [128, 1024], BF16)) for i in range(2)]
                pk = [E1(nc.psum_tensor(f"pkA{i}", [128, 512], F32)) for i in range(2)]
                pv = [E1(nc.psum_tensor(f"pvA{i}", [128, 512], F32)) for i in range(4)]
                sl_xt = [slotinfo() for _ in range(2)]
                sl_xt[0]["users"] += seam0
                sl_xt[1]["users"] += seam0
                sl_fm = [slotinfo() for _ in range(2)]
                sl_ptr = [slotinfo() for _ in range(2)]
                sl_pk = [slotinfo() for _ in range(2)]
                sl_pv = [slotinfo() for _ in range(4)]

                wload(wA[:].rearrange("p (c m) -> p c m", m=D), wsl(W["aWqkv"], (D, 2 * D)))
                wload(wB[:].rearrange("p (c m) -> p c m", m=D), wsl(W["aWqkv"], (2 * D, 3 * D)))
                twall = p.tok("dW")

                for t in range(NTOK):
                    ix = t % 2
                    ipr = (t // 2) % 2
                    th = t % 2
                    tload, ttr = ln_tile(xin, t, xts[ix], sl_xt[ix], f"dX{ix}",
                                         ptr[ix], sl_ptr[ix])
                    tev = p.do("vector",
                               lambda g, ix=ix, ipr=ipr, th=th: g.tensor_copy(
                                   xfm[ipr][:].rearrange("p (c t) -> p c t", t=256)[:, :, th * 128:(th + 1) * 128],
                                   ptr[ix][:].rearrange("p (c t) -> p c t", t=128)),
                               waits=[ttr] + (sl_fm[ipr]["users"] if th == 0 else []),
                               sem="dve")
                    if th == 0:
                        sl_fm[ipr]["users"] = []
                        tev_first = tev
                    sl_ptr[ix]["users"].append(tev)
                    if th == 1:
                        tw = dma("gpsimd", "dA",
                                 scrA[:, :, (t - 1) * 128:(t + 1) * 128].rearrange("c p t -> p c t"),
                                 xfm[ipr][:].rearrange("p (c t) -> p c t", t=256),
                                 waits=[tev_first, tev])
                        sl_fm[ipr]["users"].append(tw)
                    for hf in range(2):
                        iv = 2 * (t % 2) + hf
                        for dc in range(NCH):
                            p.do("tensor",
                                 lambda g, dc=dc, hf=hf, iv=iv, ipr=ipr, th=th: g.matmul(
                                     pv[iv][:, :],
                                     xfm[ipr][:, dc * 256 + th * 128: dc * 256 + th * 128 + 128],
                                     wB[:, dc * D + hf * 512: dc * D + hf * 512 + 512],
                                     start=(dc == 0), stop=False),
                                 waits=([twall, tev] + sl_pv[iv]["users"]) if dc == 0 else [],
                                 sem=None)
                        tmm = p.do("tensor",
                                   lambda g, hf=hf, iv=iv: g.matmul(
                                       pv[iv][:, :], onesr[0:1, :],
                                       brow[0:1, hf * 512: hf * 512 + 512],
                                       start=False, stop=True), sem="pe")
                        sl_pv[iv]["users"] = []
                        tevv = p.do("vector",
                                    lambda g, t=t, hf=hf, iv=iv: g.tensor_copy(
                                        va[:, t * D + hf * 512: t * D + hf * 512 + 512],
                                        pv[iv][:, :]),
                                    waits=[tmm], sem="dve")
                        sl_pv[iv]["users"].append(tevv)
                        sl_fm[ipr]["users"].append(tevv)
                    if th == 1:
                        for dk in range(NCH):
                            ip = dk % 2
                            tmm = None
                            for dc in range(NCH):
                                tmm = p.do("tensor",
                                           lambda g, dc=dc, dk=dk, ip=ip, ipr=ipr: g.matmul(
                                               pk[ip][:, 0:256],
                                               wA[:, dc * D + dk * 128: dc * D + dk * 128 + 128],
                                               xfm[ipr][:, dc * 256:(dc + 1) * 256],
                                               start=(dc == 0), stop=(dc == NCH - 1)),
                                           waits=([twall, tev_first, tev] + sl_pk[ip]["users"]) if dc == 0 else [],
                                           sem="pe" if dc == NCH - 1 else None)
                            sl_pk[ip]["users"] = []
                            tevk = p.do("vector",
                                        lambda g, dk=dk, ip=ip, t=t: g.tensor_scalar(
                                            ka[:, dk * S + (t - 1) * 128: dk * S + (t - 1) * 128 + 256],
                                            pk[ip][:, 0:256], bias_pp[:, 8 + dk:9 + dk],
                                            None, OP.add),
                                        waits=[tmm], sem="dve")
                            sl_pk[ip]["users"].append(tevk)
                            sl_fm[ipr]["users"].append(tmm)
                if dbg:
                    p.barrier()
                    dma("sync", "dS", DBG["dka"], ka[:])
                    dma("sync", "dS", DBG["dva"], va[:])
                    dma("sync", "dS", DBG["dxl"].rearrange("c p t -> p c t"),
                        scrA.rearrange("c p t -> p c t"))
                    p.barrier()
                p.flush(block, semh)
                PA1.close()

                with ExitStack() as P2:
                    E2 = P2.enter_context
                    wload(wA[:].rearrange("p (c m) -> p c m", m=D), wsl(W["aWqkv"], (0, D)),
                          waits=[p.tok("pe")])
                    wload(wB[:].rearrange("p (c m) -> p c m", m=D), wsl(W["aWo"]))
                    twall = p.tok("dW")
                    seam_a = [p.tok(x) for x in ("pe", "act", "dve", "pool", "dA")]
                    QM = 256
                    xq2 = [E2(nc.sbuf_tensor(f"xqA{i}", [128, NCH * QM], BF16))
                           for i in range(2)]
                    qfm = E2(nc.sbuf_tensor("qfA", [128, NCH * QM], BF16))
                    probs = E2(nc.sbuf_tensor("prA", [128, NTOK * QM], BF16))
                    ctxt = E2(nc.sbuf_tensor("ctA", [128, D], BF16))
                    rcp = E2(nc.sbuf_tensor("rcA", [128, 1], F32))
                    xrs = [E2(nc.sbuf_tensor(f"xrA{i}", [128, D], F32)) for i in range(2)]
                    pS = [E2(nc.psum_tensor(f"pSA{i}", [128, 512], F32)) for i in range(2)]
                    pc0 = E2(nc.psum_tensor("pc0A", [128, 512], F32))
                    pc1 = E2(nc.psum_tensor("pc1A", [128, 512], F32))
                    pcS = E2(nc.psum_tensor("pcSA", [128, 512], F32))
                    ptrx = E2(nc.psum_tensor("ptxA", [128, 1024], BF16))
                    pO = [E2(nc.psum_tensor(f"pOA{i}", [128, 512], F32)) for i in range(2)]
                    sl_xq2 = [slotinfo() for _ in range(2)]
                    sl_qf = slotinfo()
                    sl_pS = [slotinfo() for _ in range(2)]
                    sl_probs = slotinfo()
                    sl_ctx = slotinfo()
                    sl_ctxt = slotinfo()
                    sl_ctxf = slotinfo()
                    sl_ptrx = slotinfo()
                    sl_pO = slotinfo()
                    sl_rcp = slotinfo()
                    sl_xr = [slotinfo() for _ in range(2)]

                    for m in range(S // QM):
                        iq = m % 2
                        xq = xq2[iq]
                        sl_xq = sl_xq2[iq]
                        tldq = dma("sync", f"dQ{iq}",
                                   xq[:].rearrange("p (c t) -> p c t", t=QM),
                                   scrA[:, :, m * QM:(m + 1) * QM].rearrange("c p t -> p c t"),
                                   waits=sl_xq["users"] + [p.tok("dA")] +
                                         (seam_a if m == 0 else []))
                        sl_xq["users"] = []
                        tevq = None
                        for dq in range(NCH):
                            ip = dq % 2
                            tmm = None
                            for dc in range(NCH):
                                tmm = p.do("tensor",
                                           lambda g, dc=dc, dq=dq, ip=ip, xq=xq: g.matmul(
                                               pS[ip][:, 0:QM],
                                               wA[:, dc * D + dq * 128: dc * D + dq * 128 + 128],
                                               xq[:, dc * QM:(dc + 1) * QM],
                                               start=(dc == 0), stop=(dc == NCH - 1)),
                                           waits=([twall, tldq] + sl_pS[ip]["users"]) if dc == 0 else [],
                                           sem="pe" if dc == NCH - 1 else None)
                            sl_pS[ip]["users"] = []
                            tevq = p.do("vector",
                                        lambda g, dq=dq, ip=ip: g.tensor_scalar(
                                            qfm[:, dq * QM:(dq + 1) * QM],
                                            pS[ip][:, 0:QM], bias_pp[:, dq:dq + 1],
                                            None, OP.add),
                                        waits=[tmm] + (sl_qf["users"] if dq == 0 else []),
                                        sem="dve")
                            sl_pS[ip]["users"].append(tevq)
                            sl_xq["users"].append(tevq)
                            if dq == 0:
                                sl_qf["users"] = []
                        tq = tevq
                        texp_last = None
                        for kt in range(NTOK):
                            ip = kt % 2
                            tmm = None
                            for dc in range(NCH):
                                tmm = p.do("tensor",
                                           lambda g, dc=dc, kt=kt, ip=ip: g.matmul(
                                               pS[ip][:, 0:QM],
                                               ka[:, dc * S + kt * 128: dc * S + kt * 128 + 128],
                                               qfm[:, dc * QM:(dc + 1) * QM],
                                               start=(dc == 0), stop=(dc == NCH - 1)),
                                           waits=([tq] + sl_pS[ip]["users"] +
                                                  (sl_probs["users"] if kt == 0 else [])) if dc == 0 else [],
                                           sem="pe" if dc == NCH - 1 else None)
                            sl_pS[ip]["users"] = []
                            if kt == 0:
                                sl_probs["users"] = []
                            texp = p.do("scalar",
                                        lambda g, kt=kt, ip=ip: g.activation(
                                            probs[:, kt * QM:(kt + 1) * QM], pS[ip][:, 0:QM],
                                            AF.Exp, bias=NEG_SHIFT, scale=SC_A),
                                        waits=[tmm], sem="act")
                            sl_pS[ip]["users"].append(texp)
                            sl_qf["users"].append(tmm)
                            texp_last = texp
                        for qs in range(2):
                            tmmS = None
                            for kt in range(NTOK):
                                w0 = ([texp_last] + sl_ctx["users"]) if kt == 0 else []
                                p.do("tensor",
                                     lambda g, kt=kt, qs=qs: g.matmul(
                                         pc0[:, :],
                                         probs[:, kt * QM + qs * 128: kt * QM + qs * 128 + 128],
                                         va[:, kt * D: kt * D + 512],
                                         start=(kt == 0), stop=(kt == NTOK - 1)),
                                     waits=w0, sem=None)
                                p.do("tensor",
                                     lambda g, kt=kt, qs=qs: g.matmul(
                                         pc1[:, :],
                                         probs[:, kt * QM + qs * 128: kt * QM + qs * 128 + 128],
                                         va[:, kt * D + 512: kt * D + 1024],
                                         start=(kt == 0), stop=(kt == NTOK - 1)), sem=None)
                                tmmS = p.do("tensor",
                                            lambda g, kt=kt, qs=qs: g.matmul(
                                                pcS[:, 0:1],
                                                probs[:, kt * QM + qs * 128: kt * QM + qs * 128 + 128],
                                                onesc[:, :],
                                                start=(kt == 0), stop=(kt == NTOK - 1)),
                                            sem="pe" if kt == NTOK - 1 else None)
                            sl_ctx["users"] = []
                            if qs == 1:
                                sl_probs["users"].append(tmmS)
                            trc = p.do("vector", lambda g: g.reciprocal(rcp[:, :], pcS[:, 0:1]),
                                       waits=[tmmS] + sl_rcp["users"], sem="dve")
                            sl_rcp["users"] = []
                            tc0 = p.do("vector",
                                       lambda g: g.tensor_scalar(
                                           ctxt[:, 0:512], pc0[:, :],
                                           rcp[:, 0:1], None, OP.mult),
                                       waits=[trc] + sl_ctxt["users"], sem="dve")
                            sl_ctxt["users"] = []
                            tc1 = p.do("vector",
                                       lambda g: g.tensor_scalar(
                                           ctxt[:, 512:1024], pc1[:, :],
                                           rcp[:, 0:1], None, OP.mult),
                                       waits=[trc], sem="dve")
                            sl_rcp["users"] += [tc0, tc1]
                            sl_ctx["users"] += [tc0, tc1]
                            if dbg and m == 0 and qs == 0:
                                dma("sync", "dS", DBG["dqf"], qfm[:], waits=[tq])
                                dma("sync", "dS", DBG["dpr"], probs[:], waits=[texp_last])
                                dma("sync", "dS", DBG["dct"], ctxt[:], waits=[tc0, tc1])
                            ttr = None
                            for c in range(NCH):
                                ttr = p.do("tensor",
                                           lambda g, c=c: g.transpose(
                                               ptrx[:, c * 128:(c + 1) * 128],
                                               ctxt[:, c * 128:(c + 1) * 128],
                                               idb[:]),
                                           waits=([tc0, tc1] + sl_ptrx["users"]) if c == 0 else [],
                                           sem="pe" if c == NCH - 1 else None)
                            sl_ptrx["users"] = []
                            tevc = p.do("vector", lambda g: g.tensor_copy(qfm[:, 0:D], ptrx[:]),
                                        waits=[ttr] + sl_ctxf["users"] + sl_qf["users"],
                                        sem="dve")
                            sl_ptrx["users"].append(tevc)
                            sl_ctxt["users"].append(ttr)
                            sl_ctxf["users"] = []
                            tso = m * QM + qs * 128
                            tpo = [None, None]
                            for hf in range(2):
                                for dc in range(NCH):
                                    p.do("tensor",
                                         lambda g, dc=dc, hf=hf: g.matmul(
                                             pO[hf][:, :],
                                             qfm[:, dc * 128:(dc + 1) * 128],
                                             wB[:, dc * D + hf * 512: dc * D + hf * 512 + 512],
                                             start=(dc == 0), stop=False),
                                         waits=([twall, tevc] + sl_pO["users"]) if (dc == 0 and hf == 0) else [],
                                         sem=None)
                                if hf == 0:
                                    sl_pO["users"] = []
                                tpo[hf] = p.do("tensor",
                                               lambda g, hf=hf: g.matmul(
                                                   pO[hf][:, :], onesr[32:33, :],
                                                   brow[32:33, hf * 512: hf * 512 + 512],
                                                   start=False, stop=True), sem="pe")
                            iy = qs
                            tlx = dma("sync", f"dT{iy}", xrs[iy][:], xin[tso:tso + 128, :],
                                      waits=sl_xr[iy]["users"])
                            sl_xr[iy]["users"] = []
                            ta0 = p.do("vector",
                                       lambda g, iy=iy: g.tensor_tensor(
                                           xrs[iy][:, 0:512], pO[0][:, :],
                                           xrs[iy][:, 0:512], OP.add),
                                       waits=[tpo[0], tlx], sem="dve")
                            ta1 = p.do("vector",
                                       lambda g, iy=iy: g.tensor_tensor(
                                           xrs[iy][:, 512:1024], pO[1][:, :],
                                           xrs[iy][:, 512:1024], OP.add),
                                       waits=[tpo[1]], sem="dve")
                            sl_pO["users"] += [ta0, ta1]
                            sl_ctxf["users"].append(tpo[1])
                            sl_qf["users"].append(tpo[1])
                            tyw = dma("gpsimd", "dY", yout[tso:tso + 128, :], xrs[iy][:],
                                      waits=[ta0, ta1])
                            sl_xr[iy]["users"] += [tyw]
                    p.barrier()
                    p.flush(block, semh)
                RES["ap"] = yout

        def ffn_phase(tag):
            with ExitStack() as PH:
                EE = PH.enter_context
                xlnfm = EE(nc.sbuf_tensor(f"xl{tag}", [128, NCH * 2048], BF16))
                acc = EE(nc.sbuf_tensor(f"ac{tag}", [128, 16 * D], F32))
                combine = EE(nc.sbuf_tensor(f"cb{tag}", [128, 16 * 8], F32))
                gb = EE(nc.sbuf_tensor(f"gb{tag}", [128, 2048], BF16))
                growb = EE(nc.sbuf_tensor(f"gw{tag}", [1, 512], BF16))
                ev8 = EE(nc.sbuf_tensor(f"e8{tag}", [128, 8], F32))
                ev2 = EE(nc.sbuf_tensor(f"e2{tag}", [128, 8], F32))
                sc8 = EE(nc.sbuf_tensor(f"s8{tag}", [128, 16], F32))
                h_sb = EE(nc.sbuf_tensor(f"hs{tag}", [128, NCH * 512], BF16))
                midg = EE(nc.sbuf_tensor(f"mg{tag}", [128, NCH * 512], BF16))
                sgt = EE(nc.sbuf_tensor(f"sgt{tag}", [128, 512], BF16))
                mid1 = EE(nc.sbuf_tensor(f"m1{tag}", [128, 512], BF16))
                xts = [EE(nc.sbuf_tensor(f"xt{tag}{i}", [128, D], F32)) for i in range(2)]
                xre = [EE(nc.sbuf_tensor(f"xr{tag}{i}", [128, 1024], F32)) for i in range(2)]
                ptr = [EE(nc.psum_tensor(f"pt{tag}{i}", [128, 1024], BF16)) for i in range(2)]
                pA = [EE(nc.psum_tensor(f"pa{tag}{i}", [128, 512], F32)) for i in range(2)]
                pB = [EE(nc.psum_tensor(f"pb{tag}{i}", [128, 512], F32)) for i in range(2)]
                pY = [EE(nc.psum_tensor(f"py{tag}{i}", [128, 512], F32)) for i in range(2)]
                sl_ptr = [slotinfo() for _ in range(2)]
                sl_pA = [slotinfo() for _ in range(2)]
                sl_pB = [slotinfo() for _ in range(2)]
                sl_pY = [slotinfo() for _ in range(2)]
                sl_xt = [slotinfo() for _ in range(2)]
                sl_hsb = slotinfo()
                sl_midg = slotinfo()
                sl_sgt = slotinfo()
                sl_mid1 = slotinfo()
                sl_gb = slotinfo()
                sl_acc = slotinfo()
                sl_ev = slotinfo()
                sl_cb = slotinfo()
                sl_xe = [slotinfo() for _ in range(2)]
                sl_xlq = [slotinfo() for _ in range(4)]
                sl_exw = slotinfo()

                rwb = rwb_t
                trw = p.tok("dW")

                for half in range(2):
                    hb = half * 2048
                    tx_done = []
                    for tl in range(16):
                        ix = tl % 2
                        tg = (hb // 128) + tl
                        tload, ttr = ln_tile(RES["ap"], tg, xts[ix], sl_xt[ix], f"dX{ix}",
                                             ptr[ix], sl_ptr[ix])
                        tev = p.do("scalar",
                                   lambda g, ix=ix, tl=tl: g.copy(
                                       xlnfm[:].rearrange("p (c t) -> p c t", t=2048)[:, :, tl * 128:(tl + 1) * 128],
                                       ptr[ix][:].rearrange("p (c t) -> p c t", t=128)),
                                   waits=[ttr] + sl_xlq[tl // 4]["users"],
                                   sem="act")
                        sl_xlq[tl // 4]["users"] = []
                        sl_ptr[ix]["users"].append(tev)
                        tx_done.append(tev)
                    for tl in range(16):
                        ip = tl % 2
                        tmm = None
                        for dc in range(NCH):
                            tmm = p.do("tensor",
                                       lambda g, dc=dc, tl=tl, ip=ip: g.matmul(
                                           pA[ip][:, 0:8],
                                           xlnfm[:, dc * 2048 + tl * 128: dc * 2048 + tl * 128 + 128],
                                           rwb[:, dc * 8:(dc + 1) * 8],
                                           start=(dc == 0), stop=(dc == NCH - 1)),
                                       waits=([trw, tx_done[tl]] + sl_pA[ip]["users"]) if dc == 0 else [],
                                       sem="pe" if dc == NCH - 1 else None)
                        sl_pA[ip]["users"] = []
                        t1 = p.do("vector", lambda g, ip=ip: g.reduce_max(
                            sc8[:, 0:1], pA[ip][:, 0:8], axis=AX.X, negate=True),
                            waits=[tmm] + sl_ev["users"], sem="dve")
                        sl_ev["users"] = []
                        t2 = p.do("scalar", lambda g, ip=ip: g.activation(
                            ev8[:, :], pA[ip][:, 0:8], AF.Exp, bias=sc8[:, 0:1], scale=1.0),
                            waits=[t1], sem="act")
                        sl_pA[ip]["users"].append(t2)
                        t3 = p.do("vector", lambda g: g.reduce_sum(sc8[:, 1:2], ev8[:, :],
                                                                   axis=AX.X),
                                  waits=[t2], sem="dve")
                        t4 = p.do("vector", lambda g: g.reciprocal(sc8[:, 2:3], sc8[:, 1:2]),
                                  waits=[t3], sem="dve")
                        t5 = p.do("vector", lambda g: g.max(out=sc8[:, 8:16], in_=ev8[:, :]),
                                  waits=[t2], sem="dve")
                        t6 = p.do("vector", lambda g: g.tensor_scalar(
                            ev2[:, :], ev8[:, :], sc8[:, 9:10], None, OP.is_ge),
                            waits=[t5], sem="dve")
                        t7 = p.do("vector", lambda g: g.tensor_mul(ev2[:, :], ev2[:, :], ev8[:, :]),
                                  waits=[t6], sem="dve")
                        t8 = p.do("vector", lambda g, tl=tl: g.tensor_scalar(
                            combine[:, tl * 8:(tl + 1) * 8], ev2[:, :], sc8[:, 2:3],
                            None, OP.mult),
                            waits=[t7, t4] + (sl_cb["users"] if tl == 0 else []), sem="dve")
                        if tl == 0:
                            sl_cb["users"] = []
                        sl_ev["users"] += [t7, t8]
                    tcomb = t8
                    tg4 = None
                    tgb_prev = None
                    for n in range(4):
                        ip = n % 2
                        tmm = None
                        for dc in range(NCH):
                            tmm = p.do("tensor",
                                       lambda g, n=n, dc=dc, ip=ip: g.matmul(
                                           pA[ip][0:1, :],
                                           sgwb_t[:, dc:dc + 1],
                                           xlnfm[:, dc * 2048 + n * 512: dc * 2048 + n * 512 + 512],
                                           start=(dc == 0), stop=(dc == NCH - 1)),
                                       waits=([trw] + tx_done + sl_pA[ip]["users"]) if dc == 0 else [],
                                       sem="pe" if dc == NCH - 1 else None)
                        sl_pA[ip]["users"] = []
                        tg4 = p.do("scalar", lambda g, ip=ip: g.activation(
                            growb[0:1, :], pA[ip][0:1, :], AF.Sigmoid,
                            bias=0.0, scale=1.0),
                            waits=[tmm] + ([tgb_prev] if tgb_prev else []), sem="act")
                        sl_pA[ip]["users"].append(tg4)
                        tb = p.do("tensor", lambda g, ip=ip: g.matmul(
                            pB[ip][:, :], onesr[0:1, :], growb[0:1, :],
                            start=True, stop=True),
                            waits=[tg4] + sl_pB[ip]["users"], sem="pe")
                        sl_pB[ip]["users"] = []
                        tb2 = p.do("vector", lambda g, n=n, ip=ip: g.tensor_copy(
                            gb[:, n * 512:(n + 1) * 512], pB[ip][:, :]),
                            waits=[tb] + (sl_gb["users"] if n == 0 else []), sem="dve")
                        if n == 0:
                            sl_gb["users"] = []
                        sl_pB[ip]["users"].append(tb2)
                        sl_gb["users"].append(tb2)
                        tgb_prev = tb
                    tgb = tb2
                    with ExitStack() as SH:
                        ES = SH.enter_context
                        wsgA = ES(nc.sbuf_tensor(f"wga{tag}{half}", [128, NCH * D], BF16))
                        wsgB = ES(nc.sbuf_tensor(f"wgb{tag}{half}", [128, NCH * D], BF16))
                        wsdn = ES(nc.sbuf_tensor(f"wgd{tag}{half}", [128, NCH * D], BF16))
                        wload(wsgA[:].rearrange("p (c m) -> p c m", m=D), wsl(W["sgu"], (0, D)),
                              waits=sl_exw["users"] if half else [])
                        wload(wsgB[:].rearrange("p (c m) -> p c m", m=D),
                              wsl(W["sgu"], (D, 2 * D)))
                        wload(wsdn[:].rearrange("p (c m) -> p c m", m=D), wsl(W["sdn"]))
                        tsh = p.tok("dW")
                        for mm in range(4):
                            tmid_done = None
                            for fc in range(NCH):
                                ia = fc % 2
                                tga = None
                                for dc in range(NCH):
                                    tga = p.do("tensor",
                                               lambda g, dc=dc, fc=fc, mm=mm, ia=ia: g.matmul(
                                                   pA[ia][:, :],
                                                   wsgA[:, dc * D + fc * 128: dc * D + fc * 128 + 128],
                                                   xlnfm[:, dc * 2048 + mm * 512: dc * 2048 + mm * 512 + 512],
                                                   start=(dc == 0), stop=(dc == NCH - 1)),
                                               waits=([tsh] + tx_done + sl_pA[ia]["users"]) if dc == 0 else [],
                                               sem="pe" if dc == NCH - 1 else None)
                                sl_pA[ia]["users"] = []
                                tgu = None
                                for dc in range(NCH):
                                    tgu = p.do("tensor",
                                               lambda g, dc=dc, fc=fc, mm=mm, ia=ia: g.matmul(
                                                   pB[ia][:, :],
                                                   wsgB[:, dc * D + fc * 128: dc * D + fc * 128 + 128],
                                                   xlnfm[:, dc * 2048 + mm * 512: dc * 2048 + mm * 512 + 512],
                                                   start=(dc == 0), stop=(dc == NCH - 1)),
                                               waits=([tsh] + sl_pB[ia]["users"]) if dc == 0 else [],
                                               sem="pe" if dc == NCH - 1 else None)
                                sl_pB[ia]["users"] = []
                                ts1 = p.do("scalar", lambda g, ia=ia: g.activation(
                                    sgt[:, :], pA[ia][:, :], AF.Silu, bias=0.0, scale=1.0),
                                    waits=[tga] + sl_sgt["users"], sem="act")
                                sl_sgt["users"] = []
                                sl_pA[ia]["users"].append(ts1)
                                ts2 = p.do("vector", lambda g, ia=ia: g.tensor_tensor(
                                    mid1[:, :], sgt[:, :], pB[ia][:, :], OP.mult),
                                    waits=[ts1, tgu] + sl_mid1["users"], sem="dve")
                                sl_mid1["users"] = []
                                sl_sgt["users"].append(ts2)
                                sl_pB[ia]["users"].append(ts2)
                                ts3 = p.do("vector", lambda g, fc=fc, mm=mm: g.tensor_tensor(
                                    midg[:, fc * 512:(fc + 1) * 512], mid1[:, :],
                                    gb[:, mm * 512:(mm + 1) * 512], OP.mult),
                                    waits=[ts2, tgb] + (sl_midg["users"] if fc == 0 else []),
                                    sem="dve")
                                if fc == 0:
                                    sl_midg["users"] = []
                                sl_mid1["users"].append(ts3)
                                sl_gb["users"].append(ts3)
                                tmid_done = ts3
                            for sub in range(4):
                                tlc = mm * 4 + sub
                                for hf in range(2):
                                    iy = hf
                                    tmm = None
                                    for fc in range(NCH):
                                        tmm = p.do("tensor",
                                                   lambda g, fc=fc, sub=sub, hf=hf, iy=iy: g.matmul(
                                                       pY[iy][:, :],
                                                       midg[:, fc * 512 + sub * 128: fc * 512 + sub * 128 + 128],
                                                       wsdn[:, fc * D + hf * 512: fc * D + hf * 512 + 512],
                                                       start=(fc == 0), stop=(fc == NCH - 1)),
                                                   waits=([tsh, tmid_done] + sl_pY[iy]["users"]) if fc == 0 else [],
                                                   sem="pe" if fc == NCH - 1 else None)
                                    sl_pY[iy]["users"] = []
                                    tac = p.do("vector",
                                               lambda g, tlc=tlc, hf=hf, iy=iy: g.tensor_copy(
                                                   acc[:, tlc * D + hf * 512: tlc * D + hf * 512 + 512],
                                                   pY[iy][:, :]),
                                               waits=[tmm] + (sl_acc["users"] if (mm == 0 and sub == 0 and hf == 0) else []),
                                               sem="dve")
                                    if mm == 0 and sub == 0 and hf == 0:
                                        sl_acc["users"] = []
                                    sl_pY[iy]["users"].append(tac)
                                    sl_midg["users"].append(tmm)
                    acc_final = [[] for _ in range(16)]
                    sl_shr = [p.tok("pe")]
                    with ExitStack() as EX:
                        ES = EX.enter_context
                        w1b = [ES(nc.sbuf_tensor(f"w1{tag}{half}{i}", [128, NCH * D], BF16))
                               for i in range(2)]
                        w2b0 = ES(nc.sbuf_tensor(f"w2{tag}{half}", [128, NCH * D], BF16))
                        w2b = [w2b0, w2b0]
                        sl_we = [slotinfo() for _ in range(2)]
                        sl_w2 = slotinfo()
                        for e in range(8):
                            ie = e % 2
                            wl = sl_we[ie]["users"]
                            if e < 2:
                                wl = wl + sl_shr + sl_exw["users"]
                            wload(w1b[ie][:].rearrange("p (c m) -> p c m", m=D),
                                  W["w1"][e].rearrange("(c p) m -> p c m", p=128),
                                  waits=wl, dsem=f"dE{ie}")
                            twe = wload(w2b[ie][:].rearrange("p (c m) -> p c m", m=D),
                                        W["w2"][e].rearrange("(c p) m -> p c m", p=128),
                                        waits=sl_w2["users"], dsem=f"dE{ie}")
                            sl_w2["users"] = []
                            sl_we[ie]["users"] = []
                            for mm in range(4):
                                th_done = None
                                for fc in range(NCH):
                                    ia = fc % 2
                                    tmm = None
                                    for dc in range(NCH):
                                        tmm = p.do("tensor",
                                                   lambda g, dc=dc, fc=fc, mm=mm, ia=ia, ie=ie: g.matmul(
                                                       pA[ia][:, :],
                                                       w1b[ie][:, dc * D + fc * 128: dc * D + fc * 128 + 128],
                                                       xlnfm[:, dc * 2048 + mm * 512: dc * 2048 + mm * 512 + 512],
                                                       start=(dc == 0), stop=(dc == NCH - 1)),
                                                   waits=([twe] + sl_pA[ia]["users"] +
                                                          (tx_done if (e == 0 and mm == 0 and fc == 0) else [])) if dc == 0 else [],
                                                   sem="pe" if dc == NCH - 1 else None)
                                    sl_pA[ia]["users"] = []
                                    if fc == NCH - 1:
                                        sl_xlq[mm]["users"] = [tmm]
                                    th = p.do("scalar", lambda g, fc=fc, ia=ia: g.activation(
                                        h_sb[:, fc * 512:(fc + 1) * 512], pA[ia][:, :],
                                        AF.Silu, bias=0.0, scale=1.0),
                                        waits=[tmm] + (sl_hsb["users"] if fc == 0 else []),
                                        sem="act")
                                    if fc == 0:
                                        sl_hsb["users"] = []
                                    sl_pA[ia]["users"].append(th)
                                    th_done = th
                                sl_we[ie]["users"].append(th_done)
                                for sub in range(4):
                                    tlc = mm * 4 + sub
                                    for hf in range(2):
                                        iy = hf
                                        tmm = None
                                        for fc in range(NCH):
                                            tmm = p.do("tensor",
                                                       lambda g, fc=fc, sub=sub, hf=hf, iy=iy, ie=ie: g.matmul(
                                                           pY[iy][:, :],
                                                           h_sb[:, fc * 512 + sub * 128: fc * 512 + sub * 128 + 128],
                                                           w2b[ie][:, fc * D + hf * 512: fc * D + hf * 512 + 512],
                                                           start=(fc == 0), stop=(fc == NCH - 1)),
                                                       waits=([twe, th_done] + sl_pY[iy]["users"]) if fc == 0 else [],
                                                       sem="pe" if fc == NCH - 1 else None)
                                        sl_pY[iy]["users"] = []
                                        tac = p.do("vector",
                                                   lambda g, tlc=tlc, hf=hf, iy=iy, e=e: g.scalar_tensor_tensor(
                                                       acc[:, tlc * D + hf * 512: tlc * D + hf * 512 + 512],
                                                       pY[iy][:, :],
                                                       combine[:, tlc * 8 + e: tlc * 8 + e + 1],
                                                       acc[:, tlc * D + hf * 512: tlc * D + hf * 512 + 512],
                                                       OP.mult, OP.add),
                                                   waits=[tmm, tcomb], sem="dve")
                                        sl_pY[iy]["users"].append(tac)
                                        sl_we[ie]["users"].append(tac)
                                        sl_hsb["users"].append(tmm)
                                        sl_acc["users"].append(tac)
                                        sl_cb["users"].append(tac)
                                        sl_w2["users"].append(tac)
                                        if e == 7:
                                            acc_final[tlc].append(tac)
                        sl_exw["users"] = [p.tok("pe")]
                        for tl in range(16):
                            tg = (hb // 128) + tl
                            ix = tl % 2
                            tlx = dma("gpsimd", f"dT{ix}", xre[ix][:],
                                      RES["ap"][tg * 128:(tg + 1) * 128, :],
                                      waits=sl_xe[ix]["users"])
                            sl_xe[ix]["users"] = []
                            tadd = p.do("gpsimd",
                                        lambda g, tl=tl, ix=ix: g.tensor_tensor(
                                            xre[ix][:],
                                            acc[:, tl * D:(tl + 1) * D],
                                            xre[ix][:], OP.add),
                                        waits=[tlx] + acc_final[tl], sem="pool")
                            sl_acc["users"].append(tadd)
                            tyw = dma("gpsimd", "dY",
                                      yout[tg * 128:(tg + 1) * 128, :],
                                      xre[ix][:], waits=[tadd])
                            sl_xe[ix]["users"] += [tadd, tyw]
                    if half == 1:
                        p.barrier()
                p.flush(block, semh)

        if "F1" in phases:
            ffn_phase("u")
            RES["ap"] = yout

        if "B" in phases:
            with ExitStack() as PH:
                EE = PH.enter_context
                kb = EE(nc.sbuf_tensor("kb", [128, NCH * S], BF16))

                with ExitStack() as P1:
                    E1 = P1.enter_context
                    wA = E1(nc.sbuf_tensor("wqB", [128, NCH * D], BF16))
                    wB = E1(nc.sbuf_tensor("wkB", [128, NCH * D], BF16))
                    wV = E1(nc.sbuf_tensor("wvB", [128, NCH * D], BF16))
                    wload(wA[:].rearrange("p (c m) -> p c m", m=D), wsl(W["bWq"]))
                    wload(wB[:].rearrange("p (c m) -> p c m", m=D), wsl(W["bWk"]))
                    wload(wV[:].rearrange("p (c m) -> p c m", m=D), wsl(W["bWv"]))
                    twall = p.tok("dW")
                    xts = [E1(nc.sbuf_tensor(f"xB{i}", [128, D], F32)) for i in range(2)]
                    xfm = [E1(nc.sbuf_tensor(f"xfB{i}", [128, NCH * 512], BF16))
                           for i in range(2)]
                    qr = E1(nc.sbuf_tensor("qrB", [128, NCH * 512], BF16))
                    kr = E1(nc.sbuf_tensor("krB", [128, NCH * 512], BF16))
                    rCf = E1(nc.sbuf_tensor("rCf", [128, S], BF16))
                    rSf = E1(nc.sbuf_tensor("rSf", [128, S], BF16))
                    tmp2b = [E1(nc.sbuf_tensor(f"tp2B{i}", [128, 512], BF16))
                             for i in range(2)]
                    vstage = [E1(nc.sbuf_tensor(f"vsB{i}", [128, 1040], BF16))
                              for i in range(2)]
                    ptmb = [E1(nc.sbuf_tensor(f"ptmB{i}", [128, 512], F32))
                            for i in range(2)]
                    ptr = [E1(nc.psum_tensor(f"ptB{i}", [128, 1024], BF16)) for i in range(2)]
                    pq = [E1(nc.psum_tensor(f"pqB{i}", [128, 512], F32)) for i in range(2)]
                    psw = [E1(nc.psum_tensor(f"pwB{i}", [128, 512], F32)) for i in range(2)]
                    pv = [E1(nc.psum_tensor(f"pvB{i}", [128, 512], F32)) for i in range(2)]
                    sl_xt = [slotinfo() for _ in range(2)]
                    sl_fm = [slotinfo() for _ in range(2)]
                    sl_ptr = [slotinfo() for _ in range(2)]
                    sl_pq = [slotinfo() for _ in range(2)]
                    sl_psw = [slotinfo() for _ in range(2)]
                    sl_qr = slotinfo()
                    sl_kr = slotinfo()
                    sl_rope = [slotinfo()]
                    sl_tmp1 = [slotinfo() for _ in range(2)]
                    sl_tmp2 = [slotinfo() for _ in range(2)]
                    sl_pv = [slotinfo() for _ in range(2)]
                    sl_vs = [slotinfo() for _ in range(2)]
                    tms0 = p.do("vector", lambda g: g.memset(
                        vstage[0][:].rearrange("p (h j) -> p h j", j=65)[:, :, 64:65], 1.0),
                        sem="dve")
                    tms1 = p.do("vector", lambda g: g.memset(
                        vstage[1][:].rearrange("p (h j) -> p h j", j=65)[:, :, 64:65], 1.0),
                        sem="dve")
                    sl_vs[0]["users"].append(tms0)
                    sl_vs[1]["users"].append(tms1)

                    dma("sync", "dR0", rCf[:], C["ropeC"][:, :])
                    dma("sync", "dR0", rSf[:], C["ropeS"][:, :])
                    trcs_all = p.tok("dR0")
                    for m in range(8):
                        im = m % 2
                        trcs = trcs_all
                        rC = [rCf[:, m * 512:(m + 1) * 512]]
                        rS = [rSf[:, m * 512:(m + 1) * 512]]
                        tfm_done = []
                        for sub in range(4):
                            tg = m * 4 + sub
                            ix = tg % 2
                            tload, ttr = ln_tile(RES["ap"], tg, xts[ix], sl_xt[ix], f"dX{ix}",
                                                 ptr[ix], sl_ptr[ix])
                            tev = p.do("vector",
                                       lambda g, ix=ix, im=im, sub=sub: g.tensor_copy(
                                           xfm[im][:].rearrange("p (c t) -> p c t", t=512)[:, :, sub * 128:(sub + 1) * 128],
                                           ptr[ix][:].rearrange("p (c t) -> p c t", t=128)),
                                       waits=[ttr] + (sl_fm[im]["users"] if sub == 0 else []),
                                       sem="dve")
                            if sub == 0:
                                sl_fm[im]["users"] = []
                            sl_ptr[ix]["users"].append(tev)
                            tfm_done.append(tev)
                            t_glob = m * 4 + sub
                            ivs = t_glob % 2
                            tevv = None
                            for hf in range(2):
                                tmmv = None
                                for dc in range(NCH):
                                    tmmv = p.do("tensor",
                                                lambda g, dc=dc, hf=hf, im=im, sub=sub: g.matmul(
                                                    pv[hf][:, :],
                                                    xfm[im][:, dc * 512 + sub * 128: dc * 512 + sub * 128 + 128],
                                                    wV[:, dc * D + hf * 512: dc * D + hf * 512 + 512],
                                                    start=(dc == 0), stop=False),
                                                waits=([twall, tev] + sl_pv[hf]["users"]) if dc == 0 else [],
                                                sem=None)
                                tbv = p.do("tensor",
                                           lambda g, hf=hf: g.matmul(
                                               pv[hf][:, :], onesr[0:1, :],
                                               brow[0:1, D + hf * 512: D + hf * 512 + 512],
                                               start=False, stop=True), sem="pe")
                                sl_pv[hf]["users"] = []
                                sl_fm[im]["users"].append(tbv)
                                tevv = p.do("vector",
                                            lambda g, hf=hf, ivs=ivs: g.tensor_copy(
                                                vstage[ivs][:].rearrange("p (h j) -> p h j", j=65)[:, hf * 8:(hf + 1) * 8, 0:64],
                                                pv[hf][:, :].rearrange("p (o h j) -> p o h j",
                                                                       o=1, j=64)[:, 0, :, :]),
                                            waits=[tbv] + (sl_vs[ivs]["users"] if hf == 0 else []),
                                            sem="dve")
                                if hf == 0:
                                    sl_vs[ivs]["users"] = []
                                sl_pv[hf]["users"].append(tevv)
                                sl_vs[ivs]["users"].append(tevv)
                            tvw = dma("gpsimd", "dA", scrV[:, t_glob * 1040:(t_glob + 1) * 1040],
                                      vstage[ivs][:], waits=[tevv])
                            sl_vs[ivs]["users"].append(tvw)
                        pend_ropes = []
                        for which in range(2):
                            wbuf = wA if which == 0 else wB
                            raw = qr if which == 0 else kr
                            sl_raw = sl_qr if which == 0 else sl_kr
                            for ch in range(NCH):
                                ip = ch % 2
                                tmm = None
                                for dc in range(NCH):
                                    tmm = p.do("tensor",
                                               lambda g, dc=dc, ch=ch, ip=ip, im=im, wbuf=wbuf: g.matmul(
                                                   pq[ip][:, :],
                                                   wbuf[:, dc * D + ch * 128: dc * D + ch * 128 + 128],
                                                   xfm[im][:, dc * 512:(dc + 1) * 512],
                                                   start=(dc == 0), stop=(dc == NCH - 1)),
                                               waits=([twall] + tfm_done + sl_pq[ip]["users"]) if dc == 0 else [],
                                               sem="pe" if dc == NCH - 1 else None)
                                sl_pq[ip]["users"] = []
                                tev = p.do("vector",
                                           lambda g, ch=ch, ip=ip, raw=raw, which=which: g.tensor_scalar(
                                               raw[:, ch * 512:(ch + 1) * 512], pq[ip][:, :],
                                               bias_pp[:, 16 + which * 8 + ch: 17 + which * 8 + ch],
                                               None, OP.add),
                                           waits=[tmm] + (sl_raw["users"] if ch == 0 else []),
                                           sem="dve")
                                if ch == 0:
                                    sl_raw["users"] = []
                                sl_pq[ip]["users"].append(tev)
                                sl_fm[im]["users"].append(tmm)

                                def mk_rope(ch=ch, tev=tev, raw=raw, which=which,
                                            m=m, trcs=trcs, sl_raw=sl_raw,
                                            rC=rC, rS=rS):
                                    isw = ch % 2
                                    tmp2 = tmp2b[isw]
                                    ptm = ptmb[isw]
                                    tsw = p.do("tensor",
                                               lambda g, ch=ch, raw=raw, isw=isw: g.matmul(
                                                   psw[isw][:, :], p32[:, :],
                                                   raw[:, ch * 512:(ch + 1) * 512],
                                                   start=True, stop=True),
                                               waits=[tev] + sl_psw[isw]["users"], sem="pe")
                                    sl_psw[isw]["users"] = []
                                    e1, s1 = ("gpsimd", "pool") if isw == 0 else ("vector", "dve")
                                    tt1 = p.do(e1,
                                               lambda g, ch=ch, raw=raw, tmp2=tmp2: g.tensor_tensor(
                                                   tmp2[:, :], raw[:, ch * 512:(ch + 1) * 512],
                                                   rC[0][:, :], OP.mult),
                                               waits=[tev, trcs, tsw] + sl_tmp2[isw]["users"], sem=s1)
                                    sl_tmp2[isw]["users"] = []
                                    tt2 = p.do("vector",
                                               lambda g, isw=isw, ptm=ptm: g.tensor_tensor(
                                                   ptm[:, :], psw[isw][:, :], rS[0][:, :],
                                                   OP.mult),
                                               waits=[tsw, trcs] + sl_tmp1[isw]["users"], sem="dve")
                                    sl_tmp1[isw]["users"] = []
                                    sl_psw[isw]["users"].append(tt2)
                                    if which == 0:
                                        dst_ap = raw[:, ch * 512:(ch + 1) * 512]
                                    else:
                                        dst_ap = kb[:, ch * S + m * 512: ch * S + m * 512 + 512]
                                    e2, s2 = ("vector", "dve") if isw == 0 else ("gpsimd", "pool")
                                    tdst = p.do(e2,
                                                lambda g, dst_ap=dst_ap, tmp2=tmp2, ptm=ptm: g.tensor_tensor(
                                                    dst_ap, tmp2[:, :], ptm[:, :], OP.add),
                                                waits=[tt1, tt2], sem=s2)
                                    sl_tmp1[isw]["users"].append(tdst)
                                    sl_tmp2[isw]["users"].append(tdst)
                                    sl_rope[0]["users"] += [tt1, tt2]
                                    sl_raw["users"].append(tdst)
                                pend_ropes.append(mk_rope)
                                if ch % 2 == 1:
                                    for fn in pend_ropes:
                                        fn()
                                    pend_ropes = []
                            if which == 0:
                                tqw = dma("gpsimd", "dB",
                                          scrB[:, :, m * 512:(m + 1) * 512].rearrange("c p t -> p c t"),
                                          qr[:].rearrange("p (c t) -> p c t", t=512),
                                          waits=list(sl_qr["users"]))
                                sl_qr["users"].append(tqw)
                    if dbg:
                        p.barrier()
                        dma("sync", "dS", DBG["dkb"], kb[:])
                        dma("sync", "dS", DBG["dqb"].rearrange("c p t -> p c t"),
                            scrB.rearrange("c p t -> p c t"))
                        p.barrier()
                    p.flush(block, semh)

                with ExitStack() as P2:
                    E2 = P2.enter_context
                    wO = E2(nc.sbuf_tensor("woB", [128, NCH * D], BF16))
                    vxb = E2(nc.sbuf_tensor("vxb", [128, NTOK * 16 * 65], BF16))
                    wload(wO[:].rearrange("p (c m) -> p c m", m=D), wsl(W["bWo"]),
                          waits=[p.tok("pe")])
                    twall = p.tok("dW")
                    seam_b = [p.tok(x) for x in ("pe", "act", "dve", "pool", "dA", "dB")]
                    tvx = dma("scalar", "dA", vxb[:], scrV[:], waits=seam_b)
                    qm2 = [E2(nc.sbuf_tensor(f"qmB{i}", [128, NCH * 512], BF16))
                           for i in range(2)]
                    probs = E2(nc.sbuf_tensor("prB", [128, 4 * 1024], BF16))
                    ctxf = E2(nc.sbuf_tensor("cfB", [128, NCH * 512], BF16))
                    rrow = E2(nc.sbuf_tensor("rrB", [1, 1024], F32))
                    rrowb = E2(nc.sbuf_tensor("rbB", [1, 1024], BF16))
                    rbt = E2(nc.sbuf_tensor("rtB", [128, 1024], BF16))
                    xrs = [E2(nc.sbuf_tensor(f"xrB{i}", [128, D], F32)) for i in range(2)]
                    pSh = E2(nc.psum_tensor("pSB", [128, 2048], F32))
                    pOE = E2(nc.psum_tensor("pOEB", [128, 1024], F32))
                    pCb = E2(nc.psum_tensor("pCB", [128, 1024], F32))
                    pC = [pCb[:, 0:512], pCb[:, 512:1024]]
                    pO = [pOE[:, 0:512], pOE[:, 512:1024]]
                    sl_qm = [slotinfo() for _ in range(2)]
                    sl_pS0 = [slotinfo() for _ in range(2)]
                    sl_pS1 = slotinfo()
                    sl_pC = [slotinfo() for _ in range(2)]
                    sl_pC[0]["users"].append(tvx)
                    sl_pC[1]["users"].append(tvx)
                    sl_pr = [slotinfo() for _ in range(2)]
                    sl_ctxf = slotinfo()
                    sl_rr = slotinfo()
                    sl_rrb = slotinfo()
                    sl_rbt = slotinfo()
                    sl_xr = [slotinfo() for _ in range(2)]

                    for m in range(8):
                        iq = m % 2
                        qm = qm2[iq]
                        tldq = dma("sync", f"dQ{iq}",
                                   qm[:].rearrange("p (c t) -> p c t", t=512),
                                   scrB[:, :, m * 512:(m + 1) * 512].rearrange("c p t -> p c t"),
                                   waits=sl_qm[iq]["users"] + [p.tok("dB")] +
                                         (seam_b if m == 0 else []))
                        sl_qm[iq]["users"] = []
                        ctx_done = []
                        pend = {"fn": None}

                        def flush_norm():
                            if pend["fn"] is not None:
                                pend["fn"]()
                                pend["fn"] = None
                        for chh in range(NCH):
                            texp = [None] * 16

                            def emit_scores(kp, chh=chh, tldq=tldq, texp=texp, qm=qm):
                                mml = [None, None]
                                tee = [None, None]
                                srcE = pSh[:, 0:1024] if kp % 2 == 0 else pOE[:, :]
                                for sel in range(2):
                                    pb = sel * 64
                                    slot = sl_pS0[kp % 2] if sel == 0 else sl_pS1
                                    for j in range(2):
                                        c = 2 * kp + j
                                        if sel == 0:
                                            dst = srcE[:, j * 512:(j + 1) * 512]
                                        else:
                                            dst = pSh[:, 1024 + j * 512: 1024 + j * 512 + 512]
                                        w0 = []
                                        if j == 0:
                                            w0 = slot["users"] + ([tldq] if sel == 0 else [])
                                        mml[sel] = p.do(
                                            "tensor",
                                            lambda g, c=c, pb=pb, chh=chh, dst=dst, qm=qm: g.matmul(
                                                dst,
                                                kb[pb:pb + 64, chh * S + c * 128: chh * S + c * 128 + 128],
                                                qm[pb:pb + 64, chh * 512:(chh + 1) * 512],
                                                start=True, stop=True,
                                                tile_position=(pb, 0)),
                                            waits=w0,
                                            sem="pe" if j == 1 else None)
                                        if j == 0:
                                            slot["users"] = []
                                        else:
                                            sl_qm[m % 2]["users"].append(mml[sel])
                                    tee[sel] = p.do(
                                        "scalar",
                                        lambda g, kp=kp, sel=sel, srcE=srcE: g.activation(
                                            probs[:, (kp % 2) * 2048 + sel * 1024:
                                                  (kp % 2) * 2048 + sel * 1024 + 1024],
                                            srcE if sel == 0 else pSh[:, 1024:2048],
                                            AF.Exp,
                                            bias=NEG_SHIFT, scale=SC_B),
                                        waits=[mml[sel]] +
                                              (sl_pr[kp % 2]["users"] if sel == 0 else []),
                                        sem="act")
                                    if sel == 0:
                                        sl_pr[kp % 2]["users"] = []
                                    slot["users"].append(tee[sel])
                                texp[kp] = tee

                            def emit_ctx(kpc, chh=chh, texp=texp):
                                outt = [None, None]
                                for sel in range(2):
                                    h = 2 * chh + sel
                                    tc = None
                                    for j in range(2):
                                        c = 2 * kpc + j
                                        w0 = []
                                        if j == 0:
                                            w0 = [texp[kpc][sel]]
                                            if c == 0:
                                                w0 = w0 + sl_pC[sel]["users"]
                                        tc = p.do(
                                            "tensor",
                                            lambda g, c=c, sel=sel, h=h, kpc=kpc, j=j: g.matmul(
                                                pC[sel][0:65, :],
                                                vxb[:, c * 1040 + h * 65: c * 1040 + h * 65 + 65],
                                                probs[:, (kpc % 2) * 2048 + sel * 1024 + j * 512:
                                                      (kpc % 2) * 2048 + sel * 1024 + j * 512 + 512],
                                                start=(c == 0), stop=(c == 31)),
                                            waits=w0,
                                            sem="pe" if j == 1 else None)
                                        if j == 0 and c == 0:
                                            sl_pC[sel]["users"] = []
                                    sl_pr[kpc % 2]["users"].append(tc)
                                    outt[sel] = tc
                                return outt

                            for kp in range(16):
                                emit_scores(kp)
                                if kp == 1:
                                    flush_norm()
                                if kp >= 1:
                                    emit_ctx(kp - 1)
                            tcs = emit_ctx(15)
                            tve = [None, None]
                            for sel in range(2):
                                tve[sel] = p.do("vector",
                                                lambda g, sel=sel, chh=chh: g.tensor_copy(
                                                    ctxf[sel * 64:sel * 64 + 64,
                                                         chh * 512:(chh + 1) * 512],
                                                    pCb[0:64, sel * 512:sel * 512 + 512]),
                                                waits=[tcs[sel]] +
                                                      (sl_ctxf["users"] if chh == 0 and sel == 0 else []),
                                                sem="dve")
                                if chh == 0 and sel == 0:
                                    sl_ctxf["users"] = []
                                sl_pC[sel]["users"].append(tve[sel])
                            tl0 = p.do("scalar", lambda g: g.activation(
                                rrow[0:1, :], pCb[64:65, :], AF.Ln,
                                bias=0.0, scale=1.0),
                                waits=[tcs[0], tcs[1]] + sl_rr["users"], sem="act")
                            sl_rr["users"] = []
                            te0 = p.do("scalar", lambda g: g.activation(
                                rrowb[0:1, :], rrow[0:1, :], AF.Exp,
                                bias=0.0, scale=-1.0),
                                waits=[tl0] + sl_rrb["users"], sem="act")
                            sl_rrb["users"] = []
                            te1 = te0
                            sl_rr["users"] += [te0]

                            def norm(chh=chh, tcs=tcs, te0=te0, te1=te1, tve=tve):
                                tb0 = p.do("tensor", lambda g: g.matmul(
                                    pO[0][:, :], onesr[0:1, :], rrowb[0:1, 0:512],
                                    start=True, stop=True),
                                    waits=[te0] + sl_pS0[1]["users"], sem="pe")
                                sl_pS0[1]["users"] = []
                                tb1 = p.do("tensor", lambda g: g.matmul(
                                    pO[1][:, :], onesr[0:1, :], rrowb[0:1, 512:1024],
                                    start=True, stop=True),
                                    waits=[te1], sem="pe")
                                sl_rrb["users"] += [tb0, tb1]
                                tr4a = p.do("vector", lambda g: g.tensor_copy(
                                    rbt[:, 0:512], pO[0][:, :]),
                                    waits=[tb0] + sl_rbt["users"], sem="dve")
                                sl_rbt["users"] = []
                                tr4 = p.do("vector", lambda g: g.tensor_copy(
                                    rbt[:, 512:1024], pO[1][:, :]),
                                    waits=[tb1], sem="dve")
                                sl_pS0[1]["users"] += [tr4a, tr4]
                                for sel in range(2):
                                    tev = p.do("vector",
                                               lambda g, sel=sel, chh=chh: g.tensor_tensor(
                                                   ctxf[sel * 64:sel * 64 + 64,
                                                        chh * 512:(chh + 1) * 512],
                                                   ctxf[sel * 64:sel * 64 + 64,
                                                        chh * 512:(chh + 1) * 512],
                                                   rbt[sel * 64:sel * 64 + 64,
                                                       sel * 512:sel * 512 + 512],
                                                   OP.mult),
                                               waits=[tr4a if sel == 0 else tr4, tve[sel]],
                                               sem="dve")
                                    sl_rbt["users"].append(tev)
                                    ctx_done.append(tev)
                            pend["fn"] = norm
                        flush_norm()
                        for sub in range(4):
                            tso = m * 512 + sub * 128
                            iy = sub % 2
                            tpo = [None, None]
                            for hf in range(2):
                                for dc in range(NCH):
                                    p.do("tensor",
                                         lambda g, dc=dc, sub=sub, hf=hf: g.matmul(
                                             pO[hf][:, :],
                                             ctxf[:, dc * 512 + sub * 128: dc * 512 + sub * 128 + 128],
                                             wO[:, dc * D + hf * 512: dc * D + hf * 512 + 512],
                                             start=(dc == 0), stop=False),
                                         waits=([twall] + ctx_done + sl_pS0[1]["users"]) if (dc == 0 and hf == 0) else [],
                                         sem=None)
                                if hf == 0:
                                    sl_pS0[1]["users"] = []
                                tpo[hf] = p.do("tensor",
                                               lambda g, hf=hf: g.matmul(
                                                   pO[hf][:, :], onesr[32:33, :],
                                                   brow[32:33, D + hf * 512: D + hf * 512 + 512],
                                                   start=False, stop=True), sem="pe")
                            tlx = dma("sync", f"dT{iy}", xrs[iy][:], RES["ap"][tso:tso + 128, :],
                                      waits=sl_xr[iy]["users"])
                            sl_xr[iy]["users"] = []
                            ta0 = p.do("vector",
                                       lambda g, iy=iy: g.tensor_tensor(
                                           xrs[iy][:, 0:512], pO[0][:, :],
                                           xrs[iy][:, 0:512], OP.add),
                                       waits=[tpo[0], tlx], sem="dve")
                            ta1 = p.do("vector",
                                       lambda g, iy=iy: g.tensor_tensor(
                                           xrs[iy][:, 512:1024], pO[1][:, :],
                                           xrs[iy][:, 512:1024], OP.add),
                                       waits=[tpo[1]], sem="dve")
                            sl_pS0[1]["users"] += [ta0, ta1]
                            sl_ctxf["users"].append(tpo[1])
                            tyw = dma("gpsimd", "dY", yout[tso:tso + 128, :], xrs[iy][:],
                                      waits=[ta0, ta1])
                            sl_xr[iy]["users"] += [tyw]
                    p.barrier()
                    p.flush(block, semh)
                RES["ap"] = yout

        if "F2" in phases:
            ffn_phase("v")

        p.wait("sync", [p.tok("dY")])
        p.flush(block, semh)

    return nc


def kernel(**inputs):
    x_full = np.ascontiguousarray(inputs["L0_attn_a_x"])
    consts = _host_consts()
    wmap = {
        "aWqkv": inputs["L0_attn_a_W_qkv"], "aBqkv": inputs["L0_attn_a_b_qkv"],
        "aWo": inputs["L0_attn_a_W_out"], "aBo": inputs["L0_attn_a_b_out"],
        "bWq": inputs["L0_attn_b_Wq"], "bBq": inputs["L0_attn_b_bq"],
        "bWk": inputs["L0_attn_b_Wk"], "bBk": inputs["L0_attn_b_bk"],
        "bWv": inputs["L0_attn_b_Wv"], "bBv": inputs["L0_attn_b_bv"],
        "bWo": inputs["L0_attn_b_Wo"], "bBo": inputs["L0_attn_b_bo"],
        "rw": inputs["L0_ffn_router_w"], "w1": inputs["L0_ffn_w1"],
        "w2": inputs["L0_ffn_w2"], "sgu": inputs["L0_ffn_shared_gate_up"],
        "sdn": inputs["L0_ffn_shared_down"], "sgw": inputs["L0_ffn_shared_gate_w"],
    }
    wmap = {k: np.ascontiguousarray(np.asarray(v, dtype=np.float32))
            for k, v in wmap.items()}
    nc = build_nc(_PHASES)
    n = 8
    in_maps = []
    for i in range(n):
        mp = {"x": np.ascontiguousarray(x_full[i])}
        mp.update(wmap)
        mp.update(consts)
        in_maps.append(mp)
    res = run_bass_kernel_spmd(nc, in_maps, core_ids=list(range(n)))
    return np.stack([r["y"] for r in res.results], axis=0)


# version 64
# speedup vs baseline: 1.1624x; 1.0139x over previous
import numpy as np
import ml_dtypes
from contextlib import ExitStack
import concourse.bass as bass
import concourse.mybir as mybir
from concourse.bass_utils import run_bass_kernel_spmd
from concourse.alu_op_type import AluOpType as OP

F32 = mybir.dt.float32
BF16 = mybir.dt.bfloat16
AF = mybir.ActivationFunctionType
AX = mybir.AxisListType

S, D = 4096, 1024
NTOK = S // 128
NCH = D // 128
EPS = 1e-5
NEG_SHIFT = -10.0
SC_A = 1.0 / 32.0
SC_B = 0.125
CONST_VALS = [0.0, 1.0, EPS, -0.5, NEG_SHIFT, SC_A, SC_B, -1.0]

ENGINES = ("sync", "tensor", "vector", "scalar", "gpsimd")
_PHASES = ("A", "F1", "B", "F2")


class Prog:
    def __init__(self):
        self.ops = {e: [] for e in ENGINES}
        self.cnt = {}
        self.lastw = {e: {} for e in ENGINES}
        self.sems = set()

    def wait(self, eng, toks):
        best = {}
        for t in toks:
            if t is None:
                continue
            s, v = t
            if v > best.get(s, 0):
                best[s] = v
        for s, v in sorted(best.items()):
            if v <= 0 or self.lastw[eng].get(s, 0) >= v:
                continue
            self.lastw[eng][s] = v
            self.ops[eng].append(("w", s, v))

    def do(self, eng, fn, waits=(), sem=None, amt=1):
        self.wait(eng, waits)
        tok = None
        if sem is not None:
            self.sems.add(sem)
            self.cnt[sem] = self.cnt.get(sem, 0) + amt
            tok = (sem, self.cnt[sem])
        self.ops[eng].append(("e", fn, sem, amt))
        return tok

    def tok(self, sem):
        self.sems.add(sem)
        return (sem, self.cnt.get(sem, 0))

    def barrier(self):
        toks = [self.tok(s) for s in sorted(self.sems)]
        for e in ENGINES:
            self.wait(e, toks)

    def flush(self, block, semh):
        for e in ENGINES:
            ops = self.ops[e]
            if not ops:
                continue

            def body(engobj, ops=ops):
                for op in ops:
                    if op[0] == "w":
                        engobj.wait_ge(semh[op[1]], op[2])
                    else:
                        ins = op[1](engobj)
                        if op[2] is not None:
                            ins.then_inc(semh[op[2]], op[3])
            getattr(block, e)(body)
            self.ops[e] = []


def slotinfo():
    return {"users": []}


def _host_consts():
    cst = np.zeros((128, 16), dtype=np.float32)
    for i, v in enumerate(CONST_VALS):
        cst[:, i] = v
    idb = np.eye(128, dtype=ml_dtypes.bfloat16)
    idf = np.eye(128, dtype=np.float32)
    j = np.arange(32, dtype=np.float32)
    inv = (1.0 / (10000.0 ** (j / 32.0))).astype(np.float32)
    t = np.arange(S, dtype=np.float32)
    ang = t[None, :] * inv[:, None]
    c = np.cos(ang).astype(np.float32)
    s = np.sin(ang).astype(np.float32)
    ropeC = np.tile(c, (4, 1)).astype(ml_dtypes.bfloat16)
    sgn = np.repeat(np.array([-1.0, 1.0, -1.0, 1.0], np.float32), 32)
    ropeS = (np.tile(s, (4, 1)) * sgn[:, None]).astype(ml_dtypes.bfloat16)
    p32 = np.zeros((128, 128), dtype=ml_dtypes.bfloat16)
    p32[np.arange(128) ^ 32, np.arange(128)] = 1.0
    return {"cst": cst, "idb": idb, "idf": idf, "ropeC": ropeC,
            "ropeS": ropeS, "p32": p32}


def build_nc(phases=None, dbg=False):
    if phases is None:
        phases = _PHASES
    nc = bass.Bass("TRN2", target_bir_lowering=False)
    xin = nc.dram_tensor("x", [S, D], F32, kind="ExternalInput").ap()
    yout = nc.dram_tensor("y", [S, D], F32, kind="ExternalOutput").ap()
    W = {}
    for nm, shp in [
        ("aWqkv", [D, 3 * D]), ("aBqkv", [3 * D]), ("aWo", [D, D]), ("aBo", [D]),
        ("bWq", [D, D]), ("bBq", [D]), ("bWk", [D, D]), ("bBk", [D]),
        ("bWv", [D, D]), ("bBv", [D]), ("bWo", [D, D]), ("bBo", [D]),
        ("rw", [D, 8]), ("w1", [8, D, D]), ("w2", [8, D, D]),
        ("sgu", [D, 2 * D]), ("sdn", [D, D]), ("sgw", [D, 1]),
    ]:
        W[nm] = nc.dram_tensor(nm, shp, F32, kind="ExternalInput").ap()
    C = {}
    for nm, shp, dt in [("cst", [128, 16], F32), ("idb", [128, 128], BF16),
                        ("idf", [128, 128], F32), ("ropeC", [128, S], BF16),
                        ("ropeS", [128, S], BF16), ("p32", [128, 128], BF16)]:
        C[nm] = nc.dram_tensor(nm, shp, dt, kind="ExternalInput").ap()
    scrA = nc.dram_tensor("scrA", [NCH, 128, S], BF16, kind="Internal").ap()
    RES = {"ap": xin}
    DBG = {}
    if dbg:
        for nm, shp, dt in [("dka", [128, NCH * S], BF16), ("dva", [128, NTOK * D], BF16),
                            ("dxl", [NCH, 128, S], BF16), ("dqf", [128, 2048], BF16),
                            ("dpr", [128, 32 * 256], BF16), ("dct", [128, D], BF16),
                            ("dkb", [128, NCH * S], BF16), ("dqb", [NCH, 128, S], BF16),
                            ("dvx", [128, NTOK * 16 * 65], BF16)]:
            DBG[nm] = nc.dram_tensor(nm, shp, dt, kind="ExternalOutput").ap()
    scrB = nc.dram_tensor("scrB", [NCH, 128, S], BF16, kind="Internal").ap()
    scrV = nc.dram_tensor("scrV", [128, NTOK * 1040], BF16, kind="Internal").ap()

    p = Prog()

    with ExitStack() as TOP:
        E = TOP.enter_context
        semh = {}
        for s_ in ("pe", "dve", "act", "pool", "dS", "dW", "dY", "dA", "dB",
                   "dX0", "dX1", "dQ0", "dQ1", "dT0", "dT1", "dR0", "dR1",
                   "dE0", "dE1"):
            semh[s_] = E(nc.semaphore(s_))
        block = E(nc.Block())

        cst = E(nc.sbuf_tensor("cst_s", [128, 16], F32))
        for i, v in enumerate(CONST_VALS):
            nc.const_aps.register(F32, float(v), cst[:, i:i + 1])
        idb = E(nc.sbuf_tensor("idb_s", [128, 128], BF16))
        idf = E(nc.sbuf_tensor("idf_s", [128, 128], F32))
        onesr = E(nc.sbuf_tensor("onesr", [128, 128], BF16))
        onesc = E(nc.sbuf_tensor("onesc", [128, 1], BF16))
        bias_pp = E(nc.sbuf_tensor("bias_pp", [128, 32], F32))
        brow = E(nc.sbuf_tensor("brow", [128, 2 * D], BF16))
        p32 = E(nc.sbuf_tensor("p32_s", [128, 128], BF16))
        mvt2 = [E(nc.sbuf_tensor(f"mvt{i}", [128, 4], F32)) for i in range(2)]
        rwb_t = E(nc.sbuf_tensor("rwb_t", [128, NCH * 8], BF16))
        sgwb_t = E(nc.sbuf_tensor("sgwb_t", [128, 8], BF16))
        st122 = [E(nc.sbuf_tensor(f"st12{i}", [128, 12], F32)) for i in range(2)]
        xlnb2 = [E(nc.sbuf_tensor(f"xlnb{i}", [128, 1024], BF16)) for i in range(2)]

        def dma(eng, dsem, out, in_, waits=()):
            return p.do(eng, lambda g, o=out, i=in_: g.dma_start(out=o, in_=i),
                        waits=waits, sem=dsem, amt=16)

        def wload(dst3, src3, waits=(), dsem="dW"):
            return dma("gpsimd", dsem, dst3, src3, waits=waits)

        def wsl(w_ap, cols=None):
            a = w_ap if cols is None else w_ap[:, cols[0]:cols[1]]
            return a.rearrange("(c p) m -> p c m", p=128)

        LN = [{"mv": None, "st": None, "xln": None, "ap": None} for _ in range(2)]
        LNCNT = [0]

        def ln_tile(src_dram, tg, xt, sl_xt, dsem, ptr, sl_ptr):
            i = LNCNT[0] % 2
            LNCNT[0] += 1
            st12 = st122[i]
            mvt = mvt2[i]
            xlnb = xlnb2[i]
            L = LN[i]
            tload = dma("sync", dsem, xt[:], src_dram[tg * 128:(tg + 1) * 128, :],
                        waits=sl_xt["users"])
            sl_xt["users"] = []
            t1 = p.do("vector", lambda g, st12=st12: g.bn_stats(st12[:, 0:6], xt[:, 0:512]),
                      waits=[tload, L["st"]], sem="dve")
            t2 = p.do("vector", lambda g, st12=st12: g.bn_stats(st12[:, 6:12], xt[:, 512:1024]),
                      waits=[t1], sem="dve")
            t3 = p.do("vector", lambda g, st12=st12, mvt=mvt: g.bn_aggr(mvt[:, 0:2], st12[:, 0:12]),
                      waits=[t2, L["mv"]], sem="dve")
            t4 = p.do("scalar", lambda g, mvt=mvt: g.activation(mvt[:, 2:3], mvt[:, 1:2], AF.Ln,
                                                                bias=EPS, scale=1.0),
                      waits=[t3, L["ap"]], sem="act")
            t5 = p.do("scalar", lambda g, mvt=mvt: g.activation(mvt[:, 2:3], mvt[:, 2:3], AF.Exp,
                                                                bias=0.0, scale=-0.5),
                      waits=[t4], sem="act")
            t6 = p.do("vector", lambda g, mvt=mvt: g.tensor_scalar(mvt[:, 3:4], mvt[:, 0:1],
                                                                   mvt[:, 2:3], -1.0, OP.mult, OP.mult),
                      waits=[t5], sem="dve")
            t7 = p.do("scalar", lambda g, mvt=mvt, xlnb=xlnb: g.activation(
                xlnb[:], xt[:], AF.Identity, bias=mvt[:, 3:4], scale=mvt[:, 2:3]),
                waits=[t6, L["xln"]], sem="act")
            sl_xt["users"].append(t7)
            L["mv"], L["st"], L["ap"] = t6, t3, t7
            ttr = None
            for c in range(NCH):
                ttr = p.do("tensor",
                           lambda g, c=c, xlnb=xlnb: g.transpose(ptr[:, c * 128:(c + 1) * 128],
                                                                 xlnb[:, c * 128:(c + 1) * 128], idb[:]),
                           waits=([t7] + sl_ptr["users"]) if c == 0 else [],
                           sem="pe" if c == NCH - 1 else None)
            sl_ptr["users"] = []
            L["xln"] = ttr
            return tload, ttr

        dma("sync", "dS", cst[:], C["cst"])
        dma("sync", "dS", idb[:], C["idb"])
        dma("sync", "dS", idf[:], C["idf"])
        dma("sync", "dS", p32[:], C["p32"])
        tS = p.tok("dS")
        p.do("gpsimd", lambda g: g.memset(onesr[:, :], 1.0), sem="pool")
        p.do("gpsimd", lambda g: g.memset(onesc[:, :], 1.0), sem="pool")
        dma("gpsimd", "dW", brow[0:1, 0:D],
            W["aBqkv"][2 * D:3 * D].rearrange("(o d) -> o d", o=1))
        dma("gpsimd", "dW", brow[32:33, 0:D], W["aBo"].rearrange("(o d) -> o d", o=1))
        dma("gpsimd", "dW", brow[0:1, D:2 * D], W["bBv"].rearrange("(o d) -> o d", o=1))
        dma("gpsimd", "dW", brow[32:33, D:2 * D], W["bBo"].rearrange("(o d) -> o d", o=1))
        with nc.sbuf_tensor("btmp", [8, 128], F32) as btmp, \
             nc.sbuf_tensor("btmpb", [8, 128], BF16) as btmpb, \
             nc.psum_tensor("pbias", [128, 512], F32) as pbias, \
             nc.psum_tensor("pbias2", [128, 1024], BF16) as pbias2:
            dma("gpsimd", "dW", rwb_t[:].rearrange("p (c m) -> p c m", m=8), wsl(W["rw"]))
            tprev = None
            for k, src in enumerate([W["aBqkv"][0:D], W["aBqkv"][D:2 * D],
                                     W["bBq"], W["bBk"]]):
                tl = dma("gpsimd", "dW", btmp[:], src.rearrange("(c p) -> c p", p=128),
                         waits=[tprev] if tprev else [])
                tt = p.do("tensor", lambda g: g.transpose(pbias[:, 0:8], btmp[:], idf[0:8, 0:8]),
                          waits=[tl, tS, tprev, p.tok("dW")], sem="pe")
                tprev = p.do("vector",
                             lambda g, k=k: g.tensor_copy(bias_pp[:, k * 8:(k + 1) * 8],
                                                          pbias[:, 0:8]),
                             waits=[tt], sem="dve")
            tlg = dma("gpsimd", "dW", btmpb[:],
                      W["sgw"].rearrange("(c p) m -> c (p m)", p=128))
            ttg = p.do("tensor", lambda g: g.transpose(pbias2[:, 0:8], btmpb[:], idb[0:8, 0:8]),
                       waits=[tlg, tprev, p.tok("dW")], sem="pe")
            p.do("vector", lambda g: g.tensor_copy(sgwb_t[:, 0:8], pbias2[:, 0:8]),
                 waits=[ttg], sem="dve")
            seam0 = [p.tok(x) for x in ("dS", "dW", "pe", "dve", "pool")]
            p.flush(block, semh)

        if "A" in phases:
            with ExitStack() as PH:
                EE = PH.enter_context
                ka = EE(nc.sbuf_tensor("ka", [128, NCH * S], BF16))
                va = EE(nc.sbuf_tensor("va", [128, NTOK * D], BF16))
                wA = EE(nc.sbuf_tensor("wAa", [128, NCH * D], BF16))
                wB = EE(nc.sbuf_tensor("wBa", [128, NCH * D], BF16))
                PA1 = PH.enter_context(ExitStack())
                E1 = PA1.enter_context
                xts = [E1(nc.sbuf_tensor(f"xa{i}", [128, D], F32)) for i in range(2)]
                xfm = [E1(nc.sbuf_tensor(f"xfa{i}", [128, NCH * 256], BF16))
                       for i in range(2)]
                ptr = [E1(nc.psum_tensor(f"ptA{i}", [128, 1024], BF16)) for i in range(2)]
                pk = [E1(nc.psum_tensor(f"pkA{i}", [128, 512], F32)) for i in range(2)]
                pv = [E1(nc.psum_tensor(f"pvA{i}", [128, 512], F32)) for i in range(4)]
                sl_xt = [slotinfo() for _ in range(2)]
                sl_xt[0]["users"] += seam0
                sl_xt[1]["users"] += seam0
                sl_fm = [slotinfo() for _ in range(2)]
                sl_ptr = [slotinfo() for _ in range(2)]
                sl_pk = [slotinfo() for _ in range(2)]
                sl_pv = [slotinfo() for _ in range(4)]

                wload(wA[:].rearrange("p (c m) -> p c m", m=D), wsl(W["aWqkv"], (D, 2 * D)))
                wload(wB[:].rearrange("p (c m) -> p c m", m=D), wsl(W["aWqkv"], (2 * D, 3 * D)))
                twall = p.tok("dW")

                for t in range(NTOK):
                    ix = t % 2
                    ipr = (t // 2) % 2
                    th = t % 2
                    tload, ttr = ln_tile(xin, t, xts[ix], sl_xt[ix], f"dX{ix}",
                                         ptr[ix], sl_ptr[ix])
                    tev = p.do("vector",
                               lambda g, ix=ix, ipr=ipr, th=th: g.tensor_copy(
                                   xfm[ipr][:].rearrange("p (c t) -> p c t", t=256)[:, :, th * 128:(th + 1) * 128],
                                   ptr[ix][:].rearrange("p (c t) -> p c t", t=128)),
                               waits=[ttr] + (sl_fm[ipr]["users"] if th == 0 else []),
                               sem="dve")
                    if th == 0:
                        sl_fm[ipr]["users"] = []
                        tev_first = tev
                    sl_ptr[ix]["users"].append(tev)
                    if th == 1:
                        tw = dma("gpsimd", "dA",
                                 scrA[:, :, (t - 1) * 128:(t + 1) * 128].rearrange("c p t -> p c t"),
                                 xfm[ipr][:].rearrange("p (c t) -> p c t", t=256),
                                 waits=[tev_first, tev])
                        sl_fm[ipr]["users"].append(tw)
                    for hf in range(2):
                        iv = 2 * (t % 2) + hf
                        for dc in range(NCH):
                            p.do("tensor",
                                 lambda g, dc=dc, hf=hf, iv=iv, ipr=ipr, th=th: g.matmul(
                                     pv[iv][:, :],
                                     xfm[ipr][:, dc * 256 + th * 128: dc * 256 + th * 128 + 128],
                                     wB[:, dc * D + hf * 512: dc * D + hf * 512 + 512],
                                     start=(dc == 0), stop=False),
                                 waits=([twall, tev] + sl_pv[iv]["users"]) if dc == 0 else [],
                                 sem=None)
                        tmm = p.do("tensor",
                                   lambda g, hf=hf, iv=iv: g.matmul(
                                       pv[iv][:, :], onesr[0:1, :],
                                       brow[0:1, hf * 512: hf * 512 + 512],
                                       start=False, stop=True), sem="pe")
                        sl_pv[iv]["users"] = []
                        tevv = p.do("vector",
                                    lambda g, t=t, hf=hf, iv=iv: g.tensor_copy(
                                        va[:, t * D + hf * 512: t * D + hf * 512 + 512],
                                        pv[iv][:, :]),
                                    waits=[tmm], sem="dve")
                        sl_pv[iv]["users"].append(tevv)
                        sl_fm[ipr]["users"].append(tevv)
                    if th == 1:
                        for dk in range(NCH):
                            ip = dk % 2
                            tmm = None
                            for dc in range(NCH):
                                tmm = p.do("tensor",
                                           lambda g, dc=dc, dk=dk, ip=ip, ipr=ipr: g.matmul(
                                               pk[ip][:, 0:256],
                                               wA[:, dc * D + dk * 128: dc * D + dk * 128 + 128],
                                               xfm[ipr][:, dc * 256:(dc + 1) * 256],
                                               start=(dc == 0), stop=(dc == NCH - 1)),
                                           waits=([twall, tev_first, tev] + sl_pk[ip]["users"]) if dc == 0 else [],
                                           sem="pe" if dc == NCH - 1 else None)
                            sl_pk[ip]["users"] = []
                            tevk = p.do("vector",
                                        lambda g, dk=dk, ip=ip, t=t: g.tensor_scalar(
                                            ka[:, dk * S + (t - 1) * 128: dk * S + (t - 1) * 128 + 256],
                                            pk[ip][:, 0:256], bias_pp[:, 8 + dk:9 + dk],
                                            None, OP.add),
                                        waits=[tmm], sem="dve")
                            sl_pk[ip]["users"].append(tevk)
                            sl_fm[ipr]["users"].append(tmm)
                if dbg:
                    p.barrier()
                    dma("sync", "dS", DBG["dka"], ka[:])
                    dma("sync", "dS", DBG["dva"], va[:])
                    dma("sync", "dS", DBG["dxl"].rearrange("c p t -> p c t"),
                        scrA.rearrange("c p t -> p c t"))
                    p.barrier()
                p.flush(block, semh)
                PA1.close()

                with ExitStack() as P2:
                    E2 = P2.enter_context
                    wload(wA[:].rearrange("p (c m) -> p c m", m=D), wsl(W["aWqkv"], (0, D)),
                          waits=[p.tok("pe")])
                    wload(wB[:].rearrange("p (c m) -> p c m", m=D), wsl(W["aWo"]))
                    twall = p.tok("dW")
                    seam_a = [p.tok(x) for x in ("pe", "act", "dve", "pool", "dA")]
                    QM = 256
                    xq2 = [E2(nc.sbuf_tensor(f"xqA{i}", [128, NCH * QM], BF16))
                           for i in range(2)]
                    qfm = E2(nc.sbuf_tensor("qfA", [128, NCH * QM], BF16))
                    probs = E2(nc.sbuf_tensor("prA", [128, NTOK * QM], BF16))
                    ctxt = E2(nc.sbuf_tensor("ctA", [128, D], BF16))
                    rcp = E2(nc.sbuf_tensor("rcA", [128, 1], F32))
                    xrs = [E2(nc.sbuf_tensor(f"xrA{i}", [128, D], F32)) for i in range(2)]
                    pS = [E2(nc.psum_tensor(f"pSA{i}", [128, 512], F32)) for i in range(2)]
                    pc0 = E2(nc.psum_tensor("pc0A", [128, 512], F32))
                    pc1 = E2(nc.psum_tensor("pc1A", [128, 512], F32))
                    pcS = E2(nc.psum_tensor("pcSA", [128, 512], F32))
                    ptrx = E2(nc.psum_tensor("ptxA", [128, 1024], BF16))
                    pO = [E2(nc.psum_tensor(f"pOA{i}", [128, 512], F32)) for i in range(2)]
                    sl_xq2 = [slotinfo() for _ in range(2)]
                    sl_qf = slotinfo()
                    sl_pS = [slotinfo() for _ in range(2)]
                    sl_probs = slotinfo()
                    sl_ctx = slotinfo()
                    sl_ctxt = slotinfo()
                    sl_ctxf = slotinfo()
                    sl_ptrx = slotinfo()
                    sl_pO = slotinfo()
                    sl_rcp = slotinfo()
                    sl_xr = [slotinfo() for _ in range(2)]

                    for m in range(S // QM):
                        iq = m % 2
                        xq = xq2[iq]
                        sl_xq = sl_xq2[iq]
                        tldq = dma("sync", f"dQ{iq}",
                                   xq[:].rearrange("p (c t) -> p c t", t=QM),
                                   scrA[:, :, m * QM:(m + 1) * QM].rearrange("c p t -> p c t"),
                                   waits=sl_xq["users"] + [p.tok("dA")] +
                                         (seam_a if m == 0 else []))
                        sl_xq["users"] = []
                        tevq = None
                        for dq in range(NCH):
                            ip = dq % 2
                            tmm = None
                            for dc in range(NCH):
                                tmm = p.do("tensor",
                                           lambda g, dc=dc, dq=dq, ip=ip, xq=xq: g.matmul(
                                               pS[ip][:, 0:QM],
                                               wA[:, dc * D + dq * 128: dc * D + dq * 128 + 128],
                                               xq[:, dc * QM:(dc + 1) * QM],
                                               start=(dc == 0), stop=(dc == NCH - 1)),
                                           waits=([twall, tldq] + sl_pS[ip]["users"]) if dc == 0 else [],
                                           sem="pe" if dc == NCH - 1 else None)
                            sl_pS[ip]["users"] = []
                            tevq = p.do("vector",
                                        lambda g, dq=dq, ip=ip: g.tensor_scalar(
                                            qfm[:, dq * QM:(dq + 1) * QM],
                                            pS[ip][:, 0:QM], bias_pp[:, dq:dq + 1],
                                            None, OP.add),
                                        waits=[tmm] + (sl_qf["users"] if dq == 0 else []),
                                        sem="dve")
                            sl_pS[ip]["users"].append(tevq)
                            sl_xq["users"].append(tevq)
                            if dq == 0:
                                sl_qf["users"] = []
                        tq = tevq
                        texp_last = None
                        for kt in range(NTOK):
                            ip = kt % 2
                            tmm = None
                            for dc in range(NCH):
                                tmm = p.do("tensor",
                                           lambda g, dc=dc, kt=kt, ip=ip: g.matmul(
                                               pS[ip][:, 0:QM],
                                               ka[:, dc * S + kt * 128: dc * S + kt * 128 + 128],
                                               qfm[:, dc * QM:(dc + 1) * QM],
                                               start=(dc == 0), stop=(dc == NCH - 1)),
                                           waits=([tq] + sl_pS[ip]["users"] +
                                                  (sl_probs["users"] if kt == 0 else [])) if dc == 0 else [],
                                           sem="pe" if dc == NCH - 1 else None)
                            sl_pS[ip]["users"] = []
                            if kt == 0:
                                sl_probs["users"] = []
                            texp = p.do("scalar",
                                        lambda g, kt=kt, ip=ip: g.activation(
                                            probs[:, kt * QM:(kt + 1) * QM], pS[ip][:, 0:QM],
                                            AF.Exp, bias=NEG_SHIFT, scale=SC_A),
                                        waits=[tmm], sem="act")
                            sl_pS[ip]["users"].append(texp)
                            sl_qf["users"].append(tmm)
                            texp_last = texp
                        for qs in range(2):
                            tmmS = None
                            for kt in range(NTOK):
                                w0 = ([texp_last] + sl_ctx["users"]) if kt == 0 else []
                                p.do("tensor",
                                     lambda g, kt=kt, qs=qs: g.matmul(
                                         pc0[:, :],
                                         probs[:, kt * QM + qs * 128: kt * QM + qs * 128 + 128],
                                         va[:, kt * D: kt * D + 512],
                                         start=(kt == 0), stop=(kt == NTOK - 1)),
                                     waits=w0, sem=None)
                                p.do("tensor",
                                     lambda g, kt=kt, qs=qs: g.matmul(
                                         pc1[:, :],
                                         probs[:, kt * QM + qs * 128: kt * QM + qs * 128 + 128],
                                         va[:, kt * D + 512: kt * D + 1024],
                                         start=(kt == 0), stop=(kt == NTOK - 1)), sem=None)
                                tmmS = p.do("tensor",
                                            lambda g, kt=kt, qs=qs: g.matmul(
                                                pcS[:, 0:1],
                                                probs[:, kt * QM + qs * 128: kt * QM + qs * 128 + 128],
                                                onesc[:, :],
                                                start=(kt == 0), stop=(kt == NTOK - 1)),
                                            sem="pe" if kt == NTOK - 1 else None)
                            sl_ctx["users"] = []
                            if qs == 1:
                                sl_probs["users"].append(tmmS)
                            trc = p.do("vector", lambda g: g.reciprocal(rcp[:, :], pcS[:, 0:1]),
                                       waits=[tmmS] + sl_rcp["users"], sem="dve")
                            sl_rcp["users"] = []
                            tc0 = p.do("vector",
                                       lambda g: g.tensor_scalar(
                                           ctxt[:, 0:512], pc0[:, :],
                                           rcp[:, 0:1], None, OP.mult),
                                       waits=[trc] + sl_ctxt["users"], sem="dve")
                            sl_ctxt["users"] = []
                            tc1 = p.do("vector",
                                       lambda g: g.tensor_scalar(
                                           ctxt[:, 512:1024], pc1[:, :],
                                           rcp[:, 0:1], None, OP.mult),
                                       waits=[trc], sem="dve")
                            sl_rcp["users"] += [tc0, tc1]
                            sl_ctx["users"] += [tc0, tc1]
                            if dbg and m == 0 and qs == 0:
                                dma("sync", "dS", DBG["dqf"], qfm[:], waits=[tq])
                                dma("sync", "dS", DBG["dpr"], probs[:], waits=[texp_last])
                                dma("sync", "dS", DBG["dct"], ctxt[:], waits=[tc0, tc1])
                            ttr = None
                            for c in range(NCH):
                                ttr = p.do("tensor",
                                           lambda g, c=c: g.transpose(
                                               ptrx[:, c * 128:(c + 1) * 128],
                                               ctxt[:, c * 128:(c + 1) * 128],
                                               idb[:]),
                                           waits=([tc0, tc1] + sl_ptrx["users"]) if c == 0 else [],
                                           sem="pe" if c == NCH - 1 else None)
                            sl_ptrx["users"] = []
                            tevc = p.do("vector", lambda g: g.tensor_copy(qfm[:, 0:D], ptrx[:]),
                                        waits=[ttr] + sl_ctxf["users"] + sl_qf["users"],
                                        sem="dve")
                            sl_ptrx["users"].append(tevc)
                            sl_ctxt["users"].append(ttr)
                            sl_ctxf["users"] = []
                            tso = m * QM + qs * 128
                            tpo = [None, None]
                            for hf in range(2):
                                for dc in range(NCH):
                                    p.do("tensor",
                                         lambda g, dc=dc, hf=hf: g.matmul(
                                             pO[hf][:, :],
                                             qfm[:, dc * 128:(dc + 1) * 128],
                                             wB[:, dc * D + hf * 512: dc * D + hf * 512 + 512],
                                             start=(dc == 0), stop=False),
                                         waits=([twall, tevc] + sl_pO["users"]) if (dc == 0 and hf == 0) else [],
                                         sem=None)
                                if hf == 0:
                                    sl_pO["users"] = []
                                tpo[hf] = p.do("tensor",
                                               lambda g, hf=hf: g.matmul(
                                                   pO[hf][:, :], onesr[32:33, :],
                                                   brow[32:33, hf * 512: hf * 512 + 512],
                                                   start=False, stop=True), sem="pe")
                            iy = qs
                            tlx = dma("sync", f"dT{iy}", xrs[iy][:], xin[tso:tso + 128, :],
                                      waits=sl_xr[iy]["users"])
                            sl_xr[iy]["users"] = []
                            ta0 = p.do("vector",
                                       lambda g, iy=iy: g.tensor_tensor(
                                           xrs[iy][:, 0:512], pO[0][:, :],
                                           xrs[iy][:, 0:512], OP.add),
                                       waits=[tpo[0], tlx], sem="dve")
                            ta1 = p.do("vector",
                                       lambda g, iy=iy: g.tensor_tensor(
                                           xrs[iy][:, 512:1024], pO[1][:, :],
                                           xrs[iy][:, 512:1024], OP.add),
                                       waits=[tpo[1]], sem="dve")
                            sl_pO["users"] += [ta0, ta1]
                            sl_ctxf["users"].append(tpo[1])
                            sl_qf["users"].append(tpo[1])
                            tyw = dma("gpsimd", "dY", yout[tso:tso + 128, :], xrs[iy][:],
                                      waits=[ta0, ta1])
                            sl_xr[iy]["users"] += [tyw]
                    p.barrier()
                    p.flush(block, semh)
                RES["ap"] = yout

        def ffn_phase(tag):
            with ExitStack() as PH:
                EE = PH.enter_context
                xlnfm = EE(nc.sbuf_tensor(f"xl{tag}", [128, NCH * 2048], BF16))
                acc = EE(nc.sbuf_tensor(f"ac{tag}", [128, 16 * D], F32))
                combine = EE(nc.sbuf_tensor(f"cb{tag}", [128, 16 * 8], F32))
                gb = EE(nc.sbuf_tensor(f"gb{tag}", [128, 2048], BF16))
                growb = EE(nc.sbuf_tensor(f"gw{tag}", [1, 512], BF16))
                ev8 = EE(nc.sbuf_tensor(f"e8{tag}", [128, 8], F32))
                ev2 = EE(nc.sbuf_tensor(f"e2{tag}", [128, 8], F32))
                sc8 = EE(nc.sbuf_tensor(f"s8{tag}", [128, 16], F32))
                h_sb = EE(nc.sbuf_tensor(f"hs{tag}", [128, NCH * 512], BF16))
                midg = EE(nc.sbuf_tensor(f"mg{tag}", [128, NCH * 512], BF16))
                sgt = EE(nc.sbuf_tensor(f"sgt{tag}", [128, 512], BF16))
                mid1 = EE(nc.sbuf_tensor(f"m1{tag}", [128, 512], BF16))
                xts = [EE(nc.sbuf_tensor(f"xt{tag}{i}", [128, D], F32)) for i in range(2)]
                xrw = EE(nc.sbuf_tensor(f"xr{tag}", [128, 1024], F32))
                ptr = [EE(nc.psum_tensor(f"pt{tag}{i}", [128, 1024], BF16)) for i in range(2)]
                pA = [EE(nc.psum_tensor(f"pa{tag}{i}", [128, 512], F32)) for i in range(2)]
                pB = [EE(nc.psum_tensor(f"pb{tag}{i}", [128, 512], F32)) for i in range(2)]
                pY = [EE(nc.psum_tensor(f"py{tag}{i}", [128, 512], F32)) for i in range(2)]
                sl_ptr = [slotinfo() for _ in range(2)]
                sl_pA = [slotinfo() for _ in range(2)]
                sl_pB = [slotinfo() for _ in range(2)]
                sl_pY = [slotinfo() for _ in range(2)]
                sl_xt = [slotinfo() for _ in range(2)]
                sl_hsb = slotinfo()
                sl_midg = slotinfo()
                sl_sgt = slotinfo()
                sl_mid1 = slotinfo()
                sl_gb = slotinfo()
                sl_acc = slotinfo()
                sl_ev = slotinfo()
                sl_cb = slotinfo()
                sl_xw = slotinfo()
                sl_xlq = [slotinfo() for _ in range(4)]
                sl_exw = slotinfo()

                rwb = rwb_t
                trw = p.tok("dW")

                for half in range(2):
                    hb = half * 2048
                    tx_done = []
                    for tl in range(16):
                        ix = tl % 2
                        tg = (hb // 128) + tl
                        tload, ttr = ln_tile(RES["ap"], tg, xts[ix], sl_xt[ix], f"dX{ix}",
                                             ptr[ix], sl_ptr[ix])
                        tev = p.do("scalar",
                                   lambda g, ix=ix, tl=tl: g.copy(
                                       xlnfm[:].rearrange("p (c t) -> p c t", t=2048)[:, :, tl * 128:(tl + 1) * 128],
                                       ptr[ix][:].rearrange("p (c t) -> p c t", t=128)),
                                   waits=[ttr] + sl_xlq[tl // 4]["users"],
                                   sem="act")
                        sl_xlq[tl // 4]["users"] = []
                        sl_ptr[ix]["users"].append(tev)
                        tx_done.append(tev)
                    for tl in range(16):
                        ip = tl % 2
                        tmm = None
                        for dc in range(NCH):
                            tmm = p.do("tensor",
                                       lambda g, dc=dc, tl=tl, ip=ip: g.matmul(
                                           pA[ip][:, 0:8],
                                           xlnfm[:, dc * 2048 + tl * 128: dc * 2048 + tl * 128 + 128],
                                           rwb[:, dc * 8:(dc + 1) * 8],
                                           start=(dc == 0), stop=(dc == NCH - 1)),
                                       waits=([trw, tx_done[tl]] + sl_pA[ip]["users"]) if dc == 0 else [],
                                       sem="pe" if dc == NCH - 1 else None)
                        sl_pA[ip]["users"] = []
                        t1 = p.do("vector", lambda g, ip=ip: g.reduce_max(
                            sc8[:, 0:1], pA[ip][:, 0:8], axis=AX.X, negate=True),
                            waits=[tmm] + sl_ev["users"], sem="dve")
                        sl_ev["users"] = []
                        t2 = p.do("scalar", lambda g, ip=ip: g.activation(
                            ev8[:, :], pA[ip][:, 0:8], AF.Exp, bias=sc8[:, 0:1], scale=1.0),
                            waits=[t1], sem="act")
                        sl_pA[ip]["users"].append(t2)
                        t3 = p.do("vector", lambda g: g.reduce_sum(sc8[:, 1:2], ev8[:, :],
                                                                   axis=AX.X),
                                  waits=[t2], sem="dve")
                        t4 = p.do("vector", lambda g: g.reciprocal(sc8[:, 2:3], sc8[:, 1:2]),
                                  waits=[t3], sem="dve")
                        t5 = p.do("vector", lambda g: g.max(out=sc8[:, 8:16], in_=ev8[:, :]),
                                  waits=[t2], sem="dve")
                        t6 = p.do("vector", lambda g: g.tensor_scalar(
                            ev2[:, :], ev8[:, :], sc8[:, 9:10], None, OP.is_ge),
                            waits=[t5], sem="dve")
                        t7 = p.do("vector", lambda g: g.tensor_mul(ev2[:, :], ev2[:, :], ev8[:, :]),
                                  waits=[t6], sem="dve")
                        t8 = p.do("vector", lambda g, tl=tl: g.tensor_scalar(
                            combine[:, tl * 8:(tl + 1) * 8], ev2[:, :], sc8[:, 2:3],
                            None, OP.mult),
                            waits=[t7, t4] + (sl_cb["users"] if tl == 0 else []), sem="dve")
                        if tl == 0:
                            sl_cb["users"] = []
                        sl_ev["users"] += [t7, t8]
                    tcomb = t8
                    tg4 = None
                    tgb_prev = None
                    for n in range(4):
                        ip = n % 2
                        tmm = None
                        for dc in range(NCH):
                            tmm = p.do("tensor",
                                       lambda g, n=n, dc=dc, ip=ip: g.matmul(
                                           pA[ip][0:1, :],
                                           sgwb_t[:, dc:dc + 1],
                                           xlnfm[:, dc * 2048 + n * 512: dc * 2048 + n * 512 + 512],
                                           start=(dc == 0), stop=(dc == NCH - 1)),
                                       waits=([trw] + tx_done + sl_pA[ip]["users"]) if dc == 0 else [],
                                       sem="pe" if dc == NCH - 1 else None)
                        sl_pA[ip]["users"] = []
                        tg4 = p.do("scalar", lambda g, ip=ip: g.activation(
                            growb[0:1, :], pA[ip][0:1, :], AF.Sigmoid,
                            bias=0.0, scale=1.0),
                            waits=[tmm] + ([tgb_prev] if tgb_prev else []), sem="act")
                        sl_pA[ip]["users"].append(tg4)
                        tb = p.do("tensor", lambda g, ip=ip: g.matmul(
                            pB[ip][:, :], onesr[0:1, :], growb[0:1, :],
                            start=True, stop=True),
                            waits=[tg4] + sl_pB[ip]["users"], sem="pe")
                        sl_pB[ip]["users"] = []
                        tb2 = p.do("vector", lambda g, n=n, ip=ip: g.tensor_copy(
                            gb[:, n * 512:(n + 1) * 512], pB[ip][:, :]),
                            waits=[tb] + (sl_gb["users"] if n == 0 else []), sem="dve")
                        if n == 0:
                            sl_gb["users"] = []
                        sl_pB[ip]["users"].append(tb2)
                        sl_gb["users"].append(tb2)
                        tgb_prev = tb
                    tgb = tb2
                    with ExitStack() as SH:
                        ES = SH.enter_context
                        wsgA = ES(nc.sbuf_tensor(f"wga{tag}{half}", [128, NCH * D], BF16))
                        wsgB = ES(nc.sbuf_tensor(f"wgb{tag}{half}", [128, NCH * D], BF16))
                        wsdn = ES(nc.sbuf_tensor(f"wgd{tag}{half}", [128, NCH * D], BF16))
                        wload(wsgA[:].rearrange("p (c m) -> p c m", m=D), wsl(W["sgu"], (0, D)),
                              waits=sl_exw["users"] if half else [])
                        wload(wsgB[:].rearrange("p (c m) -> p c m", m=D),
                              wsl(W["sgu"], (D, 2 * D)))
                        wload(wsdn[:].rearrange("p (c m) -> p c m", m=D), wsl(W["sdn"]))
                        tsh = p.tok("dW")
                        for mm in range(4):
                            tmid_done = None
                            for fc in range(NCH):
                                ia = fc % 2
                                tga = None
                                for dc in range(NCH):
                                    tga = p.do("tensor",
                                               lambda g, dc=dc, fc=fc, mm=mm, ia=ia: g.matmul(
                                                   pA[ia][:, :],
                                                   wsgA[:, dc * D + fc * 128: dc * D + fc * 128 + 128],
                                                   xlnfm[:, dc * 2048 + mm * 512: dc * 2048 + mm * 512 + 512],
                                                   start=(dc == 0), stop=(dc == NCH - 1)),
                                               waits=([tsh] + tx_done + sl_pA[ia]["users"]) if dc == 0 else [],
                                               sem="pe" if dc == NCH - 1 else None)
                                sl_pA[ia]["users"] = []
                                tgu = None
                                for dc in range(NCH):
                                    tgu = p.do("tensor",
                                               lambda g, dc=dc, fc=fc, mm=mm, ia=ia: g.matmul(
                                                   pB[ia][:, :],
                                                   wsgB[:, dc * D + fc * 128: dc * D + fc * 128 + 128],
                                                   xlnfm[:, dc * 2048 + mm * 512: dc * 2048 + mm * 512 + 512],
                                                   start=(dc == 0), stop=(dc == NCH - 1)),
                                               waits=([tsh] + sl_pB[ia]["users"]) if dc == 0 else [],
                                               sem="pe" if dc == NCH - 1 else None)
                                sl_pB[ia]["users"] = []
                                ts1 = p.do("scalar", lambda g, ia=ia: g.activation(
                                    sgt[:, :], pA[ia][:, :], AF.Silu, bias=0.0, scale=1.0),
                                    waits=[tga] + sl_sgt["users"], sem="act")
                                sl_sgt["users"] = []
                                sl_pA[ia]["users"].append(ts1)
                                ts2 = p.do("vector", lambda g, ia=ia: g.tensor_tensor(
                                    mid1[:, :], sgt[:, :], pB[ia][:, :], OP.mult),
                                    waits=[ts1, tgu] + sl_mid1["users"], sem="dve")
                                sl_mid1["users"] = []
                                sl_sgt["users"].append(ts2)
                                sl_pB[ia]["users"].append(ts2)
                                ts3 = p.do("vector", lambda g, fc=fc, mm=mm: g.tensor_tensor(
                                    midg[:, fc * 512:(fc + 1) * 512], mid1[:, :],
                                    gb[:, mm * 512:(mm + 1) * 512], OP.mult),
                                    waits=[ts2, tgb] + (sl_midg["users"] if fc == 0 else []),
                                    sem="dve")
                                if fc == 0:
                                    sl_midg["users"] = []
                                sl_mid1["users"].append(ts3)
                                sl_gb["users"].append(ts3)
                                tmid_done = ts3
                            for sub in range(4):
                                tlc = mm * 4 + sub
                                for hf in range(2):
                                    iy = hf
                                    tmm = None
                                    for fc in range(NCH):
                                        tmm = p.do("tensor",
                                                   lambda g, fc=fc, sub=sub, hf=hf, iy=iy: g.matmul(
                                                       pY[iy][:, :],
                                                       midg[:, fc * 512 + sub * 128: fc * 512 + sub * 128 + 128],
                                                       wsdn[:, fc * D + hf * 512: fc * D + hf * 512 + 512],
                                                       start=(fc == 0), stop=(fc == NCH - 1)),
                                                   waits=([tsh, tmid_done] + sl_pY[iy]["users"]) if fc == 0 else [],
                                                   sem="pe" if fc == NCH - 1 else None)
                                    sl_pY[iy]["users"] = []
                                    tac = p.do("vector",
                                               lambda g, tlc=tlc, hf=hf, iy=iy: g.tensor_copy(
                                                   acc[:, tlc * D + hf * 512: tlc * D + hf * 512 + 512],
                                                   pY[iy][:, :]),
                                               waits=[tmm] + (sl_acc["users"] if (mm == 0 and sub == 0 and hf == 0) else []),
                                               sem="dve")
                                    if mm == 0 and sub == 0 and hf == 0:
                                        sl_acc["users"] = []
                                    sl_pY[iy]["users"].append(tac)
                                    sl_midg["users"].append(tmm)
                    acc_final = [[] for _ in range(16)]
                    sl_shr = [p.tok("pe")]
                    with ExitStack() as EX:
                        ES = EX.enter_context
                        w1b = [ES(nc.sbuf_tensor(f"w1{tag}{half}{i}", [128, NCH * D], BF16))
                               for i in range(2)]
                        w2b = [ES(nc.sbuf_tensor(f"w2{tag}{half}{i}", [128, NCH * D], BF16))
                               for i in range(2)]
                        sl_we = [slotinfo() for _ in range(2)]
                        for e in range(8):
                            ie = e % 2
                            wl = sl_we[ie]["users"]
                            if e < 2:
                                wl = wl + sl_shr + sl_exw["users"]
                            wload(w1b[ie][:].rearrange("p (c m) -> p c m", m=D),
                                  W["w1"][e].rearrange("(c p) m -> p c m", p=128),
                                  waits=wl, dsem=f"dE{ie}")
                            twe = wload(w2b[ie][:].rearrange("p (c m) -> p c m", m=D),
                                        W["w2"][e].rearrange("(c p) m -> p c m", p=128),
                                        dsem=f"dE{ie}")
                            sl_we[ie]["users"] = []
                            for mm in range(4):
                                th_done = None
                                for fc in range(NCH):
                                    ia = fc % 2
                                    tmm = None
                                    for dc in range(NCH):
                                        tmm = p.do("tensor",
                                                   lambda g, dc=dc, fc=fc, mm=mm, ia=ia, ie=ie: g.matmul(
                                                       pA[ia][:, :],
                                                       w1b[ie][:, dc * D + fc * 128: dc * D + fc * 128 + 128],
                                                       xlnfm[:, dc * 2048 + mm * 512: dc * 2048 + mm * 512 + 512],
                                                       start=(dc == 0), stop=(dc == NCH - 1)),
                                                   waits=([twe] + sl_pA[ia]["users"] +
                                                          (tx_done if (e == 0 and mm == 0 and fc == 0) else [])) if dc == 0 else [],
                                                   sem="pe" if dc == NCH - 1 else None)
                                    sl_pA[ia]["users"] = []
                                    if fc == NCH - 1:
                                        sl_xlq[mm]["users"] = [tmm]
                                    th = p.do("scalar", lambda g, fc=fc, ia=ia: g.activation(
                                        h_sb[:, fc * 512:(fc + 1) * 512], pA[ia][:, :],
                                        AF.Silu, bias=0.0, scale=1.0),
                                        waits=[tmm] + (sl_hsb["users"] if fc == 0 else []),
                                        sem="act")
                                    if fc == 0:
                                        sl_hsb["users"] = []
                                    sl_pA[ia]["users"].append(th)
                                    th_done = th
                                sl_we[ie]["users"].append(th_done)
                                for sub in range(4):
                                    tlc = mm * 4 + sub
                                    for hf in range(2):
                                        iy = hf
                                        tmm = None
                                        for fc in range(NCH):
                                            tmm = p.do("tensor",
                                                       lambda g, fc=fc, sub=sub, hf=hf, iy=iy, ie=ie: g.matmul(
                                                           pY[iy][:, :],
                                                           h_sb[:, fc * 512 + sub * 128: fc * 512 + sub * 128 + 128],
                                                           w2b[ie][:, fc * D + hf * 512: fc * D + hf * 512 + 512],
                                                           start=(fc == 0), stop=(fc == NCH - 1)),
                                                       waits=([twe, th_done] + sl_pY[iy]["users"]) if fc == 0 else [],
                                                       sem="pe" if fc == NCH - 1 else None)
                                        sl_pY[iy]["users"] = []
                                        tac = p.do("vector",
                                                   lambda g, tlc=tlc, hf=hf, iy=iy, e=e: g.scalar_tensor_tensor(
                                                       acc[:, tlc * D + hf * 512: tlc * D + hf * 512 + 512],
                                                       pY[iy][:, :],
                                                       combine[:, tlc * 8 + e: tlc * 8 + e + 1],
                                                       acc[:, tlc * D + hf * 512: tlc * D + hf * 512 + 512],
                                                       OP.mult, OP.add),
                                                   waits=[tmm, tcomb], sem="dve")
                                        sl_pY[iy]["users"].append(tac)
                                        sl_we[ie]["users"].append(tac)
                                        sl_hsb["users"].append(tmm)
                                        sl_acc["users"].append(tac)
                                        sl_cb["users"].append(tac)
                                        if e == 7:
                                            acc_final[tlc].append(tac)
                        sl_exw["users"] = [p.tok("pe")]
                        for tl in range(16):
                            tg = (hb // 128) + tl
                            tlx = dma("gpsimd", "dT0", xrw[:],
                                      RES["ap"][tg * 128:(tg + 1) * 128, :],
                                      waits=sl_xw["users"])
                            sl_xw["users"] = []
                            tadds = []
                            for hf in range(2):
                                tadd = p.do("gpsimd",
                                            lambda g, tl=tl, hf=hf: g.tensor_tensor(
                                                xrw[:, hf * 512:(hf + 1) * 512],
                                                acc[:, tl * D + hf * 512: tl * D + hf * 512 + 512],
                                                xrw[:, hf * 512:(hf + 1) * 512], OP.add),
                                            waits=[tlx, acc_final[tl][hf]], sem="pool")
                                sl_acc["users"].append(tadd)
                                tadds.append(tadd)
                            tyw = dma("gpsimd", "dY",
                                      yout[tg * 128:(tg + 1) * 128, :],
                                      xrw[:], waits=tadds)
                            sl_xw["users"] = tadds + [tyw]
                    if half == 1:
                        p.barrier()
                p.flush(block, semh)

        if "F1" in phases:
            ffn_phase("u")
            RES["ap"] = yout

        if "B" in phases:
            with ExitStack() as PH:
                EE = PH.enter_context
                kb = EE(nc.sbuf_tensor("kb", [128, NCH * S], BF16))

                with ExitStack() as P1:
                    E1 = P1.enter_context
                    wA = E1(nc.sbuf_tensor("wqB", [128, NCH * D], BF16))
                    wB = E1(nc.sbuf_tensor("wkB", [128, NCH * D], BF16))
                    wV = E1(nc.sbuf_tensor("wvB", [128, NCH * D], BF16))
                    wload(wA[:].rearrange("p (c m) -> p c m", m=D), wsl(W["bWq"]))
                    wload(wB[:].rearrange("p (c m) -> p c m", m=D), wsl(W["bWk"]))
                    wload(wV[:].rearrange("p (c m) -> p c m", m=D), wsl(W["bWv"]))
                    twall = p.tok("dW")
                    xts = [E1(nc.sbuf_tensor(f"xB{i}", [128, D], F32)) for i in range(2)]
                    xfm = [E1(nc.sbuf_tensor(f"xfB{i}", [128, NCH * 512], BF16))
                           for i in range(2)]
                    qr = E1(nc.sbuf_tensor("qrB", [128, NCH * 512], BF16))
                    kr = E1(nc.sbuf_tensor("krB", [128, NCH * 512], BF16))
                    rCf = E1(nc.sbuf_tensor("rCf", [128, S], BF16))
                    rSf = E1(nc.sbuf_tensor("rSf", [128, S], BF16))
                    tmp2b = [E1(nc.sbuf_tensor(f"tp2B{i}", [128, 512], BF16))
                             for i in range(2)]
                    vstage = [E1(nc.sbuf_tensor(f"vsB{i}", [128, 1040], BF16))
                              for i in range(2)]
                    ptmb = [E1(nc.sbuf_tensor(f"ptmB{i}", [128, 512], F32))
                            for i in range(2)]
                    ptr = [E1(nc.psum_tensor(f"ptB{i}", [128, 1024], BF16)) for i in range(2)]
                    pq = [E1(nc.psum_tensor(f"pqB{i}", [128, 512], F32)) for i in range(2)]
                    psw = [E1(nc.psum_tensor(f"pwB{i}", [128, 512], F32)) for i in range(2)]
                    pv = [E1(nc.psum_tensor(f"pvB{i}", [128, 512], F32)) for i in range(2)]
                    sl_xt = [slotinfo() for _ in range(2)]
                    sl_fm = [slotinfo() for _ in range(2)]
                    sl_ptr = [slotinfo() for _ in range(2)]
                    sl_pq = [slotinfo() for _ in range(2)]
                    sl_psw = [slotinfo() for _ in range(2)]
                    sl_qr = slotinfo()
                    sl_kr = slotinfo()
                    sl_rope = [slotinfo()]
                    sl_tmp1 = [slotinfo() for _ in range(2)]
                    sl_tmp2 = [slotinfo() for _ in range(2)]
                    sl_pv = [slotinfo() for _ in range(2)]
                    sl_vs = [slotinfo() for _ in range(2)]
                    tms0 = p.do("vector", lambda g: g.memset(
                        vstage[0][:].rearrange("p (h j) -> p h j", j=65)[:, :, 64:65], 1.0),
                        sem="dve")
                    tms1 = p.do("vector", lambda g: g.memset(
                        vstage[1][:].rearrange("p (h j) -> p h j", j=65)[:, :, 64:65], 1.0),
                        sem="dve")
                    sl_vs[0]["users"].append(tms0)
                    sl_vs[1]["users"].append(tms1)

                    dma("sync", "dR0", rCf[:], C["ropeC"][:, :])
                    dma("sync", "dR0", rSf[:], C["ropeS"][:, :])
                    trcs_all = p.tok("dR0")
                    for m in range(8):
                        im = m % 2
                        trcs = trcs_all
                        rC = [rCf[:, m * 512:(m + 1) * 512]]
                        rS = [rSf[:, m * 512:(m + 1) * 512]]
                        tfm_done = []
                        for sub in range(4):
                            tg = m * 4 + sub
                            ix = tg % 2
                            tload, ttr = ln_tile(RES["ap"], tg, xts[ix], sl_xt[ix], f"dX{ix}",
                                                 ptr[ix], sl_ptr[ix])
                            tev = p.do("vector",
                                       lambda g, ix=ix, im=im, sub=sub: g.tensor_copy(
                                           xfm[im][:].rearrange("p (c t) -> p c t", t=512)[:, :, sub * 128:(sub + 1) * 128],
                                           ptr[ix][:].rearrange("p (c t) -> p c t", t=128)),
                                       waits=[ttr] + (sl_fm[im]["users"] if sub == 0 else []),
                                       sem="dve")
                            if sub == 0:
                                sl_fm[im]["users"] = []
                            sl_ptr[ix]["users"].append(tev)
                            tfm_done.append(tev)
                            t_glob = m * 4 + sub
                            ivs = t_glob % 2
                            tevv = None
                            for hf in range(2):
                                tmmv = None
                                for dc in range(NCH):
                                    tmmv = p.do("tensor",
                                                lambda g, dc=dc, hf=hf, im=im, sub=sub: g.matmul(
                                                    pv[hf][:, :],
                                                    xfm[im][:, dc * 512 + sub * 128: dc * 512 + sub * 128 + 128],
                                                    wV[:, dc * D + hf * 512: dc * D + hf * 512 + 512],
                                                    start=(dc == 0), stop=False),
                                                waits=([twall, tev] + sl_pv[hf]["users"]) if dc == 0 else [],
                                                sem=None)
                                tbv = p.do("tensor",
                                           lambda g, hf=hf: g.matmul(
                                               pv[hf][:, :], onesr[0:1, :],
                                               brow[0:1, D + hf * 512: D + hf * 512 + 512],
                                               start=False, stop=True), sem="pe")
                                sl_pv[hf]["users"] = []
                                sl_fm[im]["users"].append(tbv)
                                tevv = p.do("vector",
                                            lambda g, hf=hf, ivs=ivs: g.tensor_copy(
                                                vstage[ivs][:].rearrange("p (h j) -> p h j", j=65)[:, hf * 8:(hf + 1) * 8, 0:64],
                                                pv[hf][:, :].rearrange("p (o h j) -> p o h j",
                                                                       o=1, j=64)[:, 0, :, :]),
                                            waits=[tbv] + (sl_vs[ivs]["users"] if hf == 0 else []),
                                            sem="dve")
                                if hf == 0:
                                    sl_vs[ivs]["users"] = []
                                sl_pv[hf]["users"].append(tevv)
                                sl_vs[ivs]["users"].append(tevv)
                            tvw = dma("gpsimd", "dA", scrV[:, t_glob * 1040:(t_glob + 1) * 1040],
                                      vstage[ivs][:], waits=[tevv])
                            sl_vs[ivs]["users"].append(tvw)
                        pend_ropes = []
                        for which in range(2):
                            wbuf = wA if which == 0 else wB
                            raw = qr if which == 0 else kr
                            sl_raw = sl_qr if which == 0 else sl_kr
                            for ch in range(NCH):
                                ip = ch % 2
                                tmm = None
                                for dc in range(NCH):
                                    tmm = p.do("tensor",
                                               lambda g, dc=dc, ch=ch, ip=ip, im=im, wbuf=wbuf: g.matmul(
                                                   pq[ip][:, :],
                                                   wbuf[:, dc * D + ch * 128: dc * D + ch * 128 + 128],
                                                   xfm[im][:, dc * 512:(dc + 1) * 512],
                                                   start=(dc == 0), stop=(dc == NCH - 1)),
                                               waits=([twall] + tfm_done + sl_pq[ip]["users"]) if dc == 0 else [],
                                               sem="pe" if dc == NCH - 1 else None)
                                sl_pq[ip]["users"] = []
                                tev = p.do("vector",
                                           lambda g, ch=ch, ip=ip, raw=raw, which=which: g.tensor_scalar(
                                               raw[:, ch * 512:(ch + 1) * 512], pq[ip][:, :],
                                               bias_pp[:, 16 + which * 8 + ch: 17 + which * 8 + ch],
                                               None, OP.add),
                                           waits=[tmm] + (sl_raw["users"] if ch == 0 else []),
                                           sem="dve")
                                if ch == 0:
                                    sl_raw["users"] = []
                                sl_pq[ip]["users"].append(tev)
                                sl_fm[im]["users"].append(tmm)

                                def mk_rope(ch=ch, tev=tev, raw=raw, which=which,
                                            m=m, trcs=trcs, sl_raw=sl_raw,
                                            rC=rC, rS=rS):
                                    isw = ch % 2
                                    tmp2 = tmp2b[isw]
                                    ptm = ptmb[isw]
                                    tsw = p.do("tensor",
                                               lambda g, ch=ch, raw=raw, isw=isw: g.matmul(
                                                   psw[isw][:, :], p32[:, :],
                                                   raw[:, ch * 512:(ch + 1) * 512],
                                                   start=True, stop=True),
                                               waits=[tev] + sl_psw[isw]["users"], sem="pe")
                                    sl_psw[isw]["users"] = []
                                    e1, s1 = ("gpsimd", "pool") if isw == 0 else ("vector", "dve")
                                    tt1 = p.do(e1,
                                               lambda g, ch=ch, raw=raw, tmp2=tmp2: g.tensor_tensor(
                                                   tmp2[:, :], raw[:, ch * 512:(ch + 1) * 512],
                                                   rC[0][:, :], OP.mult),
                                               waits=[tev, trcs, tsw] + sl_tmp2[isw]["users"], sem=s1)
                                    sl_tmp2[isw]["users"] = []
                                    tt2 = p.do("vector",
                                               lambda g, isw=isw, ptm=ptm: g.tensor_tensor(
                                                   ptm[:, :], psw[isw][:, :], rS[0][:, :],
                                                   OP.mult),
                                               waits=[tsw, trcs] + sl_tmp1[isw]["users"], sem="dve")
                                    sl_tmp1[isw]["users"] = []
                                    sl_psw[isw]["users"].append(tt2)
                                    if which == 0:
                                        dst_ap = raw[:, ch * 512:(ch + 1) * 512]
                                    else:
                                        dst_ap = kb[:, ch * S + m * 512: ch * S + m * 512 + 512]
                                    e2, s2 = ("vector", "dve") if isw == 0 else ("gpsimd", "pool")
                                    tdst = p.do(e2,
                                                lambda g, dst_ap=dst_ap, tmp2=tmp2, ptm=ptm: g.tensor_tensor(
                                                    dst_ap, tmp2[:, :], ptm[:, :], OP.add),
                                                waits=[tt1, tt2], sem=s2)
                                    sl_tmp1[isw]["users"].append(tdst)
                                    sl_tmp2[isw]["users"].append(tdst)
                                    sl_rope[0]["users"] += [tt1, tt2]
                                    sl_raw["users"].append(tdst)
                                pend_ropes.append(mk_rope)
                                if ch % 2 == 1:
                                    for fn in pend_ropes:
                                        fn()
                                    pend_ropes = []
                            if which == 0:
                                tqw = dma("gpsimd", "dB",
                                          scrB[:, :, m * 512:(m + 1) * 512].rearrange("c p t -> p c t"),
                                          qr[:].rearrange("p (c t) -> p c t", t=512),
                                          waits=list(sl_qr["users"]))
                                sl_qr["users"].append(tqw)
                    if dbg:
                        p.barrier()
                        dma("sync", "dS", DBG["dkb"], kb[:])
                        dma("sync", "dS", DBG["dqb"].rearrange("c p t -> p c t"),
                            scrB.rearrange("c p t -> p c t"))
                        p.barrier()
                    p.flush(block, semh)

                with ExitStack() as P2:
                    E2 = P2.enter_context
                    wO = E2(nc.sbuf_tensor("woB", [128, NCH * D], BF16))
                    vxb = E2(nc.sbuf_tensor("vxb", [128, NTOK * 16 * 65], BF16))
                    wload(wO[:].rearrange("p (c m) -> p c m", m=D), wsl(W["bWo"]),
                          waits=[p.tok("pe")])
                    twall = p.tok("dW")
                    seam_b = [p.tok(x) for x in ("pe", "act", "dve", "pool", "dA", "dB")]
                    tvx = dma("scalar", "dA", vxb[:], scrV[:], waits=seam_b)
                    qm2 = [E2(nc.sbuf_tensor(f"qmB{i}", [128, NCH * 512], BF16))
                           for i in range(2)]
                    probs = E2(nc.sbuf_tensor("prB", [128, 4 * 1024], BF16))
                    ctxf = E2(nc.sbuf_tensor("cfB", [128, NCH * 512], BF16))
                    rrow = E2(nc.sbuf_tensor("rrB", [1, 1024], F32))
                    rrowb = E2(nc.sbuf_tensor("rbB", [1, 1024], BF16))
                    rbt = E2(nc.sbuf_tensor("rtB", [128, 1024], BF16))
                    xrs = [E2(nc.sbuf_tensor(f"xrB{i}", [128, D], F32)) for i in range(2)]
                    pSh = E2(nc.psum_tensor("pSB", [128, 2048], F32))
                    pOE = E2(nc.psum_tensor("pOEB", [128, 1024], F32))
                    pCb = E2(nc.psum_tensor("pCB", [128, 1024], F32))
                    pC = [pCb[:, 0:512], pCb[:, 512:1024]]
                    pO = [pOE[:, 0:512], pOE[:, 512:1024]]
                    sl_qm = [slotinfo() for _ in range(2)]
                    sl_pS0 = [slotinfo() for _ in range(2)]
                    sl_pS1 = slotinfo()
                    sl_pC = [slotinfo() for _ in range(2)]
                    sl_pC[0]["users"].append(tvx)
                    sl_pC[1]["users"].append(tvx)
                    sl_pr = [slotinfo() for _ in range(2)]
                    sl_ctxf = slotinfo()
                    sl_rr = slotinfo()
                    sl_rrb = slotinfo()
                    sl_rbt = slotinfo()
                    sl_xr = [slotinfo() for _ in range(2)]

                    for m in range(8):
                        iq = m % 2
                        qm = qm2[iq]
                        tldq = dma("sync", f"dQ{iq}",
                                   qm[:].rearrange("p (c t) -> p c t", t=512),
                                   scrB[:, :, m * 512:(m + 1) * 512].rearrange("c p t -> p c t"),
                                   waits=sl_qm[iq]["users"] + [p.tok("dB")] +
                                         (seam_b if m == 0 else []))
                        sl_qm[iq]["users"] = []
                        ctx_done = []
                        pend = {"fn": None}

                        def flush_norm():
                            if pend["fn"] is not None:
                                pend["fn"]()
                                pend["fn"] = None
                        for chh in range(NCH):
                            texp = [None] * 16

                            def emit_scores(kp, chh=chh, tldq=tldq, texp=texp, qm=qm):
                                mml = [None, None]
                                tee = [None, None]
                                srcE = pSh[:, 0:1024] if kp % 2 == 0 else pOE[:, :]
                                for sel in range(2):
                                    pb = sel * 64
                                    slot = sl_pS0[kp % 2] if sel == 0 else sl_pS1
                                    for j in range(2):
                                        c = 2 * kp + j
                                        if sel == 0:
                                            dst = srcE[:, j * 512:(j + 1) * 512]
                                        else:
                                            dst = pSh[:, 1024 + j * 512: 1024 + j * 512 + 512]
                                        w0 = []
                                        if j == 0:
                                            w0 = slot["users"] + ([tldq] if sel == 0 else [])
                                        mml[sel] = p.do(
                                            "tensor",
                                            lambda g, c=c, pb=pb, chh=chh, dst=dst, qm=qm: g.matmul(
                                                dst,
                                                kb[pb:pb + 64, chh * S + c * 128: chh * S + c * 128 + 128],
                                                qm[pb:pb + 64, chh * 512:(chh + 1) * 512],
                                                start=True, stop=True,
                                                tile_position=(pb, 0)),
                                            waits=w0,
                                            sem="pe" if j == 1 else None)
                                        if j == 0:
                                            slot["users"] = []
                                        else:
                                            sl_qm[m % 2]["users"].append(mml[sel])
                                    tee[sel] = p.do(
                                        "scalar",
                                        lambda g, kp=kp, sel=sel, srcE=srcE: g.activation(
                                            probs[:, (kp % 2) * 2048 + sel * 1024:
                                                  (kp % 2) * 2048 + sel * 1024 + 1024],
                                            srcE if sel == 0 else pSh[:, 1024:2048],
                                            AF.Exp,
                                            bias=NEG_SHIFT, scale=SC_B),
                                        waits=[mml[sel]] +
                                              (sl_pr[kp % 2]["users"] if sel == 0 else []),
                                        sem="act")
                                    if sel == 0:
                                        sl_pr[kp % 2]["users"] = []
                                    slot["users"].append(tee[sel])
                                texp[kp] = tee

                            def emit_ctx(kpc, chh=chh, texp=texp):
                                outt = [None, None]
                                for sel in range(2):
                                    h = 2 * chh + sel
                                    tc = None
                                    for j in range(2):
                                        c = 2 * kpc + j
                                        w0 = []
                                        if j == 0:
                                            w0 = [texp[kpc][sel]]
                                            if c == 0:
                                                w0 = w0 + sl_pC[sel]["users"]
                                        tc = p.do(
                                            "tensor",
                                            lambda g, c=c, sel=sel, h=h, kpc=kpc, j=j: g.matmul(
                                                pC[sel][0:65, :],
                                                vxb[:, c * 1040 + h * 65: c * 1040 + h * 65 + 65],
                                                probs[:, (kpc % 2) * 2048 + sel * 1024 + j * 512:
                                                      (kpc % 2) * 2048 + sel * 1024 + j * 512 + 512],
                                                start=(c == 0), stop=(c == 31)),
                                            waits=w0,
                                            sem="pe" if j == 1 else None)
                                        if j == 0 and c == 0:
                                            sl_pC[sel]["users"] = []
                                    sl_pr[kpc % 2]["users"].append(tc)
                                    outt[sel] = tc
                                return outt

                            for kp in range(16):
                                emit_scores(kp)
                                if kp == 1:
                                    flush_norm()
                                if kp >= 1:
                                    emit_ctx(kp - 1)
                            tcs = emit_ctx(15)
                            tve = [None, None]
                            for sel in range(2):
                                tve[sel] = p.do("vector",
                                                lambda g, sel=sel, chh=chh: g.tensor_copy(
                                                    ctxf[sel * 64:sel * 64 + 64,
                                                         chh * 512:(chh + 1) * 512],
                                                    pCb[0:64, sel * 512:sel * 512 + 512]),
                                                waits=[tcs[sel]] +
                                                      (sl_ctxf["users"] if chh == 0 and sel == 0 else []),
                                                sem="dve")
                                if chh == 0 and sel == 0:
                                    sl_ctxf["users"] = []
                                sl_pC[sel]["users"].append(tve[sel])
                            tl0 = p.do("scalar", lambda g: g.activation(
                                rrow[0:1, :], pCb[64:65, :], AF.Ln,
                                bias=0.0, scale=1.0),
                                waits=[tcs[0], tcs[1]] + sl_rr["users"], sem="act")
                            sl_rr["users"] = []
                            te0 = p.do("scalar", lambda g: g.activation(
                                rrowb[0:1, :], rrow[0:1, :], AF.Exp,
                                bias=0.0, scale=-1.0),
                                waits=[tl0] + sl_rrb["users"], sem="act")
                            sl_rrb["users"] = []
                            te1 = te0
                            sl_rr["users"] += [te0]

                            def norm(chh=chh, tcs=tcs, te0=te0, te1=te1, tve=tve):
                                tb0 = p.do("tensor", lambda g: g.matmul(
                                    pO[0][:, :], onesr[0:1, :], rrowb[0:1, 0:512],
                                    start=True, stop=True),
                                    waits=[te0] + sl_pS0[1]["users"], sem="pe")
                                sl_pS0[1]["users"] = []
                                tb1 = p.do("tensor", lambda g: g.matmul(
                                    pO[1][:, :], onesr[0:1, :], rrowb[0:1, 512:1024],
                                    start=True, stop=True),
                                    waits=[te1], sem="pe")
                                sl_rrb["users"] += [tb0, tb1]
                                tr4a = p.do("vector", lambda g: g.tensor_copy(
                                    rbt[:, 0:512], pO[0][:, :]),
                                    waits=[tb0] + sl_rbt["users"], sem="dve")
                                sl_rbt["users"] = []
                                tr4 = p.do("vector", lambda g: g.tensor_copy(
                                    rbt[:, 512:1024], pO[1][:, :]),
                                    waits=[tb1], sem="dve")
                                sl_pS0[1]["users"] += [tr4a, tr4]
                                for sel in range(2):
                                    tev = p.do("vector",
                                               lambda g, sel=sel, chh=chh: g.tensor_tensor(
                                                   ctxf[sel * 64:sel * 64 + 64,
                                                        chh * 512:(chh + 1) * 512],
                                                   ctxf[sel * 64:sel * 64 + 64,
                                                        chh * 512:(chh + 1) * 512],
                                                   rbt[sel * 64:sel * 64 + 64,
                                                       sel * 512:sel * 512 + 512],
                                                   OP.mult),
                                               waits=[tr4a if sel == 0 else tr4, tve[sel]],
                                               sem="dve")
                                    sl_rbt["users"].append(tev)
                                    ctx_done.append(tev)
                            pend["fn"] = norm
                        flush_norm()
                        for sub in range(4):
                            tso = m * 512 + sub * 128
                            iy = sub % 2
                            tpo = [None, None]
                            for hf in range(2):
                                for dc in range(NCH):
                                    p.do("tensor",
                                         lambda g, dc=dc, sub=sub, hf=hf: g.matmul(
                                             pO[hf][:, :],
                                             ctxf[:, dc * 512 + sub * 128: dc * 512 + sub * 128 + 128],
                                             wO[:, dc * D + hf * 512: dc * D + hf * 512 + 512],
                                             start=(dc == 0), stop=False),
                                         waits=([twall] + ctx_done + sl_pS0[1]["users"]) if (dc == 0 and hf == 0) else [],
                                         sem=None)
                                if hf == 0:
                                    sl_pS0[1]["users"] = []
                                tpo[hf] = p.do("tensor",
                                               lambda g, hf=hf: g.matmul(
                                                   pO[hf][:, :], onesr[32:33, :],
                                                   brow[32:33, D + hf * 512: D + hf * 512 + 512],
                                                   start=False, stop=True), sem="pe")
                            tlx = dma("sync", f"dT{iy}", xrs[iy][:], RES["ap"][tso:tso + 128, :],
                                      waits=sl_xr[iy]["users"])
                            sl_xr[iy]["users"] = []
                            ta0 = p.do("vector",
                                       lambda g, iy=iy: g.tensor_tensor(
                                           xrs[iy][:, 0:512], pO[0][:, :],
                                           xrs[iy][:, 0:512], OP.add),
                                       waits=[tpo[0], tlx], sem="dve")
                            ta1 = p.do("vector",
                                       lambda g, iy=iy: g.tensor_tensor(
                                           xrs[iy][:, 512:1024], pO[1][:, :],
                                           xrs[iy][:, 512:1024], OP.add),
                                       waits=[tpo[1]], sem="dve")
                            sl_pS0[1]["users"] += [ta0, ta1]
                            sl_ctxf["users"].append(tpo[1])
                            tyw = dma("gpsimd", "dY", yout[tso:tso + 128, :], xrs[iy][:],
                                      waits=[ta0, ta1])
                            sl_xr[iy]["users"] += [tyw]
                    p.barrier()
                    p.flush(block, semh)
                RES["ap"] = yout

        if "F2" in phases:
            ffn_phase("v")

        p.wait("sync", [p.tok("dY")])
        p.flush(block, semh)

    return nc


def kernel(**inputs):
    x_full = np.ascontiguousarray(inputs["L0_attn_a_x"])
    consts = _host_consts()
    wmap = {
        "aWqkv": inputs["L0_attn_a_W_qkv"], "aBqkv": inputs["L0_attn_a_b_qkv"],
        "aWo": inputs["L0_attn_a_W_out"], "aBo": inputs["L0_attn_a_b_out"],
        "bWq": inputs["L0_attn_b_Wq"], "bBq": inputs["L0_attn_b_bq"],
        "bWk": inputs["L0_attn_b_Wk"], "bBk": inputs["L0_attn_b_bk"],
        "bWv": inputs["L0_attn_b_Wv"], "bBv": inputs["L0_attn_b_bv"],
        "bWo": inputs["L0_attn_b_Wo"], "bBo": inputs["L0_attn_b_bo"],
        "rw": inputs["L0_ffn_router_w"], "w1": inputs["L0_ffn_w1"],
        "w2": inputs["L0_ffn_w2"], "sgu": inputs["L0_ffn_shared_gate_up"],
        "sdn": inputs["L0_ffn_shared_down"], "sgw": inputs["L0_ffn_shared_gate_w"],
    }
    wmap = {k: np.ascontiguousarray(np.asarray(v, dtype=np.float32))
            for k, v in wmap.items()}
    nc = build_nc(_PHASES)
    n = 8
    in_maps = []
    for i in range(n):
        mp = {"x": np.ascontiguousarray(x_full[i])}
        mp.update(wmap)
        mp.update(consts)
        in_maps.append(mp)
    res = run_bass_kernel_spmd(nc, in_maps, core_ids=list(range(n)))
    return np.stack([r["y"] for r in res.results], axis=0)


# version 65
# speedup vs baseline: 1.1810x; 1.0160x over previous
import numpy as np
import ml_dtypes
from contextlib import ExitStack
import concourse.bass as bass
import concourse.mybir as mybir
from concourse.bass_utils import run_bass_kernel_spmd
from concourse.alu_op_type import AluOpType as OP

F32 = mybir.dt.float32
BF16 = mybir.dt.bfloat16
AF = mybir.ActivationFunctionType
AX = mybir.AxisListType

S, D = 4096, 1024
NTOK = S // 128
NCH = D // 128
EPS = 1e-5
NEG_SHIFT = -10.0
SC_A = 1.0 / 32.0
SC_B = 0.125
CONST_VALS = [0.0, 1.0, EPS, -0.5, NEG_SHIFT, SC_A, SC_B, -1.0]

ENGINES = ("sync", "tensor", "vector", "scalar", "gpsimd")
_PHASES = ("A", "F1", "B", "F2")


class Prog:
    def __init__(self):
        self.ops = {e: [] for e in ENGINES}
        self.cnt = {}
        self.lastw = {e: {} for e in ENGINES}
        self.sems = set()

    def wait(self, eng, toks):
        best = {}
        for t in toks:
            if t is None:
                continue
            s, v = t
            if v > best.get(s, 0):
                best[s] = v
        for s, v in sorted(best.items()):
            if v <= 0 or self.lastw[eng].get(s, 0) >= v:
                continue
            self.lastw[eng][s] = v
            self.ops[eng].append(("w", s, v))

    def do(self, eng, fn, waits=(), sem=None, amt=1):
        self.wait(eng, waits)
        tok = None
        if sem is not None:
            self.sems.add(sem)
            self.cnt[sem] = self.cnt.get(sem, 0) + amt
            tok = (sem, self.cnt[sem])
        self.ops[eng].append(("e", fn, sem, amt))
        return tok

    def tok(self, sem):
        self.sems.add(sem)
        return (sem, self.cnt.get(sem, 0))

    def barrier(self):
        toks = [self.tok(s) for s in sorted(self.sems)]
        for e in ENGINES:
            self.wait(e, toks)

    def flush(self, block, semh):
        for e in ENGINES:
            ops = self.ops[e]
            if not ops:
                continue

            def body(engobj, ops=ops):
                for op in ops:
                    if op[0] == "w":
                        engobj.wait_ge(semh[op[1]], op[2])
                    else:
                        ins = op[1](engobj)
                        if op[2] is not None:
                            ins.then_inc(semh[op[2]], op[3])
            getattr(block, e)(body)
            self.ops[e] = []


def slotinfo():
    return {"users": []}


def _host_consts():
    cst = np.zeros((128, 16), dtype=np.float32)
    for i, v in enumerate(CONST_VALS):
        cst[:, i] = v
    idb = np.eye(128, dtype=ml_dtypes.bfloat16)
    idf = np.eye(128, dtype=np.float32)
    j = np.arange(32, dtype=np.float32)
    inv = (1.0 / (10000.0 ** (j / 32.0))).astype(np.float32)
    t = np.arange(S, dtype=np.float32)
    ang = t[None, :] * inv[:, None]
    c = np.cos(ang).astype(np.float32)
    s = np.sin(ang).astype(np.float32)
    ropeC = np.tile(c, (4, 1)).astype(ml_dtypes.bfloat16)
    sgn = np.repeat(np.array([-1.0, 1.0, -1.0, 1.0], np.float32), 32)
    ropeS = (np.tile(s, (4, 1)) * sgn[:, None]).astype(ml_dtypes.bfloat16)
    p32 = np.zeros((128, 128), dtype=ml_dtypes.bfloat16)
    p32[np.arange(128) ^ 32, np.arange(128)] = 1.0
    return {"cst": cst, "idb": idb, "idf": idf, "ropeC": ropeC,
            "ropeS": ropeS, "p32": p32}


def build_nc(phases=None, dbg=False):
    if phases is None:
        phases = _PHASES
    nc = bass.Bass("TRN2", target_bir_lowering=False)
    xin = nc.dram_tensor("x", [S, D], F32, kind="ExternalInput").ap()
    yout = nc.dram_tensor("y", [S, D], F32, kind="ExternalOutput").ap()
    W = {}
    for nm, shp in [
        ("aWqkv", [D, 3 * D]), ("aBqkv", [3 * D]), ("aWo", [D, D]), ("aBo", [D]),
        ("bWq", [D, D]), ("bBq", [D]), ("bWk", [D, D]), ("bBk", [D]),
        ("bWv", [D, D]), ("bBv", [D]), ("bWo", [D, D]), ("bBo", [D]),
        ("rw", [D, 8]), ("w1", [8, D, D]), ("w2", [8, D, D]),
        ("sgu", [D, 2 * D]), ("sdn", [D, D]), ("sgw", [D, 1]),
    ]:
        W[nm] = nc.dram_tensor(nm, shp, F32, kind="ExternalInput").ap()
    C = {}
    for nm, shp, dt in [("cst", [128, 16], F32), ("idb", [128, 128], BF16),
                        ("idf", [128, 128], F32), ("ropeC", [128, S], BF16),
                        ("ropeS", [128, S], BF16), ("p32", [128, 128], BF16)]:
        C[nm] = nc.dram_tensor(nm, shp, dt, kind="ExternalInput").ap()
    scrA = nc.dram_tensor("scrA", [NCH, 128, S], BF16, kind="Internal").ap()
    RES = {"ap": xin}
    DBG = {}
    if dbg:
        for nm, shp, dt in [("dka", [128, NCH * S], BF16), ("dva", [128, NTOK * D], BF16),
                            ("dxl", [NCH, 128, S], BF16), ("dqf", [128, 2048], BF16),
                            ("dpr", [128, 32 * 256], BF16), ("dct", [128, D], BF16),
                            ("dkb", [128, NCH * S], BF16), ("dqb", [NCH, 128, S], BF16),
                            ("dvx", [128, NTOK * 16 * 65], BF16)]:
            DBG[nm] = nc.dram_tensor(nm, shp, dt, kind="ExternalOutput").ap()
    scrB = nc.dram_tensor("scrB", [NCH, 128, S], BF16, kind="Internal").ap()
    scrV = nc.dram_tensor("scrV", [128, NTOK * 1040], BF16, kind="Internal").ap()

    p = Prog()

    with ExitStack() as TOP:
        E = TOP.enter_context
        semh = {}
        for s_ in ("pe", "dve", "act", "pool", "dS", "dW", "dY", "dA", "dB",
                   "dX0", "dX1", "dQ0", "dQ1", "dT0", "dT1", "dR0", "dR1",
                   "dE0", "dE1"):
            semh[s_] = E(nc.semaphore(s_))
        block = E(nc.Block())

        cst = E(nc.sbuf_tensor("cst_s", [128, 16], F32))
        for i, v in enumerate(CONST_VALS):
            nc.const_aps.register(F32, float(v), cst[:, i:i + 1])
        idb = E(nc.sbuf_tensor("idb_s", [128, 128], BF16))
        idf = E(nc.sbuf_tensor("idf_s", [128, 128], F32))
        onesr = E(nc.sbuf_tensor("onesr", [128, 128], BF16))
        onesc = E(nc.sbuf_tensor("onesc", [128, 1], BF16))
        bias_pp = E(nc.sbuf_tensor("bias_pp", [128, 32], F32))
        brow = E(nc.sbuf_tensor("brow", [128, 2 * D], BF16))
        p32 = E(nc.sbuf_tensor("p32_s", [128, 128], BF16))
        mvt2 = [E(nc.sbuf_tensor(f"mvt{i}", [128, 4], F32)) for i in range(2)]
        rwb_t = E(nc.sbuf_tensor("rwb_t", [128, NCH * 8], BF16))
        sgwb_t = E(nc.sbuf_tensor("sgwb_t", [128, 8], BF16))
        st122 = [E(nc.sbuf_tensor(f"st12{i}", [128, 12], F32)) for i in range(2)]
        xlnb2 = [E(nc.sbuf_tensor(f"xlnb{i}", [128, 1024], BF16)) for i in range(2)]

        def dma(eng, dsem, out, in_, waits=()):
            return p.do(eng, lambda g, o=out, i=in_: g.dma_start(out=o, in_=i),
                        waits=waits, sem=dsem, amt=16)

        def wload(dst3, src3, waits=(), dsem="dW"):
            return dma("gpsimd", dsem, dst3, src3, waits=waits)

        def wsl(w_ap, cols=None):
            a = w_ap if cols is None else w_ap[:, cols[0]:cols[1]]
            return a.rearrange("(c p) m -> p c m", p=128)

        LN = [{"mv": None, "st": None, "xln": None, "ap": None} for _ in range(2)]
        LNCNT = [0]

        def ln_tile(src_dram, tg, xt, sl_xt, dsem, ptr, sl_ptr):
            i = LNCNT[0] % 2
            LNCNT[0] += 1
            st12 = st122[i]
            mvt = mvt2[i]
            xlnb = xlnb2[i]
            L = LN[i]
            tload = dma("sync", dsem, xt[:], src_dram[tg * 128:(tg + 1) * 128, :],
                        waits=sl_xt["users"])
            sl_xt["users"] = []
            t1 = p.do("vector", lambda g, st12=st12: g.bn_stats(st12[:, 0:6], xt[:, 0:512]),
                      waits=[tload, L["st"]], sem="dve")
            t2 = p.do("vector", lambda g, st12=st12: g.bn_stats(st12[:, 6:12], xt[:, 512:1024]),
                      waits=[t1], sem="dve")
            t3 = p.do("vector", lambda g, st12=st12, mvt=mvt: g.bn_aggr(mvt[:, 0:2], st12[:, 0:12]),
                      waits=[t2, L["mv"]], sem="dve")
            t4 = p.do("scalar", lambda g, mvt=mvt: g.activation(mvt[:, 2:3], mvt[:, 1:2], AF.Ln,
                                                                bias=EPS, scale=1.0),
                      waits=[t3, L["ap"]], sem="act")
            t5 = p.do("scalar", lambda g, mvt=mvt: g.activation(mvt[:, 2:3], mvt[:, 2:3], AF.Exp,
                                                                bias=0.0, scale=-0.5),
                      waits=[t4], sem="act")
            t6 = p.do("vector", lambda g, mvt=mvt: g.tensor_scalar(mvt[:, 3:4], mvt[:, 0:1],
                                                                   mvt[:, 2:3], -1.0, OP.mult, OP.mult),
                      waits=[t5], sem="dve")
            t7 = p.do("scalar", lambda g, mvt=mvt, xlnb=xlnb: g.activation(
                xlnb[:], xt[:], AF.Identity, bias=mvt[:, 3:4], scale=mvt[:, 2:3]),
                waits=[t6, L["xln"]], sem="act")
            sl_xt["users"].append(t7)
            L["mv"], L["st"], L["ap"] = t6, t3, t7
            ttr = None
            for c in range(NCH):
                ttr = p.do("tensor",
                           lambda g, c=c, xlnb=xlnb: g.transpose(ptr[:, c * 128:(c + 1) * 128],
                                                                 xlnb[:, c * 128:(c + 1) * 128], idb[:]),
                           waits=([t7] + sl_ptr["users"]) if c == 0 else [],
                           sem="pe" if c == NCH - 1 else None)
            sl_ptr["users"] = []
            L["xln"] = ttr
            return tload, ttr

        dma("sync", "dS", cst[:], C["cst"])
        dma("sync", "dS", idb[:], C["idb"])
        dma("sync", "dS", idf[:], C["idf"])
        dma("sync", "dS", p32[:], C["p32"])
        tS = p.tok("dS")
        p.do("gpsimd", lambda g: g.memset(onesr[:, :], 1.0), sem="pool")
        p.do("gpsimd", lambda g: g.memset(onesc[:, :], 1.0), sem="pool")
        dma("gpsimd", "dW", brow[0:1, 0:D],
            W["aBqkv"][2 * D:3 * D].rearrange("(o d) -> o d", o=1))
        dma("gpsimd", "dW", brow[32:33, 0:D], W["aBo"].rearrange("(o d) -> o d", o=1))
        dma("gpsimd", "dW", brow[0:1, D:2 * D], W["bBv"].rearrange("(o d) -> o d", o=1))
        dma("gpsimd", "dW", brow[32:33, D:2 * D], W["bBo"].rearrange("(o d) -> o d", o=1))
        with nc.sbuf_tensor("btmp", [8, 128], F32) as btmp, \
             nc.sbuf_tensor("btmpb", [8, 128], BF16) as btmpb, \
             nc.psum_tensor("pbias", [128, 512], F32) as pbias, \
             nc.psum_tensor("pbias2", [128, 1024], BF16) as pbias2:
            dma("gpsimd", "dW", rwb_t[:].rearrange("p (c m) -> p c m", m=8), wsl(W["rw"]))
            tprev = None
            for k, src in enumerate([W["aBqkv"][0:D], W["aBqkv"][D:2 * D],
                                     W["bBq"], W["bBk"]]):
                tl = dma("gpsimd", "dW", btmp[:], src.rearrange("(c p) -> c p", p=128),
                         waits=[tprev] if tprev else [])
                tt = p.do("tensor", lambda g: g.transpose(pbias[:, 0:8], btmp[:], idf[0:8, 0:8]),
                          waits=[tl, tS, tprev, p.tok("dW")], sem="pe")
                tprev = p.do("vector",
                             lambda g, k=k: g.tensor_copy(bias_pp[:, k * 8:(k + 1) * 8],
                                                          pbias[:, 0:8]),
                             waits=[tt], sem="dve")
            tlg = dma("gpsimd", "dW", btmpb[:],
                      W["sgw"].rearrange("(c p) m -> c (p m)", p=128))
            ttg = p.do("tensor", lambda g: g.transpose(pbias2[:, 0:8], btmpb[:], idb[0:8, 0:8]),
                       waits=[tlg, tprev, p.tok("dW")], sem="pe")
            p.do("vector", lambda g: g.tensor_copy(sgwb_t[:, 0:8], pbias2[:, 0:8]),
                 waits=[ttg], sem="dve")
            seam0 = [p.tok(x) for x in ("dS", "dW", "pe", "dve", "pool")]
            p.flush(block, semh)

        if "A" in phases:
            with ExitStack() as PH:
                EE = PH.enter_context
                ka = EE(nc.sbuf_tensor("ka", [128, NCH * S], BF16))
                va = EE(nc.sbuf_tensor("va", [128, NTOK * D], BF16))
                wA = EE(nc.sbuf_tensor("wAa", [128, NCH * D], BF16))
                wB = EE(nc.sbuf_tensor("wBa", [128, NCH * D], BF16))
                PA1 = PH.enter_context(ExitStack())
                E1 = PA1.enter_context
                xts = [E1(nc.sbuf_tensor(f"xa{i}", [128, D], F32)) for i in range(2)]
                xfm = [E1(nc.sbuf_tensor(f"xfa{i}", [128, NCH * 256], BF16))
                       for i in range(2)]
                ptr = [E1(nc.psum_tensor(f"ptA{i}", [128, 1024], BF16)) for i in range(2)]
                pk = [E1(nc.psum_tensor(f"pkA{i}", [128, 512], F32)) for i in range(2)]
                pv = [E1(nc.psum_tensor(f"pvA{i}", [128, 512], F32)) for i in range(4)]
                sl_xt = [slotinfo() for _ in range(2)]
                sl_xt[0]["users"] += seam0
                sl_xt[1]["users"] += seam0
                sl_fm = [slotinfo() for _ in range(2)]
                sl_ptr = [slotinfo() for _ in range(2)]
                sl_pk = [slotinfo() for _ in range(2)]
                sl_pv = [slotinfo() for _ in range(4)]

                wload(wA[:].rearrange("p (c m) -> p c m", m=D), wsl(W["aWqkv"], (D, 2 * D)))
                wload(wB[:].rearrange("p (c m) -> p c m", m=D), wsl(W["aWqkv"], (2 * D, 3 * D)))
                twall = p.tok("dW")

                for t in range(NTOK):
                    ix = t % 2
                    ipr = (t // 2) % 2
                    th = t % 2
                    tload, ttr = ln_tile(xin, t, xts[ix], sl_xt[ix], f"dX{ix}",
                                         ptr[ix], sl_ptr[ix])
                    tev = p.do("vector",
                               lambda g, ix=ix, ipr=ipr, th=th: g.tensor_copy(
                                   xfm[ipr][:].rearrange("p (c t) -> p c t", t=256)[:, :, th * 128:(th + 1) * 128],
                                   ptr[ix][:].rearrange("p (c t) -> p c t", t=128)),
                               waits=[ttr] + (sl_fm[ipr]["users"] if th == 0 else []),
                               sem="dve")
                    if th == 0:
                        sl_fm[ipr]["users"] = []
                        tev_first = tev
                    sl_ptr[ix]["users"].append(tev)
                    if th == 1:
                        tw = dma("gpsimd", "dA",
                                 scrA[:, :, (t - 1) * 128:(t + 1) * 128].rearrange("c p t -> p c t"),
                                 xfm[ipr][:].rearrange("p (c t) -> p c t", t=256),
                                 waits=[tev_first, tev])
                        sl_fm[ipr]["users"].append(tw)
                    for hf in range(2):
                        iv = 2 * (t % 2) + hf
                        for dc in range(NCH):
                            p.do("tensor",
                                 lambda g, dc=dc, hf=hf, iv=iv, ipr=ipr, th=th: g.matmul(
                                     pv[iv][:, :],
                                     xfm[ipr][:, dc * 256 + th * 128: dc * 256 + th * 128 + 128],
                                     wB[:, dc * D + hf * 512: dc * D + hf * 512 + 512],
                                     start=(dc == 0), stop=False),
                                 waits=([twall, tev] + sl_pv[iv]["users"]) if dc == 0 else [],
                                 sem=None)
                        tmm = p.do("tensor",
                                   lambda g, hf=hf, iv=iv: g.matmul(
                                       pv[iv][:, :], onesr[0:1, :],
                                       brow[0:1, hf * 512: hf * 512 + 512],
                                       start=False, stop=True), sem="pe")
                        sl_pv[iv]["users"] = []
                        tevv = p.do("vector",
                                    lambda g, t=t, hf=hf, iv=iv: g.tensor_copy(
                                        va[:, t * D + hf * 512: t * D + hf * 512 + 512],
                                        pv[iv][:, :]),
                                    waits=[tmm], sem="dve")
                        sl_pv[iv]["users"].append(tevv)
                        sl_fm[ipr]["users"].append(tevv)
                    if th == 1:
                        for dk in range(NCH):
                            ip = dk % 2
                            tmm = None
                            for dc in range(NCH):
                                tmm = p.do("tensor",
                                           lambda g, dc=dc, dk=dk, ip=ip, ipr=ipr: g.matmul(
                                               pk[ip][:, 0:256],
                                               wA[:, dc * D + dk * 128: dc * D + dk * 128 + 128],
                                               xfm[ipr][:, dc * 256:(dc + 1) * 256],
                                               start=(dc == 0), stop=(dc == NCH - 1)),
                                           waits=([twall, tev_first, tev] + sl_pk[ip]["users"]) if dc == 0 else [],
                                           sem="pe" if dc == NCH - 1 else None)
                            sl_pk[ip]["users"] = []
                            tevk = p.do("vector",
                                        lambda g, dk=dk, ip=ip, t=t: g.tensor_scalar(
                                            ka[:, dk * S + (t - 1) * 128: dk * S + (t - 1) * 128 + 256],
                                            pk[ip][:, 0:256], bias_pp[:, 8 + dk:9 + dk],
                                            None, OP.add),
                                        waits=[tmm], sem="dve")
                            sl_pk[ip]["users"].append(tevk)
                            sl_fm[ipr]["users"].append(tmm)
                if dbg:
                    p.barrier()
                    dma("sync", "dS", DBG["dka"], ka[:])
                    dma("sync", "dS", DBG["dva"], va[:])
                    dma("sync", "dS", DBG["dxl"].rearrange("c p t -> p c t"),
                        scrA.rearrange("c p t -> p c t"))
                    p.barrier()
                p.flush(block, semh)
                PA1.close()

                with ExitStack() as P2:
                    E2 = P2.enter_context
                    wload(wA[:].rearrange("p (c m) -> p c m", m=D), wsl(W["aWqkv"], (0, D)),
                          waits=[p.tok("pe")])
                    wload(wB[:].rearrange("p (c m) -> p c m", m=D), wsl(W["aWo"]))
                    twall = p.tok("dW")
                    seam_a = [p.tok(x) for x in ("pe", "act", "dve", "pool", "dA")]
                    QM = 256
                    xq2 = [E2(nc.sbuf_tensor(f"xqA{i}", [128, NCH * QM], BF16))
                           for i in range(2)]
                    qfm = E2(nc.sbuf_tensor("qfA", [128, NCH * QM], BF16))
                    probs = E2(nc.sbuf_tensor("prA", [128, NTOK * QM], BF16))
                    ctxt = E2(nc.sbuf_tensor("ctA", [128, D], BF16))
                    rcp = E2(nc.sbuf_tensor("rcA", [128, 1], F32))
                    xrs = [E2(nc.sbuf_tensor(f"xrA{i}", [128, D], F32)) for i in range(2)]
                    pS = [E2(nc.psum_tensor(f"pSA{i}", [128, 512], F32)) for i in range(2)]
                    pc0 = E2(nc.psum_tensor("pc0A", [128, 512], F32))
                    pc1 = E2(nc.psum_tensor("pc1A", [128, 512], F32))
                    pcS = E2(nc.psum_tensor("pcSA", [128, 512], F32))
                    ptrx = E2(nc.psum_tensor("ptxA", [128, 1024], BF16))
                    pO = [E2(nc.psum_tensor(f"pOA{i}", [128, 512], F32)) for i in range(2)]
                    sl_xq2 = [slotinfo() for _ in range(2)]
                    sl_qf = slotinfo()
                    sl_pS = [slotinfo() for _ in range(2)]
                    sl_probs = slotinfo()
                    sl_ctx = slotinfo()
                    sl_ctxt = slotinfo()
                    sl_ctxf = slotinfo()
                    sl_ptrx = slotinfo()
                    sl_pO = slotinfo()
                    sl_rcp = slotinfo()
                    sl_xr = [slotinfo() for _ in range(2)]

                    for m in range(S // QM):
                        iq = m % 2
                        xq = xq2[iq]
                        sl_xq = sl_xq2[iq]
                        tldq = dma("sync", f"dQ{iq}",
                                   xq[:].rearrange("p (c t) -> p c t", t=QM),
                                   scrA[:, :, m * QM:(m + 1) * QM].rearrange("c p t -> p c t"),
                                   waits=sl_xq["users"] + [p.tok("dA")] +
                                         (seam_a if m == 0 else []))
                        sl_xq["users"] = []
                        tevq = None
                        for dq in range(NCH):
                            ip = dq % 2
                            tmm = None
                            for dc in range(NCH):
                                tmm = p.do("tensor",
                                           lambda g, dc=dc, dq=dq, ip=ip, xq=xq: g.matmul(
                                               pS[ip][:, 0:QM],
                                               wA[:, dc * D + dq * 128: dc * D + dq * 128 + 128],
                                               xq[:, dc * QM:(dc + 1) * QM],
                                               start=(dc == 0), stop=(dc == NCH - 1)),
                                           waits=([twall, tldq] + sl_pS[ip]["users"]) if dc == 0 else [],
                                           sem="pe" if dc == NCH - 1 else None)
                            sl_pS[ip]["users"] = []
                            tevq = p.do("vector",
                                        lambda g, dq=dq, ip=ip: g.tensor_scalar(
                                            qfm[:, dq * QM:(dq + 1) * QM],
                                            pS[ip][:, 0:QM], bias_pp[:, dq:dq + 1],
                                            None, OP.add),
                                        waits=[tmm] + (sl_qf["users"] if dq == 0 else []),
                                        sem="dve")
                            sl_pS[ip]["users"].append(tevq)
                            sl_xq["users"].append(tevq)
                            if dq == 0:
                                sl_qf["users"] = []
                        tq = tevq
                        texp_last = None
                        for kt in range(NTOK):
                            ip = kt % 2
                            tmm = None
                            for dc in range(NCH):
                                tmm = p.do("tensor",
                                           lambda g, dc=dc, kt=kt, ip=ip: g.matmul(
                                               pS[ip][:, 0:QM],
                                               ka[:, dc * S + kt * 128: dc * S + kt * 128 + 128],
                                               qfm[:, dc * QM:(dc + 1) * QM],
                                               start=(dc == 0), stop=(dc == NCH - 1)),
                                           waits=([tq] + sl_pS[ip]["users"] +
                                                  (sl_probs["users"] if kt == 0 else [])) if dc == 0 else [],
                                           sem="pe" if dc == NCH - 1 else None)
                            sl_pS[ip]["users"] = []
                            if kt == 0:
                                sl_probs["users"] = []
                            texp = p.do("scalar",
                                        lambda g, kt=kt, ip=ip: g.activation(
                                            probs[:, kt * QM:(kt + 1) * QM], pS[ip][:, 0:QM],
                                            AF.Exp, bias=NEG_SHIFT, scale=SC_A),
                                        waits=[tmm], sem="act")
                            sl_pS[ip]["users"].append(texp)
                            sl_qf["users"].append(tmm)
                            texp_last = texp
                        for qs in range(2):
                            tmmS = None
                            for kt in range(NTOK):
                                w0 = ([texp_last] + sl_ctx["users"]) if kt == 0 else []
                                p.do("tensor",
                                     lambda g, kt=kt, qs=qs: g.matmul(
                                         pc0[:, :],
                                         probs[:, kt * QM + qs * 128: kt * QM + qs * 128 + 128],
                                         va[:, kt * D: kt * D + 512],
                                         start=(kt == 0), stop=(kt == NTOK - 1)),
                                     waits=w0, sem=None)
                                p.do("tensor",
                                     lambda g, kt=kt, qs=qs: g.matmul(
                                         pc1[:, :],
                                         probs[:, kt * QM + qs * 128: kt * QM + qs * 128 + 128],
                                         va[:, kt * D + 512: kt * D + 1024],
                                         start=(kt == 0), stop=(kt == NTOK - 1)), sem=None)
                                tmmS = p.do("tensor",
                                            lambda g, kt=kt, qs=qs: g.matmul(
                                                pcS[:, 0:1],
                                                probs[:, kt * QM + qs * 128: kt * QM + qs * 128 + 128],
                                                onesc[:, :],
                                                start=(kt == 0), stop=(kt == NTOK - 1)),
                                            sem="pe" if kt == NTOK - 1 else None)
                            sl_ctx["users"] = []
                            if qs == 1:
                                sl_probs["users"].append(tmmS)
                            trc = p.do("vector", lambda g: g.reciprocal(rcp[:, :], pcS[:, 0:1]),
                                       waits=[tmmS] + sl_rcp["users"], sem="dve")
                            sl_rcp["users"] = []
                            tc0 = p.do("vector",
                                       lambda g: g.tensor_scalar(
                                           ctxt[:, 0:512], pc0[:, :],
                                           rcp[:, 0:1], None, OP.mult),
                                       waits=[trc] + sl_ctxt["users"], sem="dve")
                            sl_ctxt["users"] = []
                            tc1 = p.do("vector",
                                       lambda g: g.tensor_scalar(
                                           ctxt[:, 512:1024], pc1[:, :],
                                           rcp[:, 0:1], None, OP.mult),
                                       waits=[trc], sem="dve")
                            sl_rcp["users"] += [tc0, tc1]
                            sl_ctx["users"] += [tc0, tc1]
                            if dbg and m == 0 and qs == 0:
                                dma("sync", "dS", DBG["dqf"], qfm[:], waits=[tq])
                                dma("sync", "dS", DBG["dpr"], probs[:], waits=[texp_last])
                                dma("sync", "dS", DBG["dct"], ctxt[:], waits=[tc0, tc1])
                            ttr = None
                            for c in range(NCH):
                                ttr = p.do("tensor",
                                           lambda g, c=c: g.transpose(
                                               ptrx[:, c * 128:(c + 1) * 128],
                                               ctxt[:, c * 128:(c + 1) * 128],
                                               idb[:]),
                                           waits=([tc0, tc1] + sl_ptrx["users"]) if c == 0 else [],
                                           sem="pe" if c == NCH - 1 else None)
                            sl_ptrx["users"] = []
                            tevc = p.do("vector", lambda g: g.tensor_copy(qfm[:, 0:D], ptrx[:]),
                                        waits=[ttr] + sl_ctxf["users"] + sl_qf["users"],
                                        sem="dve")
                            sl_ptrx["users"].append(tevc)
                            sl_ctxt["users"].append(ttr)
                            sl_ctxf["users"] = []
                            tso = m * QM + qs * 128
                            tpo = [None, None]
                            for hf in range(2):
                                for dc in range(NCH):
                                    p.do("tensor",
                                         lambda g, dc=dc, hf=hf: g.matmul(
                                             pO[hf][:, :],
                                             qfm[:, dc * 128:(dc + 1) * 128],
                                             wB[:, dc * D + hf * 512: dc * D + hf * 512 + 512],
                                             start=(dc == 0), stop=False),
                                         waits=([twall, tevc] + sl_pO["users"]) if (dc == 0 and hf == 0) else [],
                                         sem=None)
                                if hf == 0:
                                    sl_pO["users"] = []
                                tpo[hf] = p.do("tensor",
                                               lambda g, hf=hf: g.matmul(
                                                   pO[hf][:, :], onesr[32:33, :],
                                                   brow[32:33, hf * 512: hf * 512 + 512],
                                                   start=False, stop=True), sem="pe")
                            iy = qs
                            tlx = dma("sync", f"dT{iy}", xrs[iy][:], xin[tso:tso + 128, :],
                                      waits=sl_xr[iy]["users"])
                            sl_xr[iy]["users"] = []
                            ta0 = p.do("vector",
                                       lambda g, iy=iy: g.tensor_tensor(
                                           xrs[iy][:, 0:512], pO[0][:, :],
                                           xrs[iy][:, 0:512], OP.add),
                                       waits=[tpo[0], tlx], sem="dve")
                            ta1 = p.do("vector",
                                       lambda g, iy=iy: g.tensor_tensor(
                                           xrs[iy][:, 512:1024], pO[1][:, :],
                                           xrs[iy][:, 512:1024], OP.add),
                                       waits=[tpo[1]], sem="dve")
                            sl_pO["users"] += [ta0, ta1]
                            sl_ctxf["users"].append(tpo[1])
                            sl_qf["users"].append(tpo[1])
                            tyw = dma("gpsimd", "dY", yout[tso:tso + 128, :], xrs[iy][:],
                                      waits=[ta0, ta1])
                            sl_xr[iy]["users"] += [tyw]
                    p.barrier()
                    p.flush(block, semh)
                RES["ap"] = yout

        def ffn_phase(tag):
            with ExitStack() as PH:
                EE = PH.enter_context
                xlnfm = EE(nc.sbuf_tensor(f"xl{tag}", [128, NCH * 2048], BF16))
                acc = EE(nc.sbuf_tensor(f"ac{tag}", [128, 16 * D], F32))
                combine = EE(nc.sbuf_tensor(f"cb{tag}", [128, 16 * 8], F32))
                gb = EE(nc.sbuf_tensor(f"gb{tag}", [128, 2048], BF16))
                growb = EE(nc.sbuf_tensor(f"gw{tag}", [1, 512], BF16))
                ev8 = EE(nc.sbuf_tensor(f"e8{tag}", [128, 8], F32))
                ev2 = EE(nc.sbuf_tensor(f"e2{tag}", [128, 8], F32))
                sc8 = EE(nc.sbuf_tensor(f"s8{tag}", [128, 16], F32))
                h_sb = EE(nc.sbuf_tensor(f"hs{tag}", [128, NCH * 512], BF16))
                midg = EE(nc.sbuf_tensor(f"mg{tag}", [128, NCH * 512], BF16))
                sgt = EE(nc.sbuf_tensor(f"sgt{tag}", [128, 512], BF16))
                mid1 = EE(nc.sbuf_tensor(f"m1{tag}", [128, 512], BF16))
                xts = [EE(nc.sbuf_tensor(f"xt{tag}{i}", [128, D], F32)) for i in range(2)]
                xrw = EE(nc.sbuf_tensor(f"xr{tag}", [128, 1024], F32))
                ptr = [EE(nc.psum_tensor(f"pt{tag}{i}", [128, 1024], BF16)) for i in range(2)]
                pA = [EE(nc.psum_tensor(f"pa{tag}{i}", [128, 512], F32)) for i in range(2)]
                pB = [EE(nc.psum_tensor(f"pb{tag}{i}", [128, 512], F32)) for i in range(2)]
                pY = [EE(nc.psum_tensor(f"py{tag}{i}", [128, 512], F32)) for i in range(2)]
                sl_ptr = [slotinfo() for _ in range(2)]
                sl_pA = [slotinfo() for _ in range(2)]
                sl_pB = [slotinfo() for _ in range(2)]
                sl_pY = [slotinfo() for _ in range(2)]
                sl_xt = [slotinfo() for _ in range(2)]
                sl_hsb = slotinfo()
                sl_midg = slotinfo()
                sl_sgt = slotinfo()
                sl_mid1 = slotinfo()
                sl_gb = slotinfo()
                sl_acc = slotinfo()
                sl_ev = slotinfo()
                sl_cb = slotinfo()
                sl_xw = slotinfo()
                sl_xlq = [slotinfo() for _ in range(4)]
                sl_exw = slotinfo()

                rwb = rwb_t
                trw = p.tok("dW")

                for half in range(2):
                    hb = half * 2048
                    tx_done = []
                    for tl in range(16):
                        ix = tl % 2
                        tg = (hb // 128) + tl
                        tload, ttr = ln_tile(RES["ap"], tg, xts[ix], sl_xt[ix], f"dX{ix}",
                                             ptr[ix], sl_ptr[ix])
                        tev = p.do("scalar",
                                   lambda g, ix=ix, tl=tl: g.copy(
                                       xlnfm[:].rearrange("p (c t) -> p c t", t=2048)[:, :, tl * 128:(tl + 1) * 128],
                                       ptr[ix][:].rearrange("p (c t) -> p c t", t=128)),
                                   waits=[ttr] + sl_xlq[tl // 4]["users"],
                                   sem="act")
                        sl_xlq[tl // 4]["users"] = []
                        sl_ptr[ix]["users"].append(tev)
                        tx_done.append(tev)
                    for tl in range(16):
                        ip = tl % 2
                        tmm = None
                        for dc in range(NCH):
                            tmm = p.do("tensor",
                                       lambda g, dc=dc, tl=tl, ip=ip: g.matmul(
                                           pA[ip][:, 0:8],
                                           xlnfm[:, dc * 2048 + tl * 128: dc * 2048 + tl * 128 + 128],
                                           rwb[:, dc * 8:(dc + 1) * 8],
                                           start=(dc == 0), stop=(dc == NCH - 1)),
                                       waits=([trw, tx_done[tl]] + sl_pA[ip]["users"]) if dc == 0 else [],
                                       sem="pe" if dc == NCH - 1 else None)
                        sl_pA[ip]["users"] = []
                        t1 = p.do("vector", lambda g, ip=ip: g.reduce_max(
                            sc8[:, 0:1], pA[ip][:, 0:8], axis=AX.X, negate=True),
                            waits=[tmm] + sl_ev["users"], sem="dve")
                        sl_ev["users"] = []
                        t2 = p.do("scalar", lambda g, ip=ip: g.activation(
                            ev8[:, :], pA[ip][:, 0:8], AF.Exp, bias=sc8[:, 0:1], scale=1.0),
                            waits=[t1], sem="act")
                        sl_pA[ip]["users"].append(t2)
                        t3 = p.do("vector", lambda g: g.reduce_sum(sc8[:, 1:2], ev8[:, :],
                                                                   axis=AX.X),
                                  waits=[t2], sem="dve")
                        t4 = p.do("vector", lambda g: g.reciprocal(sc8[:, 2:3], sc8[:, 1:2]),
                                  waits=[t3], sem="dve")
                        t5 = p.do("vector", lambda g: g.max(out=sc8[:, 8:16], in_=ev8[:, :]),
                                  waits=[t2], sem="dve")
                        t6 = p.do("vector", lambda g: g.tensor_scalar(
                            ev2[:, :], ev8[:, :], sc8[:, 9:10], None, OP.is_ge),
                            waits=[t5], sem="dve")
                        t7 = p.do("vector", lambda g: g.tensor_mul(ev2[:, :], ev2[:, :], ev8[:, :]),
                                  waits=[t6], sem="dve")
                        t8 = p.do("vector", lambda g, tl=tl: g.tensor_scalar(
                            combine[:, tl * 8:(tl + 1) * 8], ev2[:, :], sc8[:, 2:3],
                            None, OP.mult),
                            waits=[t7, t4] + (sl_cb["users"] if tl == 0 else []), sem="dve")
                        if tl == 0:
                            sl_cb["users"] = []
                        sl_ev["users"] += [t7, t8]
                    tcomb = t8
                    tg4 = None
                    tgb_prev = None
                    for n in range(4):
                        ip = n % 2
                        tmm = None
                        for dc in range(NCH):
                            tmm = p.do("tensor",
                                       lambda g, n=n, dc=dc, ip=ip: g.matmul(
                                           pA[ip][0:1, :],
                                           sgwb_t[:, dc:dc + 1],
                                           xlnfm[:, dc * 2048 + n * 512: dc * 2048 + n * 512 + 512],
                                           start=(dc == 0), stop=(dc == NCH - 1)),
                                       waits=([trw] + tx_done + sl_pA[ip]["users"]) if dc == 0 else [],
                                       sem="pe" if dc == NCH - 1 else None)
                        sl_pA[ip]["users"] = []
                        tg4 = p.do("scalar", lambda g, ip=ip: g.activation(
                            growb[0:1, :], pA[ip][0:1, :], AF.Sigmoid,
                            bias=0.0, scale=1.0),
                            waits=[tmm] + ([tgb_prev] if tgb_prev else []), sem="act")
                        sl_pA[ip]["users"].append(tg4)
                        tb = p.do("tensor", lambda g, ip=ip: g.matmul(
                            pB[ip][:, :], onesr[0:1, :], growb[0:1, :],
                            start=True, stop=True),
                            waits=[tg4] + sl_pB[ip]["users"], sem="pe")
                        sl_pB[ip]["users"] = []
                        tb2 = p.do("vector", lambda g, n=n, ip=ip: g.tensor_copy(
                            gb[:, n * 512:(n + 1) * 512], pB[ip][:, :]),
                            waits=[tb] + (sl_gb["users"] if n == 0 else []), sem="dve")
                        if n == 0:
                            sl_gb["users"] = []
                        sl_pB[ip]["users"].append(tb2)
                        sl_gb["users"].append(tb2)
                        tgb_prev = tb
                    tgb = tb2
                    with ExitStack() as SH:
                        ES = SH.enter_context
                        wsgA = ES(nc.sbuf_tensor(f"wga{tag}{half}", [128, NCH * D], BF16))
                        wsgB = ES(nc.sbuf_tensor(f"wgb{tag}{half}", [128, NCH * D], BF16))
                        wsdn = ES(nc.sbuf_tensor(f"wgd{tag}{half}", [128, NCH * D], BF16))
                        wload(wsgA[:].rearrange("p (c m) -> p c m", m=D), wsl(W["sgu"], (0, D)),
                              waits=sl_exw["users"] if half else [])
                        wload(wsgB[:].rearrange("p (c m) -> p c m", m=D),
                              wsl(W["sgu"], (D, 2 * D)))
                        wload(wsdn[:].rearrange("p (c m) -> p c m", m=D), wsl(W["sdn"]))
                        tsh = p.tok("dW")
                        for mm in range(4):
                            tmid_done = None
                            for fc in range(NCH):
                                ia = fc % 2
                                tga = None
                                for dc in range(NCH):
                                    tga = p.do("tensor",
                                               lambda g, dc=dc, fc=fc, mm=mm, ia=ia: g.matmul(
                                                   pA[ia][:, :],
                                                   wsgA[:, dc * D + fc * 128: dc * D + fc * 128 + 128],
                                                   xlnfm[:, dc * 2048 + mm * 512: dc * 2048 + mm * 512 + 512],
                                                   start=(dc == 0), stop=(dc == NCH - 1)),
                                               waits=([tsh] + tx_done + sl_pA[ia]["users"]) if dc == 0 else [],
                                               sem="pe" if dc == NCH - 1 else None)
                                sl_pA[ia]["users"] = []
                                tgu = None
                                for dc in range(NCH):
                                    tgu = p.do("tensor",
                                               lambda g, dc=dc, fc=fc, mm=mm, ia=ia: g.matmul(
                                                   pB[ia][:, :],
                                                   wsgB[:, dc * D + fc * 128: dc * D + fc * 128 + 128],
                                                   xlnfm[:, dc * 2048 + mm * 512: dc * 2048 + mm * 512 + 512],
                                                   start=(dc == 0), stop=(dc == NCH - 1)),
                                               waits=([tsh] + sl_pB[ia]["users"]) if dc == 0 else [],
                                               sem="pe" if dc == NCH - 1 else None)
                                sl_pB[ia]["users"] = []
                                ts1 = p.do("scalar", lambda g, ia=ia: g.activation(
                                    sgt[:, :], pA[ia][:, :], AF.Silu, bias=0.0, scale=1.0),
                                    waits=[tga] + sl_sgt["users"], sem="act")
                                sl_sgt["users"] = []
                                sl_pA[ia]["users"].append(ts1)
                                ts2 = p.do("vector", lambda g, ia=ia: g.tensor_tensor(
                                    mid1[:, :], sgt[:, :], pB[ia][:, :], OP.mult),
                                    waits=[ts1, tgu] + sl_mid1["users"], sem="dve")
                                sl_mid1["users"] = []
                                sl_sgt["users"].append(ts2)
                                sl_pB[ia]["users"].append(ts2)
                                ts3 = p.do("vector", lambda g, fc=fc, mm=mm: g.tensor_tensor(
                                    midg[:, fc * 512:(fc + 1) * 512], mid1[:, :],
                                    gb[:, mm * 512:(mm + 1) * 512], OP.mult),
                                    waits=[ts2, tgb] + (sl_midg["users"] if fc == 0 else []),
                                    sem="dve")
                                if fc == 0:
                                    sl_midg["users"] = []
                                sl_mid1["users"].append(ts3)
                                sl_gb["users"].append(ts3)
                                tmid_done = ts3
                            for sub in range(4):
                                tlc = mm * 4 + sub
                                for hf in range(2):
                                    iy = hf
                                    tmm = None
                                    for fc in range(NCH):
                                        tmm = p.do("tensor",
                                                   lambda g, fc=fc, sub=sub, hf=hf, iy=iy: g.matmul(
                                                       pY[iy][:, :],
                                                       midg[:, fc * 512 + sub * 128: fc * 512 + sub * 128 + 128],
                                                       wsdn[:, fc * D + hf * 512: fc * D + hf * 512 + 512],
                                                       start=(fc == 0), stop=(fc == NCH - 1)),
                                                   waits=([tsh, tmid_done] + sl_pY[iy]["users"]) if fc == 0 else [],
                                                   sem="pe" if fc == NCH - 1 else None)
                                    sl_pY[iy]["users"] = []
                                    tac = p.do("vector",
                                               lambda g, tlc=tlc, hf=hf, iy=iy: g.tensor_copy(
                                                   acc[:, tlc * D + hf * 512: tlc * D + hf * 512 + 512],
                                                   pY[iy][:, :]),
                                               waits=[tmm] + (sl_acc["users"] if (mm == 0 and sub == 0 and hf == 0) else []),
                                               sem="dve")
                                    if mm == 0 and sub == 0 and hf == 0:
                                        sl_acc["users"] = []
                                    sl_pY[iy]["users"].append(tac)
                                    sl_midg["users"].append(tmm)
                    acc_final = [[] for _ in range(16)]
                    sl_shr = [p.tok("pe")]
                    with ExitStack() as EX:
                        ES = EX.enter_context
                        w1b = [ES(nc.sbuf_tensor(f"w1{tag}{half}{i}", [128, NCH * D], BF16))
                               for i in range(2)]
                        w2b = [ES(nc.sbuf_tensor(f"w2{tag}{half}{i}", [128, NCH * D], BF16))
                               for i in range(2)]
                        sl_we = [slotinfo() for _ in range(2)]
                        for e in range(8):
                            ie = e % 2
                            wl = sl_we[ie]["users"]
                            if e < 2:
                                wl = wl + sl_shr + sl_exw["users"]
                            wload(w1b[ie][:].rearrange("p (c m) -> p c m", m=D),
                                  W["w1"][e].rearrange("(c p) m -> p c m", p=128),
                                  waits=wl, dsem=f"dE{ie}")
                            twe = wload(w2b[ie][:].rearrange("p (c m) -> p c m", m=D),
                                        W["w2"][e].rearrange("(c p) m -> p c m", p=128),
                                        dsem=f"dE{ie}")
                            sl_we[ie]["users"] = []
                            for mm in range(4):
                                th_done = None
                                for fc in range(NCH):
                                    ia = fc % 2
                                    tmm = None
                                    for dc in range(NCH):
                                        tmm = p.do("tensor",
                                                   lambda g, dc=dc, fc=fc, mm=mm, ia=ia, ie=ie: g.matmul(
                                                       pA[ia][:, :],
                                                       w1b[ie][:, dc * D + fc * 128: dc * D + fc * 128 + 128],
                                                       xlnfm[:, dc * 2048 + mm * 512: dc * 2048 + mm * 512 + 512],
                                                       start=(dc == 0), stop=(dc == NCH - 1)),
                                                   waits=([twe] + sl_pA[ia]["users"] +
                                                          (tx_done if (e == 0 and mm == 0 and fc == 0) else [])) if dc == 0 else [],
                                                   sem="pe" if dc == NCH - 1 else None)
                                    sl_pA[ia]["users"] = []
                                    if fc == NCH - 1:
                                        sl_xlq[mm]["users"] = [tmm]
                                    th = p.do("scalar", lambda g, fc=fc, ia=ia: g.activation(
                                        h_sb[:, fc * 512:(fc + 1) * 512], pA[ia][:, :],
                                        AF.Silu, bias=0.0, scale=1.0),
                                        waits=[tmm] + (sl_hsb["users"] if fc == 0 else []),
                                        sem="act")
                                    if fc == 0:
                                        sl_hsb["users"] = []
                                    sl_pA[ia]["users"].append(th)
                                    th_done = th
                                sl_we[ie]["users"].append(th_done)
                                for sub in range(4):
                                    tlc = mm * 4 + sub
                                    for hf in range(2):
                                        iy = hf
                                        tmm = None
                                        for fc in range(NCH):
                                            tmm = p.do("tensor",
                                                       lambda g, fc=fc, sub=sub, hf=hf, iy=iy, ie=ie: g.matmul(
                                                           pY[iy][:, :],
                                                           h_sb[:, fc * 512 + sub * 128: fc * 512 + sub * 128 + 128],
                                                           w2b[ie][:, fc * D + hf * 512: fc * D + hf * 512 + 512],
                                                           start=(fc == 0), stop=(fc == NCH - 1)),
                                                       waits=([twe, th_done] + sl_pY[iy]["users"]) if fc == 0 else [],
                                                       sem="pe" if fc == NCH - 1 else None)
                                        sl_pY[iy]["users"] = []
                                        tac = p.do("vector",
                                                   lambda g, tlc=tlc, hf=hf, iy=iy, e=e: g.scalar_tensor_tensor(
                                                       acc[:, tlc * D + hf * 512: tlc * D + hf * 512 + 512],
                                                       pY[iy][:, :],
                                                       combine[:, tlc * 8 + e: tlc * 8 + e + 1],
                                                       acc[:, tlc * D + hf * 512: tlc * D + hf * 512 + 512],
                                                       OP.mult, OP.add),
                                                   waits=[tmm, tcomb], sem="dve")
                                        sl_pY[iy]["users"].append(tac)
                                        sl_we[ie]["users"].append(tac)
                                        sl_hsb["users"].append(tmm)
                                        sl_acc["users"].append(tac)
                                        sl_cb["users"].append(tac)
                                        if e == 7:
                                            acc_final[tlc].append(tac)
                        sl_exw["users"] = [p.tok("pe")]
                        for tl in range(16):
                            tg = (hb // 128) + tl
                            tlx = dma("gpsimd", "dT0", xrw[:],
                                      RES["ap"][tg * 128:(tg + 1) * 128, :],
                                      waits=sl_xw["users"])
                            sl_xw["users"] = []
                            tadds = []
                            for hf in range(2):
                                tadd = p.do("gpsimd",
                                            lambda g, tl=tl, hf=hf: g.tensor_tensor(
                                                xrw[:, hf * 512:(hf + 1) * 512],
                                                acc[:, tl * D + hf * 512: tl * D + hf * 512 + 512],
                                                xrw[:, hf * 512:(hf + 1) * 512], OP.add),
                                            waits=([tlx] if hf == 0 else []) + [acc_final[tl][hf]],
                                            sem="pool")
                                sl_acc["users"].append(tadd)
                                tadds.append(tadd)
                            tyw = dma("gpsimd", "dY",
                                      yout[tg * 128:(tg + 1) * 128, :],
                                      xrw[:], waits=[tadds[1]])
                            sl_xw["users"] = [tyw]
                    if half == 1:
                        p.barrier()
                p.flush(block, semh)

        if "F1" in phases:
            ffn_phase("u")
            RES["ap"] = yout

        if "B" in phases:
            with ExitStack() as PH:
                EE = PH.enter_context
                kb = EE(nc.sbuf_tensor("kb", [128, NCH * S], BF16))

                with ExitStack() as P1:
                    E1 = P1.enter_context
                    wA = E1(nc.sbuf_tensor("wqB", [128, NCH * D], BF16))
                    wB = E1(nc.sbuf_tensor("wkB", [128, NCH * D], BF16))
                    wV = E1(nc.sbuf_tensor("wvB", [128, NCH * D], BF16))
                    wload(wA[:].rearrange("p (c m) -> p c m", m=D), wsl(W["bWq"]))
                    wload(wB[:].rearrange("p (c m) -> p c m", m=D), wsl(W["bWk"]))
                    wload(wV[:].rearrange("p (c m) -> p c m", m=D), wsl(W["bWv"]))
                    twall = p.tok("dW")
                    xts = [E1(nc.sbuf_tensor(f"xB{i}", [128, D], F32)) for i in range(2)]
                    xfm = [E1(nc.sbuf_tensor(f"xfB{i}", [128, NCH * 512], BF16))
                           for i in range(2)]
                    qr = E1(nc.sbuf_tensor("qrB", [128, NCH * 512], BF16))
                    kr = E1(nc.sbuf_tensor("krB", [128, NCH * 512], BF16))
                    rCf = E1(nc.sbuf_tensor("rCf", [128, S], BF16))
                    rSf = E1(nc.sbuf_tensor("rSf", [128, S], BF16))
                    tmp2b = [E1(nc.sbuf_tensor(f"tp2B{i}", [128, 512], BF16))
                             for i in range(2)]
                    vstage = [E1(nc.sbuf_tensor(f"vsB{i}", [128, 1040], BF16))
                              for i in range(2)]
                    ptmb = [E1(nc.sbuf_tensor(f"ptmB{i}", [128, 512], F32))
                            for i in range(2)]
                    ptr = [E1(nc.psum_tensor(f"ptB{i}", [128, 1024], BF16)) for i in range(2)]
                    pq = [E1(nc.psum_tensor(f"pqB{i}", [128, 512], F32)) for i in range(2)]
                    psw = [E1(nc.psum_tensor(f"pwB{i}", [128, 512], F32)) for i in range(2)]
                    pv = [E1(nc.psum_tensor(f"pvB{i}", [128, 512], F32)) for i in range(2)]
                    sl_xt = [slotinfo() for _ in range(2)]
                    sl_fm = [slotinfo() for _ in range(2)]
                    sl_ptr = [slotinfo() for _ in range(2)]
                    sl_pq = [slotinfo() for _ in range(2)]
                    sl_psw = [slotinfo() for _ in range(2)]
                    sl_qr = slotinfo()
                    sl_kr = slotinfo()
                    sl_rope = [slotinfo()]
                    sl_tmp1 = [slotinfo() for _ in range(2)]
                    sl_tmp2 = [slotinfo() for _ in range(2)]
                    sl_pv = [slotinfo() for _ in range(2)]
                    sl_vs = [slotinfo() for _ in range(2)]
                    tms0 = p.do("vector", lambda g: g.memset(
                        vstage[0][:].rearrange("p (h j) -> p h j", j=65)[:, :, 64:65], 1.0),
                        sem="dve")
                    tms1 = p.do("vector", lambda g: g.memset(
                        vstage[1][:].rearrange("p (h j) -> p h j", j=65)[:, :, 64:65], 1.0),
                        sem="dve")
                    sl_vs[0]["users"].append(tms0)
                    sl_vs[1]["users"].append(tms1)

                    dma("sync", "dR0", rCf[:], C["ropeC"][:, :])
                    dma("sync", "dR0", rSf[:], C["ropeS"][:, :])
                    trcs_all = p.tok("dR0")
                    for m in range(8):
                        im = m % 2
                        trcs = trcs_all
                        rC = [rCf[:, m * 512:(m + 1) * 512]]
                        rS = [rSf[:, m * 512:(m + 1) * 512]]
                        tfm_done = []
                        for sub in range(4):
                            tg = m * 4 + sub
                            ix = tg % 2
                            tload, ttr = ln_tile(RES["ap"], tg, xts[ix], sl_xt[ix], f"dX{ix}",
                                                 ptr[ix], sl_ptr[ix])
                            tev = p.do("vector",
                                       lambda g, ix=ix, im=im, sub=sub: g.tensor_copy(
                                           xfm[im][:].rearrange("p (c t) -> p c t", t=512)[:, :, sub * 128:(sub + 1) * 128],
                                           ptr[ix][:].rearrange("p (c t) -> p c t", t=128)),
                                       waits=[ttr] + (sl_fm[im]["users"] if sub == 0 else []),
                                       sem="dve")
                            if sub == 0:
                                sl_fm[im]["users"] = []
                            sl_ptr[ix]["users"].append(tev)
                            tfm_done.append(tev)
                            t_glob = m * 4 + sub
                            ivs = t_glob % 2
                            tevv = None
                            for hf in range(2):
                                tmmv = None
                                for dc in range(NCH):
                                    tmmv = p.do("tensor",
                                                lambda g, dc=dc, hf=hf, im=im, sub=sub: g.matmul(
                                                    pv[hf][:, :],
                                                    xfm[im][:, dc * 512 + sub * 128: dc * 512 + sub * 128 + 128],
                                                    wV[:, dc * D + hf * 512: dc * D + hf * 512 + 512],
                                                    start=(dc == 0), stop=False),
                                                waits=([twall, tev] + sl_pv[hf]["users"]) if dc == 0 else [],
                                                sem=None)
                                tbv = p.do("tensor",
                                           lambda g, hf=hf: g.matmul(
                                               pv[hf][:, :], onesr[0:1, :],
                                               brow[0:1, D + hf * 512: D + hf * 512 + 512],
                                               start=False, stop=True), sem="pe")
                                sl_pv[hf]["users"] = []
                                sl_fm[im]["users"].append(tbv)
                                tevv = p.do("vector",
                                            lambda g, hf=hf, ivs=ivs: g.tensor_copy(
                                                vstage[ivs][:].rearrange("p (h j) -> p h j", j=65)[:, hf * 8:(hf + 1) * 8, 0:64],
                                                pv[hf][:, :].rearrange("p (o h j) -> p o h j",
                                                                       o=1, j=64)[:, 0, :, :]),
                                            waits=[tbv] + (sl_vs[ivs]["users"] if hf == 0 else []),
                                            sem="dve")
                                if hf == 0:
                                    sl_vs[ivs]["users"] = []
                                sl_pv[hf]["users"].append(tevv)
                                sl_vs[ivs]["users"].append(tevv)
                            tvw = dma("gpsimd", "dA", scrV[:, t_glob * 1040:(t_glob + 1) * 1040],
                                      vstage[ivs][:], waits=[tevv])
                            sl_vs[ivs]["users"].append(tvw)
                        pend_ropes = []
                        for which in range(2):
                            wbuf = wA if which == 0 else wB
                            raw = qr if which == 0 else kr
                            sl_raw = sl_qr if which == 0 else sl_kr
                            for ch in range(NCH):
                                ip = ch % 2
                                tmm = None
                                for dc in range(NCH):
                                    tmm = p.do("tensor",
                                               lambda g, dc=dc, ch=ch, ip=ip, im=im, wbuf=wbuf: g.matmul(
                                                   pq[ip][:, :],
                                                   wbuf[:, dc * D + ch * 128: dc * D + ch * 128 + 128],
                                                   xfm[im][:, dc * 512:(dc + 1) * 512],
                                                   start=(dc == 0), stop=(dc == NCH - 1)),
                                               waits=([twall] + tfm_done + sl_pq[ip]["users"]) if dc == 0 else [],
                                               sem="pe" if dc == NCH - 1 else None)
                                sl_pq[ip]["users"] = []
                                tev = p.do("vector",
                                           lambda g, ch=ch, ip=ip, raw=raw, which=which: g.tensor_scalar(
                                               raw[:, ch * 512:(ch + 1) * 512], pq[ip][:, :],
                                               bias_pp[:, 16 + which * 8 + ch: 17 + which * 8 + ch],
                                               None, OP.add),
                                           waits=[tmm] + (sl_raw["users"] if ch == 0 else []),
                                           sem="dve")
                                if ch == 0:
                                    sl_raw["users"] = []
                                sl_pq[ip]["users"].append(tev)
                                sl_fm[im]["users"].append(tmm)

                                def mk_rope(ch=ch, tev=tev, raw=raw, which=which,
                                            m=m, trcs=trcs, sl_raw=sl_raw,
                                            rC=rC, rS=rS):
                                    isw = ch % 2
                                    tmp2 = tmp2b[isw]
                                    ptm = ptmb[isw]
                                    tsw = p.do("tensor",
                                               lambda g, ch=ch, raw=raw, isw=isw: g.matmul(
                                                   psw[isw][:, :], p32[:, :],
                                                   raw[:, ch * 512:(ch + 1) * 512],
                                                   start=True, stop=True),
                                               waits=[tev] + sl_psw[isw]["users"], sem="pe")
                                    sl_psw[isw]["users"] = []
                                    e1, s1 = ("gpsimd", "pool") if isw == 0 else ("vector", "dve")
                                    tt1 = p.do(e1,
                                               lambda g, ch=ch, raw=raw, tmp2=tmp2: g.tensor_tensor(
                                                   tmp2[:, :], raw[:, ch * 512:(ch + 1) * 512],
                                                   rC[0][:, :], OP.mult),
                                               waits=[tev, trcs, tsw] + sl_tmp2[isw]["users"], sem=s1)
                                    sl_tmp2[isw]["users"] = []
                                    tt2 = p.do("vector",
                                               lambda g, isw=isw, ptm=ptm: g.tensor_tensor(
                                                   ptm[:, :], psw[isw][:, :], rS[0][:, :],
                                                   OP.mult),
                                               waits=[tsw, trcs] + sl_tmp1[isw]["users"], sem="dve")
                                    sl_tmp1[isw]["users"] = []
                                    sl_psw[isw]["users"].append(tt2)
                                    if which == 0:
                                        dst_ap = raw[:, ch * 512:(ch + 1) * 512]
                                    else:
                                        dst_ap = kb[:, ch * S + m * 512: ch * S + m * 512 + 512]
                                    e2, s2 = ("vector", "dve") if isw == 0 else ("gpsimd", "pool")
                                    tdst = p.do(e2,
                                                lambda g, dst_ap=dst_ap, tmp2=tmp2, ptm=ptm: g.tensor_tensor(
                                                    dst_ap, tmp2[:, :], ptm[:, :], OP.add),
                                                waits=[tt1, tt2], sem=s2)
                                    sl_tmp1[isw]["users"].append(tdst)
                                    sl_tmp2[isw]["users"].append(tdst)
                                    sl_rope[0]["users"] += [tt1, tt2]
                                    sl_raw["users"].append(tdst)
                                pend_ropes.append(mk_rope)
                                if ch % 2 == 1:
                                    for fn in pend_ropes:
                                        fn()
                                    pend_ropes = []
                            if which == 0:
                                tqw = dma("gpsimd", "dB",
                                          scrB[:, :, m * 512:(m + 1) * 512].rearrange("c p t -> p c t"),
                                          qr[:].rearrange("p (c t) -> p c t", t=512),
                                          waits=list(sl_qr["users"]))
                                sl_qr["users"].append(tqw)
                    if dbg:
                        p.barrier()
                        dma("sync", "dS", DBG["dkb"], kb[:])
                        dma("sync", "dS", DBG["dqb"].rearrange("c p t -> p c t"),
                            scrB.rearrange("c p t -> p c t"))
                        p.barrier()
                    p.flush(block, semh)

                with ExitStack() as P2:
                    E2 = P2.enter_context
                    wO = E2(nc.sbuf_tensor("woB", [128, NCH * D], BF16))
                    vxb = E2(nc.sbuf_tensor("vxb", [128, NTOK * 16 * 65], BF16))
                    wload(wO[:].rearrange("p (c m) -> p c m", m=D), wsl(W["bWo"]),
                          waits=[p.tok("pe")])
                    twall = p.tok("dW")
                    seam_b = [p.tok(x) for x in ("pe", "act", "dve", "pool", "dA", "dB")]
                    tvx = dma("scalar", "dA", vxb[:], scrV[:], waits=seam_b)
                    qm2 = [E2(nc.sbuf_tensor(f"qmB{i}", [128, NCH * 512], BF16))
                           for i in range(2)]
                    probs = E2(nc.sbuf_tensor("prB", [128, 4 * 1024], BF16))
                    ctxf = E2(nc.sbuf_tensor("cfB", [128, NCH * 512], BF16))
                    rrow = E2(nc.sbuf_tensor("rrB", [1, 1024], F32))
                    rrowb = E2(nc.sbuf_tensor("rbB", [1, 1024], BF16))
                    rbt = E2(nc.sbuf_tensor("rtB", [128, 1024], BF16))
                    xrs = [E2(nc.sbuf_tensor(f"xrB{i}", [128, D], F32)) for i in range(2)]
                    pSh = E2(nc.psum_tensor("pSB", [128, 2048], F32))
                    pOE = E2(nc.psum_tensor("pOEB", [128, 1024], F32))
                    pCb = E2(nc.psum_tensor("pCB", [128, 1024], F32))
                    pC = [pCb[:, 0:512], pCb[:, 512:1024]]
                    pO = [pOE[:, 0:512], pOE[:, 512:1024]]
                    sl_qm = [slotinfo() for _ in range(2)]
                    sl_pS0 = [slotinfo() for _ in range(2)]
                    sl_pS1 = slotinfo()
                    sl_pC = [slotinfo() for _ in range(2)]
                    sl_pC[0]["users"].append(tvx)
                    sl_pC[1]["users"].append(tvx)
                    sl_pr = [slotinfo() for _ in range(2)]
                    sl_ctxf = slotinfo()
                    sl_rr = slotinfo()
                    sl_rrb = slotinfo()
                    sl_rbt = slotinfo()
                    sl_xr = [slotinfo() for _ in range(2)]

                    for m in range(8):
                        iq = m % 2
                        qm = qm2[iq]
                        tldq = dma("sync", f"dQ{iq}",
                                   qm[:].rearrange("p (c t) -> p c t", t=512),
                                   scrB[:, :, m * 512:(m + 1) * 512].rearrange("c p t -> p c t"),
                                   waits=sl_qm[iq]["users"] + [p.tok("dB")] +
                                         (seam_b if m == 0 else []))
                        sl_qm[iq]["users"] = []
                        ctx_done = []
                        pend = {"fn": None}

                        def flush_norm():
                            if pend["fn"] is not None:
                                pend["fn"]()
                                pend["fn"] = None
                        for chh in range(NCH):
                            texp = [None] * 16

                            def emit_scores(kp, chh=chh, tldq=tldq, texp=texp, qm=qm):
                                mml = [None, None]
                                tee = [None, None]
                                srcE = pSh[:, 0:1024] if kp % 2 == 0 else pOE[:, :]
                                for sel in range(2):
                                    pb = sel * 64
                                    slot = sl_pS0[kp % 2] if sel == 0 else sl_pS1
                                    for j in range(2):
                                        c = 2 * kp + j
                                        if sel == 0:
                                            dst = srcE[:, j * 512:(j + 1) * 512]
                                        else:
                                            dst = pSh[:, 1024 + j * 512: 1024 + j * 512 + 512]
                                        w0 = []
                                        if j == 0:
                                            w0 = slot["users"] + ([tldq] if sel == 0 else [])
                                        mml[sel] = p.do(
                                            "tensor",
                                            lambda g, c=c, pb=pb, chh=chh, dst=dst, qm=qm: g.matmul(
                                                dst,
                                                kb[pb:pb + 64, chh * S + c * 128: chh * S + c * 128 + 128],
                                                qm[pb:pb + 64, chh * 512:(chh + 1) * 512],
                                                start=True, stop=True,
                                                tile_position=(pb, 0)),
                                            waits=w0,
                                            sem="pe" if j == 1 else None)
                                        if j == 0:
                                            slot["users"] = []
                                        else:
                                            sl_qm[m % 2]["users"].append(mml[sel])
                                    tee[sel] = p.do(
                                        "scalar",
                                        lambda g, kp=kp, sel=sel, srcE=srcE: g.activation(
                                            probs[:, (kp % 2) * 2048 + sel * 1024:
                                                  (kp % 2) * 2048 + sel * 1024 + 1024],
                                            srcE if sel == 0 else pSh[:, 1024:2048],
                                            AF.Exp,
                                            bias=NEG_SHIFT, scale=SC_B),
                                        waits=[mml[sel]] +
                                              (sl_pr[kp % 2]["users"] if sel == 0 else []),
                                        sem="act")
                                    if sel == 0:
                                        sl_pr[kp % 2]["users"] = []
                                    slot["users"].append(tee[sel])
                                texp[kp] = tee

                            def emit_ctx(kpc, chh=chh, texp=texp):
                                outt = [None, None]
                                for sel in range(2):
                                    h = 2 * chh + sel
                                    tc = None
                                    for j in range(2):
                                        c = 2 * kpc + j
                                        w0 = []
                                        if j == 0:
                                            w0 = [texp[kpc][sel]]
                                            if c == 0:
                                                w0 = w0 + sl_pC[sel]["users"]
                                        tc = p.do(
                                            "tensor",
                                            lambda g, c=c, sel=sel, h=h, kpc=kpc, j=j: g.matmul(
                                                pC[sel][0:65, :],
                                                vxb[:, c * 1040 + h * 65: c * 1040 + h * 65 + 65],
                                                probs[:, (kpc % 2) * 2048 + sel * 1024 + j * 512:
                                                      (kpc % 2) * 2048 + sel * 1024 + j * 512 + 512],
                                                start=(c == 0), stop=(c == 31)),
                                            waits=w0,
                                            sem="pe" if j == 1 else None)
                                        if j == 0 and c == 0:
                                            sl_pC[sel]["users"] = []
                                    sl_pr[kpc % 2]["users"].append(tc)
                                    outt[sel] = tc
                                return outt

                            for kp in range(16):
                                emit_scores(kp)
                                if kp == 1:
                                    flush_norm()
                                if kp >= 1:
                                    emit_ctx(kp - 1)
                            tcs = emit_ctx(15)
                            tve = [None, None]
                            for sel in range(2):
                                tve[sel] = p.do("vector",
                                                lambda g, sel=sel, chh=chh: g.tensor_copy(
                                                    ctxf[sel * 64:sel * 64 + 64,
                                                         chh * 512:(chh + 1) * 512],
                                                    pCb[0:64, sel * 512:sel * 512 + 512]),
                                                waits=[tcs[sel]] +
                                                      (sl_ctxf["users"] if chh == 0 and sel == 0 else []),
                                                sem="dve")
                                if chh == 0 and sel == 0:
                                    sl_ctxf["users"] = []
                                sl_pC[sel]["users"].append(tve[sel])
                            tl0 = p.do("scalar", lambda g: g.activation(
                                rrow[0:1, :], pCb[64:65, :], AF.Ln,
                                bias=0.0, scale=1.0),
                                waits=[tcs[0], tcs[1]] + sl_rr["users"], sem="act")
                            sl_rr["users"] = []
                            te0 = p.do("scalar", lambda g: g.activation(
                                rrowb[0:1, :], rrow[0:1, :], AF.Exp,
                                bias=0.0, scale=-1.0),
                                waits=[tl0] + sl_rrb["users"], sem="act")
                            sl_rrb["users"] = []
                            te1 = te0
                            sl_rr["users"] += [te0]

                            def norm(chh=chh, tcs=tcs, te0=te0, te1=te1, tve=tve):
                                tb0 = p.do("tensor", lambda g: g.matmul(
                                    pO[0][:, :], onesr[0:1, :], rrowb[0:1, 0:512],
                                    start=True, stop=True),
                                    waits=[te0] + sl_pS0[1]["users"], sem="pe")
                                sl_pS0[1]["users"] = []
                                tb1 = p.do("tensor", lambda g: g.matmul(
                                    pO[1][:, :], onesr[0:1, :], rrowb[0:1, 512:1024],
                                    start=True, stop=True),
                                    waits=[te1], sem="pe")
                                sl_rrb["users"] += [tb0, tb1]
                                tr4a = p.do("vector", lambda g: g.tensor_copy(
                                    rbt[:, 0:512], pO[0][:, :]),
                                    waits=[tb0] + sl_rbt["users"], sem="dve")
                                sl_rbt["users"] = []
                                tr4 = p.do("vector", lambda g: g.tensor_copy(
                                    rbt[:, 512:1024], pO[1][:, :]),
                                    waits=[tb1], sem="dve")
                                sl_pS0[1]["users"] += [tr4a, tr4]
                                for sel in range(2):
                                    tev = p.do("vector",
                                               lambda g, sel=sel, chh=chh: g.tensor_tensor(
                                                   ctxf[sel * 64:sel * 64 + 64,
                                                        chh * 512:(chh + 1) * 512],
                                                   ctxf[sel * 64:sel * 64 + 64,
                                                        chh * 512:(chh + 1) * 512],
                                                   rbt[sel * 64:sel * 64 + 64,
                                                       sel * 512:sel * 512 + 512],
                                                   OP.mult),
                                               waits=[tr4a if sel == 0 else tr4, tve[sel]],
                                               sem="dve")
                                    sl_rbt["users"].append(tev)
                                    ctx_done.append(tev)
                            pend["fn"] = norm
                        flush_norm()
                        for sub in range(4):
                            tso = m * 512 + sub * 128
                            iy = sub % 2
                            tpo = [None, None]
                            for hf in range(2):
                                for dc in range(NCH):
                                    p.do("tensor",
                                         lambda g, dc=dc, sub=sub, hf=hf: g.matmul(
                                             pO[hf][:, :],
                                             ctxf[:, dc * 512 + sub * 128: dc * 512 + sub * 128 + 128],
                                             wO[:, dc * D + hf * 512: dc * D + hf * 512 + 512],
                                             start=(dc == 0), stop=False),
                                         waits=([twall] + ctx_done + sl_pS0[1]["users"]) if (dc == 0 and hf == 0) else [],
                                         sem=None)
                                if hf == 0:
                                    sl_pS0[1]["users"] = []
                                tpo[hf] = p.do("tensor",
                                               lambda g, hf=hf: g.matmul(
                                                   pO[hf][:, :], onesr[32:33, :],
                                                   brow[32:33, D + hf * 512: D + hf * 512 + 512],
                                                   start=False, stop=True), sem="pe")
                            tlx = dma("sync", f"dT{iy}", xrs[iy][:], RES["ap"][tso:tso + 128, :],
                                      waits=sl_xr[iy]["users"])
                            sl_xr[iy]["users"] = []
                            ta0 = p.do("vector",
                                       lambda g, iy=iy: g.tensor_tensor(
                                           xrs[iy][:, 0:512], pO[0][:, :],
                                           xrs[iy][:, 0:512], OP.add),
                                       waits=[tpo[0], tlx], sem="dve")
                            ta1 = p.do("vector",
                                       lambda g, iy=iy: g.tensor_tensor(
                                           xrs[iy][:, 512:1024], pO[1][:, :],
                                           xrs[iy][:, 512:1024], OP.add),
                                       waits=[tpo[1]], sem="dve")
                            sl_pS0[1]["users"] += [ta0, ta1]
                            sl_ctxf["users"].append(tpo[1])
                            tyw = dma("gpsimd", "dY", yout[tso:tso + 128, :], xrs[iy][:],
                                      waits=[ta0, ta1])
                            sl_xr[iy]["users"] += [tyw]
                    p.barrier()
                    p.flush(block, semh)
                RES["ap"] = yout

        if "F2" in phases:
            ffn_phase("v")

        p.wait("sync", [p.tok("dY")])
        p.flush(block, semh)

    return nc


def kernel(**inputs):
    x_full = np.ascontiguousarray(inputs["L0_attn_a_x"])
    consts = _host_consts()
    wmap = {
        "aWqkv": inputs["L0_attn_a_W_qkv"], "aBqkv": inputs["L0_attn_a_b_qkv"],
        "aWo": inputs["L0_attn_a_W_out"], "aBo": inputs["L0_attn_a_b_out"],
        "bWq": inputs["L0_attn_b_Wq"], "bBq": inputs["L0_attn_b_bq"],
        "bWk": inputs["L0_attn_b_Wk"], "bBk": inputs["L0_attn_b_bk"],
        "bWv": inputs["L0_attn_b_Wv"], "bBv": inputs["L0_attn_b_bv"],
        "bWo": inputs["L0_attn_b_Wo"], "bBo": inputs["L0_attn_b_bo"],
        "rw": inputs["L0_ffn_router_w"], "w1": inputs["L0_ffn_w1"],
        "w2": inputs["L0_ffn_w2"], "sgu": inputs["L0_ffn_shared_gate_up"],
        "sdn": inputs["L0_ffn_shared_down"], "sgw": inputs["L0_ffn_shared_gate_w"],
    }
    wmap = {k: np.ascontiguousarray(np.asarray(v, dtype=np.float32))
            for k, v in wmap.items()}
    nc = build_nc(_PHASES)
    n = 8
    in_maps = []
    for i in range(n):
        mp = {"x": np.ascontiguousarray(x_full[i])}
        mp.update(wmap)
        mp.update(consts)
        in_maps.append(mp)
    res = run_bass_kernel_spmd(nc, in_maps, core_ids=list(range(n)))
    return np.stack([r["y"] for r in res.results], axis=0)
